# version 09
# baseline (speedup 1.0000x reference)
import numpy as np
import ml_dtypes
from contextlib import ExitStack

import concourse.bass as bass
import concourse.mybir as mybir
from concourse.bass_utils import run_bass_kernel_spmd

F32 = mybir.dt.float32
BF16 = mybir.dt.bfloat16
I32 = mybir.dt.int32
AF = mybir.ActivationFunctionType
ALU = mybir.AluOpType
AX = mybir.AxisListType

S = 4096
D = 1024
NT = S // 128
NF = D // 128
TQ = 512
NQ = S // TQ
H = 16
DK = 64
LN_EPS = 1e-5

SUB_ORDER = ["l0a", "l0f", "l0b", "l0f", "l1a", "l1f", "l1b", "l1f"]


class Sem:
    def __init__(self, name):
        self.name = name
        self.n = 0
        self.h = None

    def bump(self, amt):
        self.n += amt
        return self.n


class Plan:
    ENGINES = ("sync", "gpsimd", "vector", "scalar", "tensor")

    def __init__(self):
        self.ops = {e: [] for e in self.ENGINES}
        self.serial_sems = None
        self.self_sems = {}

    def raw(self, engine, fn):
        self.ops[engine].append(fn)

    def emit(self, engine, method, *args, waits=(), inc=None, inc_amt=1, **kwargs):
        if self.serial_sems is not None:
            waits = tuple(waits) + tuple(
                (s, s.n) for s in self.serial_sems.values() if s.n > 0)
        ssem = self.self_sems.get(engine)
        if ssem is not None:
            if ssem.n > 0:
                waits = tuple(waits) + ((ssem, ssem.n),)
            if inc is None:
                inc = ssem
                inc_amt = 1
        waits = tuple(waits)

        def f(eng):
            for s, v in waits:
                eng.wait_ge(s.h, v)
            ins = getattr(eng, method)(*args, **kwargs)
            if inc is not None:
                ins.then_inc(inc.h, inc_amt)
        self.ops[engine].append(f)
        if inc is not None:
            return inc.bump(inc_amt)
        return None

    def dma(self, queue, out, in_, waits=(), inc=None, **kwargs):
        return self.emit(queue, "dma_start", out=out, in_=in_, waits=waits,
                         inc=inc, inc_amt=16, **kwargs)

    def wait(self, engine, *waits):
        waits = tuple(waits)

        def f(eng):
            for s, v in waits:
                eng.wait_ge(s.h, v)
        self.ops[engine].append(f)


class Env:
    def __init__(self):
        self.plan = Plan()
        self.sems = {}
        self.t = {}
        self.dram = {}
        self._state = {}
        for name in ("ld", "ldb", "lo", "wt", "wt0", "wt1", "wt2", "wt3",
                     "et", "tt", "vv", "ss", "oo", "oob", "ic", "ic2"):
            self.sems[name] = Sem(name)

    def sem(self, name):
        return self.sems[name]

    def state(self, key, default):
        if key not in self._state:
            self._state[key] = default
        return self._state[key]


def barrier(env):
    p = env.plan
    marks = {}
    for e, sname in (("tensor", "tt"), ("vector", "vv"), ("scalar", "ss")):
        s = env.sem(sname)
        v = s.bump(1)

        def mk(sem=s):
            def f(eng):
                eng.nop().then_inc(sem.h, 1)
            return f
        p.raw(e, mk())
        marks[e] = (s, v)
    allm = list(marks.values())
    for e in ("tensor", "vector", "scalar"):
        p.wait(e, *[m for ee, m in marks.items() if ee != e])
    p.wait("gpsimd", *allm)
    p.wait("sync", *allm)


def sub_reset(env):
    env._state = {}


def host_prep(inputs):
    dev = {}
    A = lambda a: np.ascontiguousarray(np.asarray(a))

    def tile_asc(wT, nh):
        a = wT.shape[0] // 128
        return np.ascontiguousarray(
            wT.reshape(a, 128, nh, 128).transpose(2, 1, 0, 3).reshape(
                nh, 128, a * 128))
    dev["l1f_wgT"] = tile_asc(np.asarray(inputs["L1_ffn_w_gate"]).T, 24)
    dev["l1f_wuT"] = tile_asc(np.asarray(inputs["L1_ffn_w_up"]).T, 24)
    dev["l1f_wdT"] = tile_asc(np.asarray(inputs["L1_ffn_w_down"]).T, 8)

    qkv = np.asarray(inputs["L0_attn_a_qkv_w"]).reshape(16, 192, 1024)
    qkv_b = np.asarray(inputs["L0_attn_a_qkv_b"]).reshape(16, 192)
    dev["l0a_wqT"] = A(qkv[:, 0:64, :].reshape(1024, 1024).T)
    dev["l0a_wkT"] = A(qkv[:, 64:128, :].reshape(1024, 1024).T)
    dev["l0a_wvT"] = A(qkv[:, 128:192, :].reshape(1024, 1024).T)
    dev["l0a_bq"] = A(qkv_b[:, 0:64].reshape(1024).reshape(8, 128).T)
    dev["l0a_bk"] = A(qkv_b[:, 64:128].reshape(1024).reshape(8, 128).T)
    dev["l0a_bv"] = A(qkv_b[:, 128:192].reshape(1024).reshape(8, 128).T)
    dev["l0a_woT"] = A(np.asarray(inputs["L0_attn_a_out_w"]).T)
    dev["l0a_bo"] = A(np.asarray(inputs["L0_attn_a_out_b"]).reshape(8, 128).T)
    slopes = 0.5 ** np.arange(16)
    p = np.arange(128, dtype=np.float64)
    f = np.arange(512, dtype=np.float64)
    bias = np.zeros((128, 16 * 57), dtype=np.float64)
    for h in range(16):
        sl = slopes[h]
        for m in range(1, 29):
            bias[:, 57 * h + m] = sl * (p - 64) - sl * (128 * m - 64)
        for mp in range(1, 29):
            bias[:, 57 * h + 28 + mp] = -sl * p - sl * (128 * mp - 127)
    dev["l0a_bias"] = A(bias.astype(np.float32))
    eb = np.zeros((16, 2, 128, 512), dtype=np.float64)
    for h in range(16):
        sl = slopes[h]
        eb[h, 0] = np.exp(-sl * f)[None, :]
        eb[h, 1] = np.exp(-sl * (511.0 - f))[None, :]
    dev["l0a_eb"] = A(eb.astype(np.float32))
    T = np.zeros((16, 4, 128, 512), dtype=np.float64)
    for h in range(16):
        for o in range(4):
            T[h, o] = -8.0 * slopes[h] * np.abs(f[None, :] - p[:, None] - 128.0 * o)
    dev["l0a_T"] = A(T.astype(np.float32))

    dev["l0b_wqT"] = A(np.asarray(inputs["L0_attn_b_Wq"]).T)
    dev["l0b_wkT"] = A(np.asarray(inputs["L0_attn_b_Wk"]).T)
    dev["l0b_wvT"] = A(np.asarray(inputs["L0_attn_b_Wv"]).T)
    dev["l0b_woT"] = A(np.asarray(inputs["L0_attn_b_Wout"]).T)
    dev["l0b_bout"] = A(np.asarray(inputs["L0_attn_b_bout"]).reshape(8, 128).T)

    dev["l1a_wqT"] = A(np.asarray(inputs["L1_attn_a_Wq_w"]).T)
    dev["l1a_wkT"] = A(np.asarray(inputs["L1_attn_a_Wk_w"]).T)
    dev["l1a_wvT"] = A(np.asarray(inputs["L1_attn_a_Wv_w"]).T)
    dev["l1a_woT"] = A(np.asarray(inputs["L1_attn_a_Wo_w"]).T)
    dev["l1a_bq"] = A(np.asarray(inputs["L1_attn_a_Wq_b"]).reshape(8, 128).T)
    dev["l1a_bk"] = A(np.asarray(inputs["L1_attn_a_Wk_b"]).reshape(8, 128).T)
    dev["l1a_bv"] = A(np.asarray(inputs["L1_attn_a_Wv_b"]).reshape(8, 128).T)
    dev["l1a_bo"] = A(np.asarray(inputs["L1_attn_a_Wo_b"]).reshape(8, 128).T)
    pp = np.arange(128)[:, None]
    ff = np.arange(512)[None, :]
    cm = np.zeros((4, 128, 512), dtype=np.float32)
    for o in range(4):
        cm[o] = (128 * o + pp <= ff).astype(np.float32)
    dev["l1a_cmask"] = cm
    cmT = np.zeros((4, 128, 512), dtype=np.float32)
    for qs in range(4):
        cmT[qs] = (ff <= 128 * qs + pp).astype(np.float32)
    dev["l1a_cmaskT"] = cmT

    dev["l0f_gate"] = A(inputs["L0_ffn_gate_w"])
    sel8 = np.zeros((8, 1024), dtype=np.float32)
    for e in range(8):
        sel8[e, 128 * e:128 * (e + 1)] = 1.0
    dev["sel8"] = sel8
    dev["l0f_wu"] = A(inputs["L0_ffn_Wu"])
    dev["l0f_wv"] = A(inputs["L0_ffn_Wv"])
    dev["l0f_wo"] = A(inputs["L0_ffn_Wo"])

    dev["l1b_wwT"] = A(np.asarray(inputs["L1_attn_b_W_write"]).T)
    dev["l1b_wrT"] = A(np.asarray(inputs["L1_attn_b_W_read"]).T)
    dev["l1b_decay"] = A(np.asarray(inputs["L1_attn_b_decay"]).reshape(1, 1))
    dev["l1b_la"] = A(np.asarray(inputs["L1_attn_b_log_alpha"]).reshape(1, 1))
    pp = np.arange(128, dtype=np.float64)[:, None]
    ff = np.arange(512, dtype=np.float64)[None, :]
    dev["l1b_iotaE"] = A(np.broadcast_to(ff - 64.0, (128, 512)).astype(np.float32))
    it = np.zeros((4, 128, 512), dtype=np.float32)
    tm = np.zeros((4, 128, 512), dtype=np.float32)
    for o in range(4):
        it[o] = (ff - pp - 128.0 * o - 1.0).astype(np.float32)
        tm[o] = ((ff - pp - 128.0 * o) >= 1.0).astype(np.float32)
    dev["l1b_iotaT"] = it
    dev["l1b_tmask"] = tm
    cm = np.zeros((128, 28), dtype=np.float64)
    for m in range(1, 29):
        cm[:, m - 1] = 128.0 * m + 63.0 - pp[:, 0]
    dev["l1b_colm"] = A(cm.astype(np.float32))

    dev["ident"] = np.eye(128, dtype=np.float32)
    return dev


def alloc_common(env, nc, es):
    t = env.t
    sb = lambda name, shape, dt: es.enter_context(nc.sbuf_tensor(name, shape, dt))
    t["xnT"] = sb("xnT", [128, NF * S], BF16)
    t["arena1"] = sb("arena1", [128, 32768], BF16)
    t["arena2"] = sb("arena2", [128, 16384], BF16)
    t["wstage"] = sb("wstage", [128, 8192], BF16)
    t["xf"] = sb("xf", [128, 4096], F32)
    t["yf"] = sb("yf", [128, 2048], F32)
    t["fscr"] = sb("fscr", [128, 640], F32)
    t["acols"] = sb("acols", [128, 912], F32)
    t["cols"] = sb("cols", [128, 256], F32)
    t["epsc"] = sb("epsc", [128, 1], F32)
    t["ident"] = sb("sb_ident", [128, 128], BF16)
    t["identf"] = sb("sb_identf", [128, 128], F32)
    t["psum"] = es.enter_context(nc.psum_tensor("psum", [128, 4096], F32))


def pbank(env, i):
    return env.t["psum"][:, 512 * i:512 * (i + 1)]


def pbank_bf16(env, i):
    return env.t["psum"][:, 512 * i:512 * (i + 1)].bitcast(BF16)


def wslot(env, i):
    return env.t["wstage"][:, 2048 * i:2048 * (i + 1)]


def emit_prologue(env, x_ap):
    p, t = env.plan, env.t
    tt, vv, ss = env.sem("tt"), env.sem("vv"), env.sem("ss")
    lds = [env.sem("ld"), env.sem("ldb")]
    xf = t["xf"]
    stats = t["cols"]
    xn_stage = t["arena2"]

    apply_done = {}
    tr_done = {}
    evac_done = {}

    for i in range(NT):
        buf = i % 2
        xt = xf[:, 1024 * buf:1024 * (buf + 1)]
        waits = [(vv, apply_done[i - 2])] if i >= 2 else []
        v_ld = p.dma("sync", xt, x_ap[128 * i:128 * (i + 1), :], waits=waits,
                     inc=lds[buf])
        p.wait("vector", (lds[buf], v_ld))
        p.emit("vector", "bn_stats", stats[:, 12 * buf:12 * buf + 6], xt[:, 0:512])
        p.emit("vector", "bn_stats", stats[:, 12 * buf + 6:12 * buf + 12],
               xt[:, 512:1024])
        v_ag = p.emit("vector", "bn_aggr", stats[:, 24 + 2 * buf:26 + 2 * buf],
                      stats[:, 12 * buf:12 * buf + 12], inc=vv)
        v_sq = p.emit("scalar", "activation", stats[:, 28 + buf:29 + buf],
                      stats[:, 25 + 2 * buf:26 + 2 * buf], AF.Sqrt,
                      bias=t["epsc"][:], scale=1.0,
                      waits=[(vv, v_ag)], inc=ss)
        v_rc = p.emit("vector", "reciprocal", stats[:, 30 + buf:31 + buf],
                      stats[:, 28 + buf:29 + buf], waits=[(ss, v_sq)], inc=vv)
        if getattr(env, "DEBUG_STATS", False):
            p.emit("vector", "tensor_copy", t["fscr"][:, 3 * i:3 * i + 2],
                   stats[:, 24 + 2 * buf:26 + 2 * buf])
            p.emit("vector", "tensor_copy", t["fscr"][:, 3 * i + 2:3 * i + 3],
                   stats[:, 30 + buf:31 + buf], waits=[(vv, v_rc)], inc=vv)
        xnt = xn_stage[:, 1024 * buf:1024 * (buf + 1)]
        aw = [(tt, tr_done[i - 2])] if i >= 2 else []
        aw.append((vv, v_rc))
        apply_done[i] = p.emit(
            "vector", "tensor_scalar", out=xnt, in0=xt,
            scalar1=stats[:, 24 + 2 * buf:25 + 2 * buf],
            scalar2=stats[:, 30 + buf:31 + buf],
            op0=ALU.subtract, op1=ALU.mult, waits=aw, inc=vv)
        bank = 6 + (i % 2)
        pb = pbank_bf16(env, bank)
        tw = [(vv, apply_done[i])]
        if i >= 2:
            tw.append((vv, evac_done[i - 2]))
        for fc in range(NF):
            r = p.emit("tensor", "transpose", pb[:, 128 * fc:128 * (fc + 1)],
                       xnt[:, 128 * fc:128 * (fc + 1)], t["ident"][:],
                       waits=(tw if fc == 0 else ()),
                       inc=(tt if fc == NF - 1 else None))
        tr_done[i] = r
        dst = t["xnT"][:].rearrange("p (f s) -> p f s", f=NF)[:, :,
                                                             128 * i:128 * (i + 1)]
        src = pb.rearrange("p (f s) -> p f s", f=NF)
        evac_done[i] = p.emit("vector", "tensor_copy", dst, src,
                              waits=[(tt, tr_done[i])], inc=vv)


def emit_out_phase(env, x_ap, y_ap, mk_psum_tile, tcs, bias_col=None, scale=1.0):
    p, t = env.plan, env.t
    ld, tt, vv, ss = (env.sem("ld"), env.sem("tt"), env.sem("vv"),
                      env.sem("ss"))
    oos = [env.sem("oo"), env.sem("oob")]
    stage = t["arena2"]
    so = 8192
    xf, yf = t["xf"], t["yf"]
    st = env.state("out_phase", {"bank_free": {0: 0, 1: 0},
                                 "tbank_free": {0: 0, 1: 0},
                                 "prev_add": env.sem("vv").n,
                                 "ydma": {0: 0, 1: 0}})

    for tc in tcs:
        xw = [(vv, st["prev_add"])] if st["prev_add"] else []
        v_ld = p.dma("sync", xf[:].rearrange("p (a b) -> p a b", a=4),
                     x_ap[512 * tc:512 * (tc + 1), :].rearrange(
                         "(a p) b -> p a b", p=128), waits=xw, inc=ld)
        last_ev = 0
        for oc in range(NF):
            bank = oc % 2
            bf = st["bank_free"][bank]
            war = [] if bf == 0 else [(env.sem(bf[0]), bf[1])]
            sem, val = mk_psum_tile(oc, tc, bank, war)
            dstg = stage[:, so + 512 * oc: so + 512 * (oc + 1)]
            if bias_col is not None:
                last_ev = p.emit("vector", "tensor_scalar", out=dstg,
                                 in0=pbank(env, bank), scalar1=scale,
                                 scalar2=bias_col[:, oc:oc + 1],
                                 op0=ALU.mult, op1=ALU.add,
                                 waits=[(sem, val)], inc=vv)
                st["bank_free"][bank] = ("vv", last_ev)
            else:
                last_ev = p.emit("scalar", "activation", dstg, pbank(env, bank),
                                 AF.Copy, scale=scale, waits=[(sem, val)],
                                 inc=ss)
                st["bank_free"][bank] = ("ss", last_ev)
        lsem = env.sem("vv") if bias_col is not None else env.sem("ss")
        for sub in range(4):
            bank = 2 + (sub % 2)
            pb = pbank_bf16(env, bank)
            tw = [(lsem, last_ev)]
            if st["tbank_free"][sub % 2]:
                tw.append((vv, st["tbank_free"][sub % 2]))
            for oc in range(NF):
                v_tr = p.emit("tensor", "transpose",
                              pb[:, 128 * oc:128 * (oc + 1)],
                              stage[:, so + 512 * oc + 128 * sub:
                                    so + 512 * oc + 128 * (sub + 1)],
                              t["ident"][:],
                              waits=(tw if oc == 0 else ()),
                              inc=(tt if oc == NF - 1 else None))
            ysl = yf[:, 1024 * (sub % 2):1024 * (sub % 2 + 1)]
            yw = [(tt, v_tr), (ld, v_ld)]
            if st["ydma"][sub % 2]:
                yw.append((oos[sub % 2], st["ydma"][sub % 2]))
            v_add = p.emit("vector", "tensor_add", ysl,
                           xf[:, 1024 * sub:1024 * (sub + 1)], pb[:, 0:1024],
                           waits=yw, inc=vv)
            st["tbank_free"][sub % 2] = v_add
            st["ydma"][sub % 2] = p.dma(
                "sync",
                y_ap[512 * tc + 128 * sub:512 * tc + 128 * (sub + 1), :],
                ysl, waits=[(vv, v_add)], inc=oos[sub % 2])
        st["prev_add"] = v_add


def emit_l1f(env, x_ap, y_ap):
    p, t = env.plan, env.t
    tt, vv, ss = env.sem("tt"), env.sem("vv"), env.sem("ss")
    wts_ = [env.sem("wt0"), env.sem("wt1"), env.sem("wt2"), env.sem("wt3")]
    HID = 3072
    NH = HID // 128
    TCB = 1024
    NSUB = TCB // 512
    wgT, wuT, wdT = env.dram["l1f_wgT"], env.dram["l1f_wuT"], env.dram["l1f_wdT"]

    emit_prologue(env, x_ap)
    pro_vv = env.sem("vv").n
    xnT = t["xnT"][:].rearrange("p (f s) -> p f s", f=NF)

    h = t["arena1"]
    ws_done = {}
    gfree = [0]
    ufree = [0]
    h_free = [0]
    sgt_free = {0: 0, 1: 0}

    def load_w(src, hc, slot):
        v = p.dma("gpsimd", wslot(env, slot)[:, 0:1024], src[hc],
                  waits=[(tt, ws_done.get(slot, 0))], inc=wts_[slot])
        return (wts_[slot], v)

    for tcb in range(S // TCB):
        tok0 = TCB * tcb
        for hc in range(NH):
            sg = (2 * hc) % 4
            su = (2 * hc + 1) % 4
            vg = load_w(wgT, hc, sg)
            vu = load_w(wuT, hc, su)
            for tsub in range(NSUB):
                q0 = tok0 + 512 * tsub
                for ic in range(NF):
                    wts = []
                    if ic == 0:
                        wts = [vg, (vv, pro_vv), (ss, gfree[0])]
                    v_g = p.emit("tensor", "matmul", pbank(env, 4),
                                 wslot(env, sg)[:, 128 * ic:128 * (ic + 1)],
                                 xnT[:, ic, q0:q0 + 512],
                                 start=(ic == 0), stop=(ic == NF - 1),
                                 waits=wts, inc=(tt if ic == NF - 1 else None))
                for ic in range(NF):
                    wts = []
                    if ic == 0:
                        wts = [vu, (vv, ufree[0])]
                    v_u = p.emit("tensor", "matmul", pbank(env, 5),
                                 wslot(env, su)[:, 128 * ic:128 * (ic + 1)],
                                 xnT[:, ic, q0:q0 + 512],
                                 start=(ic == 0), stop=(ic == NF - 1),
                                 waits=wts, inc=(tt if ic == NF - 1 else None))
                sgi = tsub % 2
                sg_t = t["arena2"][:, 512 * sgi:512 * (sgi + 1)]
                v_si = p.emit("scalar", "activation", sg_t, pbank(env, 4),
                              AF.Silu, scale=1.0 / 127.0,
                              waits=[(tt, v_g), (vv, sgt_free[sgi])], inc=ss)
                gfree[0] = v_si
                hsl = h[:, hc * TCB + 512 * tsub: hc * TCB + 512 * (tsub + 1)]
                hw = [(ss, v_si), (tt, v_u)]
                if tcb > 0 and hc == 0 and tsub == 0:
                    hw.append((tt, h_free[0]))
                v_h = p.emit("vector", "tensor_mul", hsl, sg_t, pbank(env, 5),
                             waits=hw, inc=vv)
                ufree[0] = v_h
                sgt_free[sgi] = v_h
            ws_done[sg] = tt.n
            ws_done[su] = tt.n
        v_h_all = vv.n

        wd_state = {"oc": -1, "v": 0}

        def mk_down(oc, tc, bank, war, _tok0=tok0, _v_h=v_h_all, _wd=wd_state):
            if _wd["oc"] != oc:
                _wd["v"] = p.dma(
                    "gpsimd", t["wstage"][:, 0:NH * 128], wdT[oc],
                    waits=[(tt, tt.n)], inc=wts_[0])
                _wd["oc"] = oc
            off = 512 * tc - _tok0
            for hc in range(NH):
                wts = []
                if hc == 0:
                    wts = list(war) + [(wts_[0], _wd["v"]), (vv, _v_h)]
                r = p.emit("tensor", "matmul", pbank(env, bank),
                           t["wstage"][:, 128 * hc:128 * (hc + 1)],
                           h[:, hc * TCB + off: hc * TCB + off + 512],
                           start=(hc == 0), stop=(hc == NH - 1),
                           waits=wts, inc=(tt if hc == NH - 1 else None))
            return tt, r

        emit_out_phase(env, x_ap, y_ap, mk_down,
                       range(NSUB * tcb, NSUB * (tcb + 1)),
                       bias_col=None, scale=1.0 / (127.0 * 127.0))
        h_free[0] = tt.n
        for sl in range(4):
            ws_done[sl] = tt.n


def emit_l0f(env, x_ap, y_ap):
    p, t = env.plan, env.t
    tt, vv, ss = env.sem("tt"), env.sem("vv"), env.sem("ss")
    icsem = env.sem("ic")
    wts_ = [env.sem("wt0"), env.sem("wt1"), env.sem("wt2"), env.sem("wt3")]
    NE, R, NR = 8, 256, 2
    TCB = 2048
    NSUB = TCB // 512

    emit_prologue(env, x_ap)
    pro_vv = env.sem("vv").n
    xnT = t["xnT"][:].rearrange("p (f s) -> p f s", f=NF)

    gstage = t["wstage"][:, 8064:8128]
    p.dma("gpsimd", gstage.rearrange("p (a e) -> p a e", a=8),
          env.dram["l0f_gate"].rearrange("(a p) e -> p a e", p=128), inc=icsem)
    sel8 = t["arena2"][0:8, 4608:5632]
    p.dma("gpsimd", sel8, env.dram["sel8"], inc=icsem)
    v_consts = icsem.n
    logitsT = t["xf"]
    lt = t["cols"]
    wrows = t["arena2"][0:8, 0:4096]

    for tc in range(NQ):
        for ic in range(NF):
            wts = []
            if ic == 0:
                wts = [(env.sem("ic"), v_consts), (vv, pro_vv)]
                if tc >= 1:
                    wts.append((vv, env._state.get("l0f_lg_evac", 0)))
            v_lg = p.emit("tensor", "matmul", pbank(env, 4)[0:8, :],
                          gstage[:, 8 * ic:8 * ic + 8],
                          xnT[:, ic, 512 * tc:512 * (tc + 1)],
                          start=(ic == 0), stop=(ic == NF - 1),
                          waits=wts, inc=(tt if ic == NF - 1 else None))
        env._state["l0f_lg_evac"] = p.emit(
            "vector", "tensor_copy", logitsT[0:8, 512 * tc:512 * (tc + 1)],
            pbank(env, 4)[0:8, :], waits=[(tt, v_lg)], inc=vv)
    v_lg_all = env.sem("vv").n

    BIGN = -1.0e9
    g_evac = {"fwd": 0, "bwd": 0}
    for i in range(NT):
        tw = [(vv, v_lg_all)] if i == 0 else [(vv, g_evac["fwd"])]
        v_tr = p.emit("tensor", "transpose", pbank(env, 5)[:, 0:8],
                      logitsT[0:8, 128 * i:128 * (i + 1)], t["identf"][0:8, 0:8],
                      waits=tw, inc=tt)
        l8 = lt[:, 8 * i:8 * i + 8]
        g_evac["fwd"] = p.emit("vector", "tensor_copy", l8, pbank(env, 5)[:, 0:8],
                               waits=[(tt, v_tr)], inc=vv)
        c = t["fscr"]
        p.emit("vector", "reduce_max", c[:, 0:1], l8, axis=AX.X)
        p.emit("vector", "tensor_scalar", out=c[:, 8:16], in0=l8,
               scalar1=c[:, 0:1], scalar2=None, op0=ALU.is_equal)
        p.emit("vector", "scalar_tensor_tensor", out=c[:, 16:24], in0=c[:, 8:16],
               scalar=BIGN, in1=l8, op0=ALU.mult, op1=ALU.add)
        p.emit("vector", "reduce_max", c[:, 1:2], c[:, 16:24], axis=AX.X)
        p.emit("vector", "tensor_scalar", out=c[:, 24:32], in0=l8,
               scalar1=c[:, 1:2], scalar2=None, op0=ALU.is_ge)
        v_nm = p.emit("vector", "tensor_scalar", out=c[:, 2:3], in0=c[:, 0:1],
                      scalar1=-1.0, scalar2=None, op0=ALU.mult, inc=vv)
        v_ex = p.emit("scalar", "activation", c[:, 32:40], l8, AF.Exp,
                      bias=c[:, 2:3], scale=1.0, waits=[(vv, v_nm)], inc=ss)
        p.emit("vector", "tensor_mul", c[:, 40:48], c[:, 32:40], c[:, 24:32],
               waits=[(ss, v_ex)])
        p.emit("vector", "reduce_sum", c[:, 3:4], c[:, 40:48], axis=AX.X)
        p.emit("vector", "reciprocal", c[:, 4:5], c[:, 3:4])
        v_wf = p.emit("vector", "tensor_scalar", out=l8, in0=c[:, 40:48],
                      scalar1=c[:, 4:5], scalar2=None, op0=ALU.mult, inc=vv)
        bw = [(vv, v_wf)]
        if g_evac["bwd"]:
            bw.append((vv, g_evac["bwd"]))
        v_tb = p.emit("tensor", "transpose", pbank(env, 5)[0:8, 128:256],
                      l8, t["identf"][:], waits=bw, inc=tt)
        g_evac["bwd"] = p.emit("vector", "tensor_copy",
                               wrows[:, 128 * i:128 * (i + 1)],
                               pbank(env, 5)[0:8, 128:256],
                               waits=[(tt, v_tb)], inc=vv)
    v_gate_done = p.emit("vector", "engine_nop", inc=vv)
    g_tt_floor = env.sem("tt").n

    sarena = t["arena1"]
    ws_done = {}
    ufree = [0]
    vfree = [0]
    bfree = [0]
    sg_free = {0: 0, 1: 0}
    s_free = [0]

    def load_uv(which, e, rc, slot):
        src = env.dram["l0f_wu" if which == "u" else "l0f_wv"]
        ap = src[e].rearrange("(a p) r -> p a r", p=128)[:, :, 128 * rc:128 * (rc + 1)]
        v = p.dma("gpsimd",
                  wslot(env, slot)[:, 0:1024].rearrange("p (a h) -> p a h", a=8),
                  ap, waits=[(tt, ws_done.get(slot, g_tt_floor))],
                  inc=wts_[slot])
        return (wts_[slot], v)

    for tcb in range(S // TCB):
        tok0 = TCB * tcb
        for e in range(NE):
            vlds = []
            for rc in range(NR):
                vlds.append(load_uv("u", e, rc, 2 * rc))
                vlds.append(load_uv("v", e, rc, 2 * rc + 1))
            for tsub in range(NSUB):
                q0 = tok0 + 512 * tsub
                bcw = [(vv, bfree[0])] if bfree[0] else [(vv, v_gate_done),
                                                          (icsem, v_consts)]
                v_bc = p.emit("tensor", "matmul", pbank(env, 6),
                              sel8[:, 128 * e:128 * (e + 1)],
                              wrows[:, q0:q0 + 512],
                              start=True, stop=True, waits=bcw, inc=tt)
                for rc in range(NR):
                    for ic in range(NF):
                        wts = []
                        if ic == 0:
                            wts = [vlds[2 * rc], (ss, ufree[0])]
                        v_u = p.emit("tensor", "matmul", pbank(env, 4),
                                     wslot(env, 2 * rc)[:, 128 * ic:128 * (ic + 1)],
                                     xnT[:, ic, q0:q0 + 512],
                                     start=(ic == 0), stop=(ic == NF - 1),
                                     waits=wts, inc=(tt if ic == NF - 1 else None))
                    for ic in range(NF):
                        wts = []
                        if ic == 0:
                            wts = [vlds[2 * rc + 1], (vv, vfree[0])]
                        v_v = p.emit("tensor", "matmul", pbank(env, 5),
                                     wslot(env, 2 * rc + 1)[:, 128 * ic:128 * (ic + 1)],
                                     xnT[:, ic, q0:q0 + 512],
                                     start=(ic == 0), stop=(ic == NF - 1),
                                     waits=wts, inc=(tt if ic == NF - 1 else None))
                    sgi = tsub % 2
                    sg_t = t["arena2"][:, 5632 + 512 * sgi:5632 + 512 * (sgi + 1)]
                    v_si = p.emit("scalar", "activation", sg_t, pbank(env, 4),
                                  AF.Silu, scale=1.0,
                                  waits=[(tt, v_u), (vv, sg_free[sgi])], inc=ss)
                    ufree[0] = v_si
                    stmp = t["arena2"][:, 6656 + 512 * sgi:6656 + 512 * (sgi + 1)]
                    v_s = p.emit("vector", "tensor_mul", stmp, sg_t, pbank(env, 5),
                                 waits=[(ss, v_si), (tt, v_v)], inc=vv)
                    vfree[0] = v_s
                    ssl = sarena[:, (e * NR + rc) * 2048 + 512 * tsub:
                                 (e * NR + rc) * 2048 + 512 * (tsub + 1)]
                    swt = [(vv, v_s), (tt, v_bc)]
                    if tcb > 0 and e == 0 and rc == 0 and tsub == 0:
                        swt.append((tt, s_free[0]))
                    v_sp = p.emit("vector", "tensor_mul", ssl, stmp,
                                  pbank(env, 6), waits=swt, inc=vv)
                    sg_free[sgi] = v_sp
                    if rc == NR - 1:
                        bfree[0] = v_sp
            for sl in range(4):
                ws_done[sl] = tt.n
        v_s_all = env.sem("vv").n

        wo_state = {"oc": -1, "v": 0}

        def mk_down(oc, tc, bank, war, _tok0=tok0, _v_s=v_s_all, _wo=wo_state):
            if _wo["oc"] != oc:
                ap = env.dram["l0f_wo"].rearrange(
                    "e (a p) d -> p (e a) d", p=128)[:, :, 128 * oc:128 * (oc + 1)]
                _wo["v"] = p.dma(
                    "gpsimd",
                    t["wstage"][:, 0:16 * 128].rearrange("p (a h) -> p a h", a=16),
                    ap, waits=[(tt, tt.n)], inc=wts_[0])
                _wo["oc"] = oc
            off = 512 * tc - _tok0
            for k in range(16):
                wts = []
                if k == 0:
                    wts = list(war) + [(wts_[0], _wo["v"]), (vv, _v_s)]
                r = p.emit("tensor", "matmul", pbank(env, bank),
                           t["wstage"][:, 128 * k:128 * (k + 1)],
                           sarena[:, k * 2048 + off: k * 2048 + off + 512],
                           start=(k == 0), stop=(k == 15),
                           waits=wts, inc=(tt if k == 15 else None))
            return tt, r

        emit_out_phase(env, x_ap, y_ap, mk_down,
                       range(NSUB * tcb, NSUB * (tcb + 1)),
                       bias_col=None, scale=1.0)
        s_free[0] = tt.n
        for sl in range(4):
            ws_done[sl] = tt.n


def emit_l0b(env, x_ap, y_ap):
    p, t = env.plan, env.t
    tt, vv, ss, ld = (env.sem("tt"), env.sem("vv"), env.sem("ss"),
                      env.sem("ld"))
    oos = [env.sem("oo"), env.sem("oob")]
    lo, ic2 = env.sem("lo"), env.sem("ic2")
    wts_ = [env.sem("wt0"), env.sem("wt1"), env.sem("wt2"), env.sem("wt3")]

    emit_prologue(env, x_ap)
    pro_vv = env.sem("vv").n
    xnT = t["xnT"][:].rearrange("p (f s) -> p f s", f=NF)
    attn_oT = env.dram["attn_oT"]

    a2 = t["arena2"]
    qT = a2[:, 0:4096]
    kT = a2[:, 4096:8192]
    vbuf = a2[:, 8192:12352]
    otst = a2[:, 12352:13376]
    bcast = a2[:, 13376:15424].bitcast(F32)
    probs = t["arena1"]
    fscr = t["fscr"]

    v_ones = p.emit("vector", "memset", fscr[0:1, 512:640], 1.0, inc=vv)
    p.emit("vector", "memset", fscr[64:65, 512:640], 1.0, inc=vv)
    v_bc0 = p.dma("sync", t["cols"][:, 240:248], env.dram["l0b_bout"], inc=ic2)

    accst = t["xf"][:, 0:1024]
    st_probs_free = [0, 0]
    st_pair_state = {}
    pend = []

    def flush_pend():
        while pend:
            pend.pop(0)()

    def stream_w(which, pr, slot):
        src = env.dram[which]
        ap = src.rearrange("(a p) h -> p a h", p=128)[:, :, 128 * pr:128 * (pr + 1)]
        v = p.dma("gpsimd",
                  wslot(env, slot)[:, 0:1024].rearrange("p (a h) -> p a h", a=8),
                  ap, waits=[(tt, st_pair_state.get(("wfree", slot), 0))],
                  inc=wts_[slot])
        return (wts_[slot], v)

    for pr in range(8):
        flush_pend()
        v_wq = stream_w("l0b_wqT", pr, 0)
        v_wk = stream_w("l0b_wkT", pr, 1)
        v_wv = stream_w("l0b_wvT", pr, 2)
        prev_evac = st_pair_state.get("proj_evac", 0)
        for dst, wv, slot, tag in ((qT, v_wq, 0, "q"), (kT, v_wk, 1, "k")):
            for tc in range(NQ):
                bank = tc % 2
                for ic in range(NF):
                    wts = []
                    if ic == 0:
                        wts = [wv, (ss, st_pair_state.get(("pe", bank), 0))]
                        if tc == 0 and tag == "q":
                            wts.append((vv, pro_vv))
                            wts.append((vv, st_pair_state.get("norm_read", 0)))
                    v_mm = p.emit("tensor", "matmul", pbank(env, bank),
                                  wslot(env, slot)[:, 128 * ic:128 * (ic + 1)],
                                  xnT[:, ic, 512 * tc:512 * (tc + 1)],
                                  start=(ic == 0), stop=(ic == NF - 1),
                                  waits=wts, inc=(tt if ic == NF - 1 else None))
                st_pair_state[("pe", bank)] = p.emit(
                    "scalar", "activation", dst[:, 512 * tc:512 * (tc + 1)],
                    pbank(env, bank), AF.Copy, scale=1.0,
                    waits=[(tt, v_mm)], inc=ss)
        v_ms = p.emit("vector", "memset",
                      vbuf[:].rearrange("p (a c) -> p a c", c=65)[:, :, 64:65],
                      1.0, inc=vv)
        for tc in range(NQ):
            bank = tc % 2
            for ic in range(NF):
                wts = []
                if ic == 0:
                    wts = [v_wv, (ss, st_pair_state.get(("pe", bank), 0))]
                v_mm = p.emit("tensor", "matmul", pbank(env, bank),
                              wslot(env, 2)[:, 128 * ic:128 * (ic + 1)],
                              xnT[:, ic, 512 * tc:512 * (tc + 1)],
                              start=(ic == 0), stop=(ic == NF - 1),
                              waits=wts, inc=(tt if ic == NF - 1 else None))
            st_pair_state[("pe", bank)] = p.emit(
                "scalar", "activation", otst[:, 0:512], pbank(env, bank),
                AF.Copy, scale=1.0,
                waits=[(tt, v_mm), (vv, st_pair_state.get("vt_done", 0)),
                       (oos[0], st_pair_state.get(("odma", 0), 0)),
                       (oos[1], st_pair_state.get(("odma", 1), 0))],
                inc=ss)
            v_ev = st_pair_state[("pe", bank)]
            pb = pbank_bf16(env, 6)
            tw = [(ss, v_ev), (vv, st_pair_state.get("vt_done", 0)),
                  (vv, v_ms), (ss, st_pair_state.get(("bce", 0), 0)),
                  (vv, st_pair_state.get(("bcw", 0), 0))]
            for b4 in range(4):
                v_tr = p.emit("tensor", "transpose",
                              pb[:, 128 * b4:128 * (b4 + 1)],
                              otst[:, 128 * b4:128 * (b4 + 1)], t["ident"][:],
                              waits=(tw if b4 == 0 else ()),
                              inc=(tt if b4 == 3 else None))
            kc0 = tc * 4
            srcA = pb[:, 0:512].rearrange("p (a c) -> p a c", c=128)[:, :, 0:64]
            srcB = pb[:, 0:512].rearrange("p (a c) -> p a c", c=128)[:, :, 64:128]
            dstA = vbuf[:, 65 * kc0:65 * (kc0 + 4)].rearrange(
                "p (a c) -> p a c", c=65)[:, :, 0:64]
            dstB = vbuf[:, 2080 + 65 * kc0:2080 + 65 * (kc0 + 4)].rearrange(
                "p (a c) -> p a c", c=65)[:, :, 0:64]
            p.emit("vector", "tensor_copy", dstA, srcA, waits=[(tt, v_tr)])
            st_pair_state["vt_done"] = p.emit(
                "vector", "tensor_copy", dstB, srcB, inc=vv)
        v_proj_done = env.sem("vv").n

        for qc in range(NQ):
            q0 = 512 * qc
            exp_done = {}

            def emit_av(kcc, hh):
                wts = [(ss, exp_done[kcc])]
                if kcc == 0:
                    wts.append((ss, st_pair_state.get(("accfree", hh), 0)))
                return p.emit("tensor", "matmul",
                              pbank(env, 4 + hh)[0:65, :],
                              vbuf[:, 2080 * hh + 65 * kcc:
                                   2080 * hh + 65 * (kcc + 1)],
                              probs[:, 16384 * hh + 512 * kcc:
                                    16384 * hh + 512 * (kcc + 1)],
                              start=(kcc == 0), stop=(kcc == 31),
                              skip_group_check=True, waits=wts,
                              inc=(tt if kcc == 31 else None))

            for kc in range(32):
                bA = kc % 2
                bB = 2 + kc % 2
                wts = []
                if kc < 2:
                    wts = [(vv, v_proj_done),
                           (ss, st_pair_state.get("exp_last", 0))]
                else:
                    wts = [(ss, exp_done[kc - 2])]
                p.emit("tensor", "matmul", pbank(env, bA),
                       kT[0:64, 128 * kc:128 * (kc + 1)],
                       qT[0:64, q0:q0 + 512], start=True, stop=True,
                       tile_position=(0, 0), skip_group_check=True, waits=wts)
                v_s = p.emit("tensor", "matmul", pbank(env, bB),
                             kT[64:128, 128 * kc:128 * (kc + 1)],
                             qT[64:128, q0:q0 + 512], start=True, stop=True,
                             tile_position=(64, 0), skip_group_check=True,
                             inc=tt)
                if kc == 6:
                    flush_pend()
                if kc % 2 == 1:
                    p.emit("scalar", "activation",
                           probs[:, 512 * (kc - 1):512 * (kc + 1)],
                           t["psum"][:, 0:1024], AF.Exp, scale=0.125,
                           waits=[(tt, v_s)], inc=ss)
                    exp_done[kc - 1] = exp_done[kc] = p.emit(
                        "scalar", "activation",
                        probs[:, 16384 + 512 * (kc - 1):16384 + 512 * (kc + 1)],
                        t["psum"][:, 1024:2048], AF.Exp, scale=0.125, inc=ss)
                    if kc >= 3:
                        for kcc in (kc - 3, kc - 2):
                            emit_av(kcc, 0)
                            emit_av(kcc, 1)
            v_avh = {}
            for kcc in (30, 31):
                for hh in range(2):
                    v_avh[hh] = emit_av(kcc, hh)
            st_pair_state["exp_last"] = exp_done[31]
            v_rcs = {}
            for hh in range(2):
                v_ae = p.emit("scalar", "activation",
                              accst[0:65, 512 * hh:512 * (hh + 1)],
                              pbank(env, 4 + hh)[0:65, :], AF.Copy, scale=1.0,
                              waits=[(tt, v_avh[hh]),
                                     (vv, st_pair_state.get(("accrd", hh), 0))],
                              inc=ss)
                st_pair_state[("accfree", hh)] = v_ae
                v_rcs[hh] = p.emit("vector", "reciprocal",
                                   accst[64:65, 512 * hh:512 * (hh + 1)],
                                   accst[64:65, 512 * hh:512 * (hh + 1)],
                                   waits=[(ss, v_ae)], inc=vv)

            def norm_tail(v_rcs=dict(v_rcs), pr_=pr, q0_=q0):
                for hh in range(2):
                    v_bm = p.emit("tensor", "matmul", pbank(env, 6 + hh),
                                  fscr[64:65, 512:640],
                                  accst[64:65, 512 * hh:512 * (hh + 1)],
                                  start=True, stop=True,
                                  skip_group_check=True,
                                  waits=[(vv, v_rcs[hh]),
                                         (ss, st_pair_state.get(("bce", hh), 0))],
                                  inc=tt)
                    st_pair_state[("bm", hh)] = v_bm
                    v_be = p.emit("scalar", "activation",
                                  bcast[:, 512 * hh:512 * (hh + 1)],
                                  pbank(env, 6 + hh), AF.Copy, scale=1.0,
                                  waits=[(tt, v_bm),
                                         (vv, st_pair_state.get(("bcw", hh), 0))],
                                  inc=ss)
                    st_pair_state[("bce", hh)] = v_be
                    v_nm = p.emit("vector", "tensor_mul",
                                  otst[:, 512 * hh:512 * (hh + 1)][0:64, :],
                                  accst[0:64, 512 * hh:512 * (hh + 1)],
                                  bcast[0:64, 512 * hh:512 * (hh + 1)],
                                  waits=[(ss, v_be),
                                         (oos[hh],
                                          st_pair_state.get(("odma", hh), 0))],
                                  inc=vv)
                    st_pair_state[("bcw", hh)] = v_nm
                    st_pair_state[("accrd", hh)] = v_nm
                    st_pair_state["norm_read"] = v_nm
                    st_pair_state[("odma", hh)] = p.dma(
                        "sync",
                        attn_oT[128 * pr_ + 64 * hh:128 * pr_ + 64 * (hh + 1),
                                q0_:q0_ + 512],
                        otst[:, 512 * hh:512 * (hh + 1)][0:64, :],
                        waits=[(vv, v_nm)], inc=oos[hh])

            pend.append(norm_tail)
            st_probs_free[0] = env.sem("tt").n
        st_pair_state[("wfree", 0)] = env.sem("tt").n
        st_pair_state[("wfree", 1)] = env.sem("tt").n
        st_pair_state[("wfree", 2)] = env.sem("tt").n
        st_pair_state["proj_evac"] = env.sem("ss").n

    flush_pend()
    v_attn_all = [env.sem("oo").n, env.sem("oob").n]

    ost = t["arena1"]
    wo_state = {"oc": -1, "v": 0}
    ld_state = {"tc": -1, "v": 0}

    def mk_out(oc, tc, bank, war):
        if ld_state["tc"] != tc:
            ld_state["v"] = p.dma(
                "sync", ost[:, 0:4096].rearrange("p (a c) -> p a c", a=8),
                attn_oT.rearrange("(a p) s -> p a s", p=128)[:, :,
                                                            512 * tc:512 * (tc + 1)],
                waits=[(oos[0], v_attn_all[0]), (oos[1], v_attn_all[1]),
                       (tt, tt.n)], inc=lo)
            ld_state["tc"] = tc
        if wo_state["oc"] != oc:
            ap = env.dram["l0b_woT"].rearrange(
                "(a p) h -> p a h", p=128)[:, :, 128 * oc:128 * (oc + 1)]
            wo_state["v"] = p.dma(
                "gpsimd",
                t["wstage"][:, 0:1024].rearrange("p (a h) -> p a h", a=8),
                ap, waits=[(tt, tt.n)], inc=wts_[0])
            wo_state["oc"] = oc
        for hc in range(NF):
            wts = []
            if hc == 0:
                wts = list(war) + [(wts_[0], wo_state["v"]), (lo, ld_state["v"])]
            r = p.emit("tensor", "matmul", pbank(env, bank),
                       t["wstage"][:, 128 * hc:128 * (hc + 1)],
                       ost[:, 512 * hc:512 * (hc + 1)],
                       start=(hc == 0), stop=(hc == NF - 1),
                       waits=wts, inc=(tt if hc == NF - 1 else None))
        return tt, r

    p.wait("scalar", (ic2, v_bc0))
    p.wait("vector", (ic2, v_bc0))
    emit_out_phase(env, x_ap, y_ap, mk_out, range(NQ),
                   bias_col=t["cols"][:, 240:248], scale=1.0)


def _l0a_regions(qc):
    out = []
    for kc in range(32):
        r = kc - 4 * qc
        if 0 <= r <= 3:
            out.append(("str", r))
        elif r < 0:
            out.append(("low", -r))
        else:
            out.append(("high", r - 3))
    return out


def emit_l0a(env, x_ap, y_ap):
    p, t = env.plan, env.t
    tt, vv, ss, ld = (env.sem("tt"), env.sem("vv"), env.sem("ss"),
                      env.sem("ld"))
    oos = [env.sem("oo"), env.sem("oob")]
    lo, ic2, et = env.sem("lo"), env.sem("ic2"), env.sem("et")
    wts_ = [env.sem("wt0"), env.sem("wt1"), env.sem("wt2"), env.sem("wt3")]

    emit_prologue(env, x_ap)
    pro_vv = env.sem("vv").n
    xnT = t["xnT"][:].rearrange("p (f s) -> p f s", f=NF)
    attn_oT = env.dram["attn_oT"]

    a2 = t["arena2"]
    qT = a2[:, 0:4096]
    kT = a2[:, 4096:8192]
    vbuf = a2[:, 8192:12352]
    otst = a2[:, 12352:13376]
    comb = a2[:, 13376:15424].bitcast(F32)
    probs = t["arena1"]
    fscr = t["fscr"]
    acols = t["acols"]
    etile = t["wstage"][:, 4096:6144]
    ttA = t["wstage"][:, 6144:8192]

    M_skip = [int(np.floor((112.2 / (0.5 ** h) + 127) / 128)) for h in range(16)]

    p.dma("sync", acols[:].rearrange("p (a c) -> p a c", a=4),
          env.dram["l0a_bias"].rearrange("p (a c) -> p a c", a=4), inc=ic2)
    p.dma("sync", t["cols"][:, 240:248], env.dram["l0a_bo"], inc=ic2)
    p.dma("sync", t["cols"][:, 216:224], env.dram["l0a_bq"], inc=ic2)
    p.dma("sync", t["cols"][:, 224:232], env.dram["l0a_bk"], inc=ic2)
    v_bo = p.dma("sync", t["cols"][:, 232:240], env.dram["l0a_bv"], inc=ic2)
    p.wait("scalar", (ic2, v_bo))
    p.wait("vector", (ic2, v_bo))
    p.emit("vector", "memset", fscr[64:65, 512:640], 1.0)
    v_ones = p.emit("vector", "memset", fscr[0:1, 512:640], 1.0, inc=vv)

    st = {}

    def stream_w(which, pr, slot):
        src = env.dram[which]
        ap = src.rearrange("(a p) h -> p a h", p=128)[:, :, 128 * pr:128 * (pr + 1)]
        v = p.dma("gpsimd",
                  wslot(env, slot)[:, 0:1024].rearrange("p (a h) -> p a h", a=8),
                  ap, waits=[(tt, st.get(("wfree", slot), 0))], inc=wts_[slot])
        return (wts_[slot], v)

    PAIRS = getattr(env, "L0A_PAIRS", range(8))
    pend = []

    def flush_pend():
        while pend:
            pend.pop(0)()

    for pr in PAIRS:
        flush_pend()
        v_wq = stream_w("l0a_wqT", pr, 0)
        v_wk = stream_w("l0a_wkT", pr, 1)
        v_wv = stream_w("l0a_wvT", pr, 2)

        for which, wv, slot, dst, bcol0, bank in (
                ("q", v_wq, 0, qT, 216, 0), ("k", v_wk, 1, kT, 224, 1)):
            for tc in range(NQ):
                for ic in range(NF):
                    wts = []
                    if ic == 0:
                        wts = [wv, (vv, st.get(("pe", bank), 0))]
                        if tc == 0 and which == "q":
                            wts.append((vv, st.get(("nmdone", 0), pro_vv)))
                    v_mm = p.emit("tensor", "matmul", pbank(env, bank),
                                  wslot(env, slot)[:, 128 * ic:128 * (ic + 1)],
                                  xnT[:, ic, 512 * tc:512 * (tc + 1)],
                                  start=(ic == 0), stop=(ic == NF - 1),
                                  waits=wts, inc=(tt if ic == NF - 1 else None))
                st[("pe", bank)] = p.emit(
                    "vector", "tensor_scalar",
                    out=dst[:, 512 * tc:512 * (tc + 1)], in0=pbank(env, bank),
                    scalar1=1.0, scalar2=t["cols"][:, bcol0 + pr:bcol0 + pr + 1],
                    op0=ALU.mult, op1=ALU.add,
                    waits=[(tt, v_mm)], inc=vv)
        v_ms = p.emit("vector", "memset",
                      vbuf[:].rearrange("p (a c) -> p a c", c=65)[:, :, 64:65],
                      1.0, inc=vv)
        for tc in range(NQ):
            bank = tc % 2
            for ic in range(NF):
                wts = []
                if ic == 0:
                    wts = [v_wv, (vv, st.get(("pe", bank), 0))]
                v_mm = p.emit("tensor", "matmul", pbank(env, bank),
                              wslot(env, 2)[:, 128 * ic:128 * (ic + 1)],
                              xnT[:, ic, 512 * tc:512 * (tc + 1)],
                              start=(ic == 0), stop=(ic == NF - 1),
                              waits=wts, inc=(tt if ic == NF - 1 else None))
            st[("pe", bank)] = p.emit(
                "vector", "tensor_scalar", out=otst[:, 0:512],
                in0=pbank(env, bank), scalar1=1.0,
                scalar2=t["cols"][:, 232 + pr:233 + pr],
                op0=ALU.mult, op1=ALU.add,
                waits=[(tt, v_mm),
                       (vv, st.get("vt_done", 0)),
                       (oos[0], st.get(("odma", 0), 0)),
                       (oos[1], st.get(("odma", 1), 0))], inc=vv)
            v_ev = st[("pe", bank)]
            pb = pbank_bf16(env, 6)
            tw = [(vv, v_ev), (vv, st.get("vt_done", 0)), (vv, v_ms),
                  (vv, st.get("cfree_all", 0))]
            for b4 in range(4):
                v_tr = p.emit("tensor", "transpose",
                              pb[:, 128 * b4:128 * (b4 + 1)],
                              otst[:, 128 * b4:128 * (b4 + 1)], t["ident"][:],
                              waits=(tw if b4 == 0 else ()),
                              inc=(tt if b4 == 3 else None))
            kc0 = tc * 4
            srcA = pb[:, 0:512].rearrange("p (a c) -> p a c", c=128)[:, :, 0:64]
            srcB = pb[:, 0:512].rearrange("p (a c) -> p a c", c=128)[:, :, 64:128]
            dstA = vbuf[:, 65 * kc0:65 * (kc0 + 4)].rearrange(
                "p (a c) -> p a c", c=65)[:, :, 0:64]
            dstB = vbuf[:, 2080 + 65 * kc0:2080 + 65 * (kc0 + 4)].rearrange(
                "p (a c) -> p a c", c=65)[:, :, 0:64]
            p.emit("vector", "tensor_copy", dstA, srcA, waits=[(tt, v_tr)])
            st["vt_done"] = p.emit("vector", "tensor_copy", dstB, srcB, inc=vv)
        v_proj_done = env.sem("vv").n
        v_proj_tt = env.sem("tt").n

        for hh in range(2):
            for side in range(2):
                p.dma("gpsimd",
                      etile[:, 512 * (2 * hh + side):512 * (2 * hh + side + 1)],
                      env.dram["l0a_eb"][2 * pr + hh, side],
                      waits=[(tt, v_proj_tt), (vv, st.get("cfree_all", 0))],
                      inc=et)
        for o in range(4):
            p.dma("gpsimd", ttA[:, 512 * o:512 * (o + 1)],
                  env.dram["l0a_T"][2 * pr, o],
                  waits=[(tt, v_proj_tt),
                         (vv, st.get("str_add_last", 0))], inc=et)
            p.dma("gpsimd", t["xf"][:, 256 * o:256 * (o + 1)].bitcast(BF16),
                  env.dram["l0a_T"][2 * pr + 1, o],
                  waits=[(vv, st.get("str_add_last", pro_vv))], inc=et)
        v_et_all = et.n

        if getattr(env, "L0A_PROJ_ONLY", False):
            continue
        for qc in range(NQ):
            q0 = 512 * qc
            regions = _l0a_regions(qc)
            comp = {0: [], 1: []}
            for hh in range(2):
                h = 2 * pr + hh
                for kc, (reg, m) in enumerate(regions):
                    if reg in ("low", "high") and m > M_skip[h]:
                        continue
                    comp[hh].append((kc, reg, m))
            bmap = {(0, "low"): 2, (0, "high"): 3, (0, "str"): 4,
                    (1, "low"): 5, (1, "high"): 6, (1, "str"): 7}
            for hh in range(2):
                h = 2 * pr + hh
                base_k = 64 * hh
                pbuf = 16384 * hh
                exp_done = {}
                seen = {}
                for (kc, reg, m) in comp[hh]:
                    seen.setdefault(reg, []).append(kc)
                first_in_reg = {reg: kcs[0] for reg, kcs in seen.items()}
                last_in_reg = {reg: kcs[-1] for reg, kcs in seen.items()}
                ncomp = len(comp[hh])

                def emit_av(j2):
                    kc2, reg2, _m2 = comp[hh][j2]
                    bank2 = bmap[(hh, reg2)]
                    wts2 = [(ss, exp_done[j2])]
                    if kc2 == first_in_reg[reg2]:
                        wts2.append((vv, st.get(("cfree", hh), 0)))
                        if reg2 == "str":
                            wts2.append((vv, st.get(("nmdone", hh), 0)))
                    return p.emit("tensor", "matmul",
                                  pbank(env, bank2)[0:65, :],
                                  vbuf[:, 2080 * hh + 65 * kc2:
                                       2080 * hh + 65 * (kc2 + 1)],
                                  probs[:, pbuf + 512 * kc2:
                                        pbuf + 512 * (kc2 + 1)],
                                  start=(kc2 == first_in_reg[reg2]),
                                  stop=(kc2 == last_in_reg[reg2]),
                                  skip_group_check=True,
                                  waits=wts2,
                                  inc=(tt if kc2 == last_in_reg[reg2]
                                       else None))

                for j, (kc, reg, m) in enumerate(comp[hh]):
                    sbank = j % 2
                    wts = []
                    if j < 2:
                        wts = [(vv, v_proj_done), (tt, st.get("avdone", 0)),
                               (ss, st.get("exp_last", 0))]
                    else:
                        wts = [(ss, exp_done[j - 2])]
                    v_s = p.emit("tensor", "matmul", pbank(env, sbank),
                                 kT[base_k:base_k + 64, 128 * kc:128 * (kc + 1)],
                                 qT[base_k:base_k + 64, q0:q0 + 512],
                                 start=True, stop=True,
                                 tile_position=(base_k, 0),
                                 skip_group_check=True,
                                 waits=wts, inc=tt)
                    ew = [(tt, v_s)]
                    esrc = pbank(env, sbank)
                    if reg == "str":
                        tsl = (ttA[:, 512 * m:512 * (m + 1)] if hh == 0 else
                               t["xf"][:, 256 * m:256 * (m + 1)].bitcast(BF16))
                        stg = t["yf"][:, 512:1024]
                        v_s2 = p.emit("vector", "tensor_add", stg,
                                      pbank(env, sbank), tsl,
                                      waits=[(tt, v_s), (et, v_et_all),
                                             (ss, env.sem("ss").n)],
                                      inc=vv)
                        st["str_add_last"] = v_s2
                        ew = [(vv, v_s2)]
                        esrc = stg
                        bcol = acols[:, 57 * h:57 * h + 1]
                    elif reg == "low":
                        bcol = acols[:, 57 * h + m:57 * h + m + 1]
                    else:
                        bcol = acols[:, 57 * h + 28 + m:57 * h + 28 + m + 1]
                    exp_done[j] = p.emit(
                        "scalar", "activation",
                        probs[:, pbuf + 512 * kc:pbuf + 512 * (kc + 1)],
                        esrc, AF.Exp, bias=bcol, scale=0.125,
                        waits=ew, inc=ss)
                    if hh == 0 and j == min(1, ncomp - 1):
                        flush_pend()
                    if j >= 1:
                        emit_av(j - 1)
                emit_av(ncomp - 1)
                st["avdone"] = env.sem("tt").n
                st["exp_last"] = exp_done[ncomp - 1]

                creg = comb[:, 512 * hh:512 * (hh + 1)]
                have = set(r for (_, r, _) in comp[hh])
                el = etile[0:65, 512 * 2 * hh:512 * (2 * hh + 1)]
                eh = etile[0:65, 512 * (2 * hh + 1):512 * (2 * hh + 2)]
                scr = t["yf"][:, 0:512]
                v_c = p.emit("vector", "tensor_copy", creg[0:65, :],
                             pbank(env, bmap[(hh, "str")])[0:65, :],
                             waits=[(tt, st["avdone"]),
                                    (vv, st.get(("nmdone", hh), 0))], inc=vv)
                if "low" in have:
                    p.emit("vector", "tensor_mul", scr[0:65, :],
                           pbank(env, bmap[(hh, "low")])[0:65, :], el,
                           waits=[(et, v_et_all)])
                    v_c = p.emit("vector", "tensor_add", creg[0:65, :],
                                 creg[0:65, :], scr[0:65, :], inc=vv)
                if "high" in have:
                    p.emit("vector", "tensor_mul", scr[0:65, :],
                           pbank(env, bmap[(hh, "high")])[0:65, :], eh,
                           waits=[(et, v_et_all)])
                    v_c = p.emit("vector", "tensor_add", creg[0:65, :],
                                 creg[0:65, :], scr[0:65, :], inc=vv)
                st[("cfree", hh)] = v_c
                st["cfree_all"] = v_c
                v_rc = p.emit("vector", "reciprocal",
                              creg[64:65, :], creg[64:65, :],
                              waits=[(vv, v_c)], inc=vv)

                def c_tail(v_rc=v_rc, hh=hh, pr_=pr, q0_=q0, creg=creg,
                           sb_=bmap[(hh, "str")]):
                    v_bm = p.emit("tensor", "matmul", pbank(env, sb_),
                                  fscr[64:65, 512:640],
                                  creg[64:65, :],
                                  start=True, stop=True,
                                  skip_group_check=True,
                                  waits=[(vv, v_rc), (vv, v_ones)], inc=tt)
                    st[("bm", hh)] = v_bm
                    v_nm = p.emit("vector", "tensor_mul",
                                  otst[:, 512 * hh:512 * (hh + 1)][0:64, :],
                                  creg[0:64, :], pbank(env, sb_)[0:64, :],
                                  waits=[(tt, v_bm),
                                         (oos[hh],
                                          st.get(("odma", hh), 0))],
                                  inc=vv)
                    st[("nmdone", hh)] = v_nm
                    st[("odma", hh)] = p.dma(
                        "sync",
                        attn_oT[128 * pr_ + 64 * hh:128 * pr_ + 64 * (hh + 1),
                                q0_:q0_ + 512],
                        otst[:, 512 * hh:512 * (hh + 1)][0:64, :],
                        waits=[(vv, v_nm)], inc=oos[hh])

                pend.append(c_tail)
        st[("wfree", 0)] = env.sem("tt").n
        st[("wfree", 1)] = env.sem("tt").n
        st[("wfree", 2)] = env.sem("tt").n

    flush_pend()
    v_attn_all = [env.sem("oo").n, env.sem("oob").n]

    ost = t["arena1"]
    wo_state = {"oc": -1, "v": 0}
    ld_state = {"tc": -1, "v": 0}

    def mk_out(oc, tc, bank, war):
        if ld_state["tc"] != tc:
            ld_state["v"] = p.dma(
                "sync", ost[:, 0:4096].rearrange("p (a c) -> p a c", a=8),
                attn_oT.rearrange("(a p) s -> p a s", p=128)[:, :,
                                                            512 * tc:512 * (tc + 1)],
                waits=[(oos[0], v_attn_all[0]), (oos[1], v_attn_all[1]),
                       (tt, tt.n)], inc=lo)
            ld_state["tc"] = tc
        if wo_state["oc"] != oc:
            ap = env.dram["l0a_woT"].rearrange(
                "(a p) h -> p a h", p=128)[:, :, 128 * oc:128 * (oc + 1)]
            wo_state["v"] = p.dma(
                "gpsimd",
                t["wstage"][:, 0:1024].rearrange("p (a h) -> p a h", a=8),
                ap, waits=[(tt, tt.n)], inc=wts_[0])
            wo_state["oc"] = oc
        for hc in range(NF):
            wts = []
            if hc == 0:
                wts = list(war) + [(wts_[0], wo_state["v"]), (lo, ld_state["v"])]
            r = p.emit("tensor", "matmul", pbank(env, bank),
                       t["wstage"][:, 128 * hc:128 * (hc + 1)],
                       ost[:, 512 * hc:512 * (hc + 1)],
                       start=(hc == 0), stop=(hc == NF - 1),
                       waits=wts, inc=(tt if hc == NF - 1 else None))
        return tt, r

    if getattr(env, "L0A_SKIP_OUT", False):
        return
    p.wait("scalar", (ic2, v_bo))
    p.wait("vector", (ic2, v_bo))
    emit_out_phase(env, x_ap, y_ap, mk_out, range(NQ),
                   bias_col=t["cols"][:, 240:248], scale=1.0)


def emit_l1a(env, x_ap, y_ap):
    p, t = env.plan, env.t
    tt, vv, ss = env.sem("tt"), env.sem("vv"), env.sem("ss")
    oos = [env.sem("oo"), env.sem("oob")]
    lo, ic2, et = env.sem("lo"), env.sem("ic2"), env.sem("et")
    wts_ = [env.sem("wt0"), env.sem("wt1"), env.sem("wt2"), env.sem("wt3")]

    emit_prologue(env, x_ap)
    pro_vv = env.sem("vv").n
    xnT = t["xnT"][:].rearrange("p (f s) -> p f s", f=NF)
    attn_oT = env.dram["attn_oT"]

    a2 = t["arena2"]
    qT = a2[:, 0:4096]
    kT = a2[:, 4096:8192]
    vbuf = a2[:, 8192:12352]
    otst = a2[:, 12352:13376]
    probsK = a2[:, 13376:15424]
    pk1 = [t["xf"][:, 2560 + 256 * i:2816 + 256 * i].bitcast(BF16)
           for i in range(4)]
    ptmp4 = {(0, 0): a2[:, 15424:15936],
             (0, 1): t["xf"][:, 1280:1536].bitcast(BF16),
             (1, 0): t["xf"][:, 1024:1280].bitcast(BF16),
             (1, 1): t["xf"][:, 3584:3840].bitcast(BF16)}
    creg_all = t["xf"][:, 1536:2560]
    probs = t["arena1"]
    fscr = t["fscr"]
    cmask = t["wstage"][:, 6144:8192]

    p.dma("sync", t["cols"][:, 240:248], env.dram["l1a_bo"], inc=ic2)
    p.dma("sync", t["cols"][:, 216:224], env.dram["l1a_bq"], inc=ic2)
    p.dma("sync", t["cols"][:, 224:232], env.dram["l1a_bk"], inc=ic2)
    v_b = p.dma("sync", t["cols"][:, 232:240], env.dram["l1a_bv"], inc=ic2)
    p.wait("scalar", (ic2, v_b))
    p.wait("vector", (ic2, v_b))
    acols = t["acols"]
    ones2 = [acols[0:1, 0:128], acols[64:65, 0:128]]
    onesb = a2[:, 16383:16384]
    p.emit("vector", "memset", ones2[0], 1.0)
    p.emit("vector", "memset", ones2[1], 1.0)
    v_ones = p.emit("vector", "memset", onesb, 1.0, inc=vv)
    st = {}
    pend = []

    def flush_pend():
        while pend:
            pend.pop(0)()

    def stream_w(which, pr, slot):
        src = env.dram[which]
        ap = src.rearrange("(a p) h -> p a h", p=128)[:, :, 128 * pr:128 * (pr + 1)]
        v = p.dma("gpsimd",
                  wslot(env, slot)[:, 0:1024].rearrange("p (a h) -> p a h", a=8),
                  ap, waits=[(tt, st.get(("wfree", slot), 0))], inc=wts_[slot])
        return (wts_[slot], v)

    for pr in range(getattr(env, "L1A_PAIRS", 8)):
        flush_pend()
        v_wq = stream_w("l1a_wqT", pr, 0)
        v_wk = stream_w("l1a_wkT", pr, 1)
        v_wv = stream_w("l1a_wvT", pr, 2)
        if pr == 0:
            for o in range(4):
                p.dma("gpsimd", cmask[:, 512 * o:512 * (o + 1)],
                      env.dram["l1a_cmaskT"][o], waits=[(tt, tt.n)], inc=et)
            v_cm = et.n

        for which, wv, slot, dst, bcol0, bank in (
                ("q", v_wq, 0, qT, 216, 0), ("k", v_wk, 1, kT, 224, 1)):
            for tc in range(NQ):
                for ic in range(NF):
                    wts = []
                    if ic == 0:
                        wts = [wv, (vv, st.get(("pe", bank), 0)),
                               (ss, st.get(("sb", bank), 0))]
                        if tc == 0 and which == "q":
                            wts.append((vv, st.get("norm_done", pro_vv)))
                    v_mm = p.emit("tensor", "matmul", pbank(env, bank),
                                  wslot(env, slot)[:, 128 * ic:128 * (ic + 1)],
                                  xnT[:, ic, 512 * tc:512 * (tc + 1)],
                                  start=(ic == 0), stop=(ic == NF - 1),
                                  waits=wts, inc=(tt if ic == NF - 1 else None))
                st[("pe", bank)] = p.emit(
                    "vector", "tensor_scalar",
                    out=dst[:, 512 * tc:512 * (tc + 1)], in0=pbank(env, bank),
                    scalar1=1.0, scalar2=t["cols"][:, bcol0 + pr:bcol0 + pr + 1],
                    op0=ALU.mult, op1=ALU.add,
                    waits=[(tt, v_mm)], inc=vv)
        v_ms = p.emit("vector", "memset",
                      vbuf[:].rearrange("p (a c) -> p a c", c=65)[:, :, 64:65],
                      1.0, inc=vv)
        for tc in range(NQ):
            bank = tc % 2
            for ic in range(NF):
                wts = []
                if ic == 0:
                    wts = [v_wv, (vv, st.get(("pe", bank), 0)),
                           (ss, st.get(("sb", bank), 0))]
                v_mm = p.emit("tensor", "matmul", pbank(env, bank),
                              wslot(env, 2)[:, 128 * ic:128 * (ic + 1)],
                              xnT[:, ic, 512 * tc:512 * (tc + 1)],
                              start=(ic == 0), stop=(ic == NF - 1),
                              waits=wts, inc=(tt if ic == NF - 1 else None))
            st[("pe", bank)] = p.emit(
                "vector", "tensor_scalar", out=otst[:, 0:512],
                in0=pbank(env, bank), scalar1=1.0,
                scalar2=t["cols"][:, 232 + pr:233 + pr],
                op0=ALU.mult, op1=ALU.add,
                waits=[(tt, v_mm), (vv, st.get("vt_done", 0)),
                       (oos[0], st.get(("odma", 0), 0)),
                       (oos[1], st.get(("odma", 1), 0))], inc=vv)
            v_ev = st[("pe", bank)]
            pb = pbank_bf16(env, 6)
            tw = [(vv, v_ev), (vv, st.get("vt_done", 0)), (vv, v_ms),
                  (vv, st.get("b6free", 0)), (vv, st.get(("tpe", 0), 0)),
                  (vv, st.get(("cnorm", 0), 0))]
            for b4 in range(4):
                v_tr = p.emit("tensor", "transpose",
                              pb[:, 128 * b4:128 * (b4 + 1)],
                              otst[:, 128 * b4:128 * (b4 + 1)], t["ident"][:],
                              waits=(tw if b4 == 0 else ()),
                              inc=(tt if b4 == 3 else None))
            kc0 = tc * 4
            srcA = pb[:, 0:512].rearrange("p (a c) -> p a c", c=128)[:, :, 0:64]
            srcB = pb[:, 0:512].rearrange("p (a c) -> p a c", c=128)[:, :, 64:128]
            dstA = vbuf[:, 65 * kc0:65 * (kc0 + 4)].rearrange(
                "p (a c) -> p a c", c=65)[:, :, 0:64]
            dstB = vbuf[:, 2080 + 65 * kc0:2080 + 65 * (kc0 + 4)].rearrange(
                "p (a c) -> p a c", c=65)[:, :, 0:64]
            p.emit("vector", "tensor_copy", dstA, srcA, waits=[(tt, v_tr)])
            st["vt_done"] = p.emit("vector", "tensor_copy", dstB, srcB, inc=vv)
        v_proj_done = env.sem("vv").n
        if getattr(env, "L1A_DBG", None) == pr:
            dbg = env.nc.dram_tensor("dbg_vbuf", [128, 4160], BF16,
                                     kind="ExternalOutput").ap()
            p.dma("sync", dbg, vbuf[:], waits=[(vv, v_proj_done)],
                  inc=env.sem("oo"))

        suf = {0: acols[0:65, 128:136], 1: acols[0:65, 136:144]}
        bstage = acols[0:65, 144:152]
        for hh in range(2):
            for kb in range(8):
                for i4 in range(4):
                    kc = 4 * kb + i4
                    wts = []
                    if kb == 0 and i4 == 0:
                        wts = [(vv, v_proj_done), (vv, st.get("b6free", 0)),
                               (vv, st.get("suf_done", 0)), (vv, v_ones),
                               (vv, st.get(("tpe", 0), 0)),
                               (vv, st.get(("cnorm", 0), 0))]
                    v_bs = p.emit("tensor", "matmul",
                                  pbank(env, 6)[0:65, kb:kb + 1],
                                  vbuf[:, 2080 * hh + 65 * kc:
                                       2080 * hh + 65 * (kc + 1)],
                                  onesb,
                                  start=(i4 == 0), stop=(i4 == 3),
                                  waits=wts,
                                  inc=(tt if i4 == 3 else None),
                                  skip_group_check=True)
            v_bse = p.emit("vector", "tensor_copy", bstage,
                           pbank(env, 6)[0:65, 0:8], waits=[(tt, v_bs)], inc=vv)
            st["b6free"] = v_bse
            p.emit("vector", "memset", suf[hh][:, 7:8], 0.0)
            for kb in range(6, -1, -1):
                p.emit("vector", "tensor_add", suf[hh][:, kb:kb + 1],
                       suf[hh][:, kb + 1:kb + 2],
                       bstage[:, kb + 1:kb + 2])
            st["suf_done"] = p.emit("vector", "engine_nop", inc=vv)
            if getattr(env, "L1A_DBG", None) == pr and hh == 1:
                dbg2 = env.nc.dram_tensor("dbg_suf", [128, 32], F32,
                                          kind="ExternalOutput").ap()
                p.dma("sync", dbg2, acols[:, 128:160],
                      waits=[(vv, st["suf_done"])], inc=env.sem("oo"))

        for qc in range(getattr(env, "L1A_MAXQC", NQ)):
            q0 = 512 * qc
            nkb = qc + 1

            def chunk_ap(hh, qs, kb):
                off = 16384 * hh + 4096 * qs + 512 * kb
                return probs[:, off:off + 512]

            def emit_A(hh, kb, qc=qc):
                base_k = 64 * hh
                for qs in range(4):
                    c2 = 32 * hh + 4 * kb + qs
                    b = 2 * hh + qs % 2
                    wts = [(ss, st.get(("sb", b), 0))]
                    if kb == 0 and qs == 0:
                        wts.append((vv, v_proj_done))
                    v_s = p.emit("tensor", "matmul", pbank(env, b),
                                 qT[base_k:base_k + 64,
                                    q0 + 128 * qs:q0 + 128 * (qs + 1)],
                                 kT[base_k:base_k + 64,
                                    512 * kb:512 * (kb + 1)],
                                 start=True, stop=True,
                                 tile_position=(base_k, 0),
                                 skip_group_check=True,
                                 waits=wts, inc=tt)
                    v_e1 = p.emit("scalar", "activation",
                                  chunk_ap(hh, qs, kb), pbank(env, b),
                                  AF.Exp, scale=0.125,
                                  waits=[(tt, v_s)], inc=ss)
                    st[("sb", b)] = v_e1
                    if kb == qc:
                        p.emit("vector", "tensor_mul",
                               chunk_ap(hh, qs, kb), chunk_ap(hh, qs, kb),
                               cmask[:, 512 * qs:512 * (qs + 1)],
                               waits=[(ss, v_e1), (et, v_cm)], inc=vv)
                        p.emit("vector", "bn_stats",
                               acols[:, 160 + 6 * c2:166 + 6 * c2],
                               chunk_ap(hh, qs, kb), inc=vv)
                    else:
                        p.emit("vector", "bn_stats",
                               acols[:, 160 + 6 * c2:166 + 6 * c2],
                               chunk_ap(hh, qs, kb),
                               waits=[(ss, v_e1)], inc=vv)

            def emit_B_both(kb, qc=qc, nkb=nkb):
                v_rcs = {}
                for hh in range(2):
                    base = 32 * hh + 4 * kb
                    for qs in range(4):
                        c2 = base + qs
                        p.emit("vector", "bn_aggr",
                               acols[:, 544 + 2 * c2:546 + 2 * c2],
                               acols[:, 160 + 6 * c2:166 + 6 * c2])
                    rr = acols[:, 672 + base:676 + base]
                    mv = acols[:, 544 + 2 * base:552 + 2 * base].rearrange(
                        "p (a c) -> p a c", c=2)[:, :, 0:1]
                    wmax = max(st.get(("rcread", base + q), 0)
                               for q in range(4))
                    p.emit("vector", "tensor_scalar_mul", rr, mv, 512.0,
                           waits=[(ss, wmax)])
                    v_rc4 = p.emit("vector", "reciprocal", rr, rr, inc=vv)
                    for qs in range(4):
                        v_rcs[(hh, qs)] = v_rc4
                ev_last = {}
                for qs in range(4):
                    for hh in range(2):
                        c2 = 32 * hh + 4 * kb + qs
                        pk = (hh, qs % 2)
                        rcol = acols[:, 672 + c2:673 + c2]
                        v_e2 = p.emit("scalar", "activation", ptmp4[pk],
                                      chunk_ap(hh, qs, kb), AF.Exp,
                                      scale=rcol,
                                      waits=[(vv, v_rcs[(hh, qs)]),
                                             (tt, st.get(("tpf", pk), 0))],
                                      inc=ss)
                        st[("rcread", c2)] = v_e2
                        pb = pbank_bf16(env, 6 + hh)
                        for i4 in range(4):
                            v_tr = p.emit(
                                "tensor", "transpose",
                                pb[:, 128 * i4:128 * (i4 + 1)],
                                ptmp4[pk][:, 128 * i4:128 * (i4 + 1)],
                                t["ident"][:],
                                waits=([(ss, v_e2),
                                        (vv, st.get(("tpe", hh), 0)),
                                        (vv, st.get(("cnorm", hh), 0))]
                                       if i4 == 0 else ()),
                                inc=(tt if i4 == 3 else None))
                        st[("tpf", pk)] = v_tr
                        ew = [(tt, v_tr), (tt, st.get(("avl", hh), 0))]
                        if hh == 0:
                            v_ev = p.emit(
                                "vector", "tensor_copy",
                                probsK.rearrange("p (a c) -> p a c",
                                                 c=512)[:, :, 128 * qs:
                                                        128 * (qs + 1)],
                                pb[:, 0:512].rearrange("p (a c) -> p a c",
                                                       c=128),
                                waits=ew, inc=vv)
                        else:
                            for i4 in range(4):
                                v_ev = p.emit(
                                    "vector", "tensor_copy",
                                    pk1[i4][:, 128 * qs:128 * (qs + 1)],
                                    pb[:, 128 * i4:128 * (i4 + 1)],
                                    waits=(ew if i4 == 0 else ()),
                                    inc=(vv if i4 == 3 else None))
                        st[("tpe", hh)] = v_ev
                        ev_last[hh] = v_ev
                for hh in range(2):
                    for i4 in range(4):
                        kc = 4 * kb + i4
                        wts = [(vv, ev_last[hh])] if i4 == 0 else []
                        if not st[("avst", hh)]:
                            wts.append((vv, st.get(("accfree", hh), 0)))
                        rhs = (probsK[:, 512 * i4:512 * (i4 + 1)] if hh == 0
                               else pk1[i4][:])
                        p.emit("tensor", "matmul",
                               pbank(env, 4 + hh)[0:65, :],
                               vbuf[:, 2080 * hh + 65 * kc:
                                    2080 * hh + 65 * (kc + 1)],
                               rhs,
                               start=(not st[("avst", hh)]),
                               stop=(kb == nkb - 1 and i4 == 3),
                               skip_group_check=True, waits=wts,
                               inc=(tt if (kb == nkb - 1 and i4 == 3)
                                    else None))
                        st[("avst", hh)] = True
                    st[("avl", hh)] = env.sem("tt").n

            st[("avst", 0)] = False
            st[("avst", 1)] = False
            for kb in range(nkb):
                for hh in range(2):
                    emit_A(hh, kb)
                if kb == 0:
                    flush_pend()
                if kb > 0:
                    emit_B_both(kb - 1)
            emit_B_both(nkb - 1)

            for hh in range(2):
                creg = creg_all[:, 512 * hh:512 * (hh + 1)]
                v_c = p.emit("vector", "tensor_scalar", out=creg[0:65, :],
                             in0=pbank(env, 4 + hh)[0:65, :],
                             scalar1=suf[hh][:, qc:qc + 1], scalar2=None,
                             op0=ALU.add,
                             waits=[(tt, st[("avl", hh)]),
                                    (vv, st.get(("cregfree", hh), 0))],
                             inc=vv)
                st[("accfree", hh)] = v_c
                v_rc2 = p.emit("vector", "reciprocal",
                               creg[64:65, :], creg[64:65, :], inc=vv)

                def c_tail(v_rc2=v_rc2, hh=hh, pr_=pr, q0_=q0, creg=creg):
                    v_bm = p.emit("tensor", "matmul", pbank(env, 6 + hh),
                                  ones2[1], creg[64:65, :],
                                  start=True, stop=True,
                                  skip_group_check=True,
                                  waits=[(vv, v_rc2),
                                         (vv, st.get(("tpe", hh), 0))],
                                  inc=tt)
                    st[("bm", hh)] = v_bm
                    v_nm = p.emit("vector", "tensor_mul",
                                  otst[:, 512 * hh:512 * (hh + 1)][0:64, :],
                                  creg[0:64, :],
                                  pbank(env, 6 + hh)[0:64, :],
                                  waits=[(tt, v_bm),
                                         (oos[hh],
                                          st.get(("odma", hh), 0))],
                                  inc=vv)
                    st[("cnorm", hh)] = v_nm
                    st[("cregfree", hh)] = v_nm
                    st[("odma", hh)] = p.dma(
                        "sync",
                        attn_oT[128 * pr_ + 64 * hh:
                                128 * pr_ + 64 * (hh + 1), q0_:q0_ + 512],
                        otst[:, 512 * hh:512 * (hh + 1)][0:64, :],
                        waits=[(vv, v_nm)], inc=oos[hh])

                pend.append(c_tail)

        st[("wfree", 0)] = env.sem("tt").n
        st[("wfree", 1)] = env.sem("tt").n
        st[("wfree", 2)] = env.sem("tt").n

    flush_pend()
    v_attn_all = [env.sem("oo").n, env.sem("oob").n]

    ost = t["arena1"]
    wo_state = {"oc": -1, "v": 0}
    ld_state = {"tc": -1, "v": 0}

    def mk_out(oc, tc, bank, war):
        if ld_state["tc"] != tc:
            ld_state["v"] = p.dma(
                "sync", ost[:, 0:4096].rearrange("p (a c) -> p a c", a=8),
                attn_oT.rearrange("(a p) s -> p a s", p=128)[:, :,
                                                            512 * tc:512 * (tc + 1)],
                waits=[(oos[0], v_attn_all[0]), (oos[1], v_attn_all[1]),
                       (tt, tt.n)], inc=lo)
            ld_state["tc"] = tc
        if wo_state["oc"] != oc:
            ap = env.dram["l1a_woT"].rearrange(
                "(a p) h -> p a h", p=128)[:, :, 128 * oc:128 * (oc + 1)]
            wo_state["v"] = p.dma(
                "gpsimd",
                t["wstage"][:, 0:1024].rearrange("p (a h) -> p a h", a=8),
                ap, waits=[(tt, tt.n)], inc=wts_[0])
            wo_state["oc"] = oc
        for hc in range(NF):
            wts = []
            if hc == 0:
                wts = list(war) + [(wts_[0], wo_state["v"]), (lo, ld_state["v"])]
            r = p.emit("tensor", "matmul", pbank(env, bank),
                       t["wstage"][:, 128 * hc:128 * (hc + 1)],
                       ost[:, 512 * hc:512 * (hc + 1)],
                       start=(hc == 0), stop=(hc == NF - 1),
                       waits=wts, inc=(tt if hc == NF - 1 else None))
        return tt, r

    emit_out_phase(env, x_ap, y_ap, mk_out, range(NQ),
                   bias_col=t["cols"][:, 240:248], scale=1.0)


def emit_l1b(env, x_ap, y_ap):
    p, t = env.plan, env.t
    tt, vv, ss = env.sem("tt"), env.sem("vv"), env.sem("ss")
    lo, ic2, et = env.sem("lo"), env.sem("ic2"), env.sem("et")
    wts_ = [env.sem("wt0"), env.sem("wt1"), env.sem("wt2"), env.sem("wt3")]

    emit_prologue(env, x_ap)
    pro_vv = env.sem("vv").n
    xnT = t["xnT"][:].rearrange("p (f s) -> p f s", f=NF)
    acols = t["acols"]
    fscr = t["fscr"]
    probs = t["arena1"]
    vstg = t["arena1"][:, 16384:32768]
    a2 = t["arena2"]
    readsq = a2[:, 0:4096]
    vbuild = a2[:, 4096:8192]
    zsh0 = a2[:, 14400:15424]
    comb = a2[:, 13376:15424].bitcast(F32)
    Etile = t["wstage"][:, 6144:7168].bitcast(F32)
    Ttile = [t["wstage"][:, 2048 + 1024 * o:3072 + 1024 * o].bitcast(F32)
             for o in range(4)]
    mstage = t["wstage"][:, 7168:7680]
    vtok_d = env.dram["vtok_d"]

    p.dma("sync", fscr[0:1, 0:1], env.dram["l1b_decay"],
          allow_slow_non_contiguous=True, inc=ic2)
    p.dma("sync", fscr[0:1, 1:2], env.dram["l1b_la"],
          allow_slow_non_contiguous=True, inc=ic2)
    p.dma("sync", acols[:, 0:512], env.dram["l1b_iotaE"], inc=ic2)
    v_c0 = p.dma("sync", acols[:, 512:540], env.dram["l1b_colm"], inc=ic2)
    p.wait("scalar", (ic2, v_c0))
    p.wait("vector", (ic2, v_c0))
    v_ones = p.emit("vector", "memset", acols[0:1, 768:896], 1.0, inc=vv)
    v_g = p.emit("scalar", "activation", fscr[0:1, 2:3], fscr[0:1, 0:1],
                 AF.Sigmoid, scale=1.0, inc=ss)
    v_lg = p.emit("scalar", "activation", fscr[0:1, 3:4], fscr[0:1, 2:3],
                  AF.Ln, scale=1.0, waits=[(ss, v_g)], inc=ss)
    v_bm = p.emit("tensor", "matmul", pbank(env, 7)[:, 0:1],
                  acols[0:1, 768:896], fscr[0:1, 3:4],
                  start=True, stop=True, waits=[(vv, v_ones), (ss, v_lg)],
                  inc=tt)
    v_bm2 = p.emit("tensor", "matmul", pbank(env, 7)[:, 1:2],
                   acols[0:1, 768:896], fscr[0:1, 1:2],
                   start=True, stop=True, inc=tt)
    lgc = fscr[:, 8:9]
    lac = fscr[:, 9:10]
    v_ev = p.emit("vector", "tensor_copy", fscr[:, 8:10], pbank(env, 7)[:, 0:2],
                  waits=[(tt, v_bm2)], inc=vv)
    v_E = p.emit("scalar", "activation", Etile[:], acols[:, 0:512], AF.Exp,
                 bias=lac, scale=lgc, waits=[(vv, v_ev)], inc=ss)
    v_cf = p.emit("scalar", "activation", fscr[:, 16:44], acols[:, 512:540],
                  AF.Exp, scale=lgc, inc=ss)
    v_T = {}
    prev_tm = 0
    for o in range(4):
        v_i = p.dma("sync", acols[:, 0:512], env.dram["l1b_iotaT"][o],
                    waits=[(ss, v_T[o - 1] if o else v_cf)], inc=ic2)
        v_m = p.dma("gpsimd", mstage, env.dram["l1b_tmask"][o],
                    waits=([(vv, prev_tm)] if o else ()), inc=et)
        v_e1 = p.emit("scalar", "activation", Ttile[o][:], acols[:, 0:512],
                      AF.Exp, bias=lac, scale=lgc,
                      waits=[(ic2, v_i)], inc=ss)
        v_T[o] = v_e1
        prev_tm = p.emit("vector", "tensor_mul", Ttile[o][:], Ttile[o][:],
                         mstage, waits=[(ss, v_e1), (et, v_m)], inc=vv)
    v_setup_v = env.sem("vv").n

    for icx in range(NF):
        p.emit("vector", "memset", zsh0[:, 128 * icx:128 * icx + 1], 0.0)
        p.emit("vector", "tensor_copy",
               zsh0[:, 128 * icx + 1:128 * (icx + 1)],
               xnT[:, icx, 0:127], waits=[(vv, pro_vv)])
    v_zsh = p.emit("vector", "engine_nop", inc=vv)

    st = {}

    for dc in range(NF):
        ap = env.dram["l1b_wwT"].rearrange(
            "(a p) h -> p a h", p=128)[:, :, 128 * dc:128 * (dc + 1)]
        v_w = p.dma("gpsimd",
                    wslot(env, 0)[:, 0:1024].rearrange("p (a h) -> p a h", a=8),
                    ap, waits=[(tt, st.get("w0free", 0))], inc=wts_[0])
        for tc in range(NQ):
            bank = tc % 2
            for icx in range(NF):
                wts = []
                if icx == 0:
                    wts = [(wts_[0], v_w), (vv, pro_vv),
                           (ss, st.get(("ve", bank), 0))]
                v_mm = p.emit("tensor", "matmul", pbank(env, bank),
                              wslot(env, 0)[:, 128 * icx:128 * (icx + 1)],
                              xnT[:, icx, 512 * tc:512 * (tc + 1)],
                              start=(icx == 0), stop=(icx == NF - 1),
                              waits=wts, inc=(tt if icx == NF - 1 else None))
            st[("ve", bank)] = p.emit(
                "scalar", "activation",
                a2[:, 9216 + 512 * tc:9216 + 512 * (tc + 1)],
                pbank(env, bank), AF.Copy, scale=1.0,
                waits=[(tt, v_mm), (tt, st.get("vtr", 0)),
                       (env.sem("oo"), st.get("vout", 0))], inc=ss)
        v_vs = env.sem("ss").n
        for tc4 in range(NQ):
            pb = pbank_bf16(env, 6)
            tw = [(ss, v_vs), (vv, st.get("vtev", 0))]
            for b4 in range(4):
                v_tr = p.emit("tensor", "transpose",
                              pb[:, 128 * b4:128 * (b4 + 1)],
                              a2[:, 9216 + 512 * tc4 + 128 * b4:
                                 9216 + 512 * tc4 + 128 * (b4 + 1)],
                              t["ident"][:],
                              waits=(tw if b4 == 0 else ()),
                              inc=(tt if b4 == 3 else None))
            st["vtr"] = v_tr
            kc0 = 4 * tc4
            src4 = pb[:, 0:512].rearrange("p (a c) -> p a c", c=128)
            dst4 = vbuild[:].rearrange("p (k c) -> p k c", k=32)[:, kc0:kc0 + 4, :]
            st["vtev"] = p.emit("vector", "tensor_copy", dst4, src4,
                                waits=[(tt, v_tr)], inc=vv)
        st["w0free"] = env.sem("tt").n
        st["vout"] = p.dma(
            "sync",
            vtok_d.rearrange("(kc pp) d -> pp kc d", pp=128)[:, :,
                                                            128 * dc:128 * (dc + 1)],
            vbuild[:].rearrange("p (k c) -> p k c", k=32),
            waits=[(vv, st["vtev"])], inc=env.sem("oo"))
    v_vtok_oo = env.sem("oo").n

    wr_state = {"oc": -1, "v": 0}
    for qc in range(getattr(env, "L1B_MAXQC", NQ)):
        q0 = 512 * qc
        nkc = 4 * qc + 4
        for kc in range(nkc):
            sbank = kc % 2
            for icx in range(NF):
                wts = []
                if icx == 0:
                    pf = st.get(("pfree", sbank))
                    wts = ([pf] if pf else []) + [(vv, v_zsh)]
                lhs = (zsh0[:, 128 * icx:128 * (icx + 1)] if kc == 0 else
                       xnT[:, icx, 128 * kc - 1:128 * kc + 127])
                v_s = p.emit("tensor", "matmul", pbank(env, sbank), lhs,
                             xnT[:, icx, q0:q0 + 512],
                             start=(icx == 0), stop=(icx == NF - 1),
                             waits=wts, inc=(tt if icx == NF - 1 else None))
            o = kc - 4 * qc
            pw = [(tt, v_s), (tt, st.get(("pused", kc), 0))]
            if o >= 0:
                v_pe = p.emit("vector", "tensor_mul",
                              probs[:, 512 * kc:512 * (kc + 1)],
                              pbank(env, sbank), Ttile[o][:],
                              waits=pw + [(vv, v_setup_v)], inc=vv)
            else:
                m = -o
                v_pe = p.emit("vector", "tensor_scalar",
                              out=probs[:, 512 * kc:512 * (kc + 1)],
                              in0=pbank(env, sbank),
                              scalar1=fscr[:, 15 + m:16 + m], scalar2=None,
                              op0=ALU.mult,
                              waits=pw + [(ss, v_cf)], inc=vv)
            st[("pfree", sbank)] = (vv, v_pe)
        v_p_all = env.sem("vv").n
        nh = (nkc + 15) // 16
        first_write = {dc: True for dc in range(NF)}
        for hno in range(nh):
            klo, khi = 16 * hno, min(16 * (hno + 1), nkc)
            v_vl = p.dma(
                "sync",
                vstg[:].rearrange("p (k d) -> p k d", k=16)[:, 0:khi - klo, :],
                vtok_d.rearrange("(kc pp) d -> pp kc d", pp=128)[:, klo:khi, :],
                waits=[(env.sem("oo"), v_vtok_oo),
                       (tt, st.get("vstg_free", 0))], inc=lo)
            for dc in range(NF):
                lows = [k for k in range(klo, min(khi, 4 * qc))]
                strs = [k for k in range(max(klo, 4 * qc), khi)]
                av = {}
                for b, kcs in ((4, lows), (5, strs)):
                    for k_i, kc in enumerate(kcs):
                        wts = []
                        if k_i == 0:
                            wts = [(lo, v_vl), (vv, v_p_all),
                                   (vv, st.get("combfree", 0)),
                                   (tt, st.get("outdone", 0))]
                        v_av = p.emit(
                            "tensor", "matmul", pbank(env, b),
                            vstg[:].rearrange("p (k d) -> p k d", k=16)[
                                :, kc - klo, 128 * dc:128 * (dc + 1)],
                            probs[:, 512 * kc:512 * (kc + 1)],
                            start=(k_i == 0), stop=(k_i == len(kcs) - 1),
                            waits=wts, inc=(tt if k_i == len(kcs) - 1
                                            else None))
                    if kcs:
                        av[b] = tt.n
                rq = readsq[:, 512 * dc:512 * (dc + 1)]
                v_cb = None
                if 4 in av:
                    p.emit("vector", "tensor_mul", comb[:, 0:512],
                           pbank(env, 4), Etile[:],
                           waits=[(tt, av[4]), (ss, v_E)])
                    if first_write[dc]:
                        v_cb = p.emit("vector", "tensor_copy", rq,
                                      comb[:, 0:512], inc=vv)
                    else:
                        v_cb = p.emit("vector", "tensor_add", rq, rq,
                                      comb[:, 0:512], inc=vv)
                    first_write[dc] = False
                if 5 in av:
                    if first_write[dc]:
                        v_cb = p.emit("vector", "tensor_copy", rq,
                                      pbank(env, 5),
                                      waits=[(tt, av[5])], inc=vv)
                    else:
                        v_cb = p.emit("vector", "tensor_add", rq, rq,
                                      pbank(env, 5),
                                      waits=[(tt, av[5])], inc=vv)
                    first_write[dc] = False
                st["combfree"] = v_cb
            st["vstg_free"] = env.sem("tt").n
        for kc in range(nkc):
            st[("pused", kc)] = env.sem("tt").n
        v_reads = env.sem("vv").n

        def mk_out(oc, tc, bank, war, _vr=v_reads, _wr=wr_state):
            if _wr["oc"] != oc:
                ap = env.dram["l1b_wrT"].rearrange(
                    "(a p) h -> p a h", p=128)[:, :, 128 * oc:128 * (oc + 1)]
                _wr["v"] = p.dma(
                    "gpsimd",
                    wslot(env, 0)[:, 0:1024].rearrange("p (a h) -> p a h", a=8),
                    ap, waits=[(tt, tt.n)], inc=wts_[0])
                _wr["oc"] = oc
            for ec in range(NF):
                wts = []
                if ec == 0:
                    wts = list(war) + [(wts_[0], _wr["v"]), (vv, _vr)]
                r = p.emit("tensor", "matmul", pbank(env, bank),
                           wslot(env, 0)[:, 128 * ec:128 * (ec + 1)],
                           readsq[:, 512 * ec:512 * (ec + 1)],
                           start=(ec == 0), stop=(ec == NF - 1),
                           waits=wts, inc=(tt if ec == NF - 1 else None))
            return tt, r

        emit_out_phase(env, x_ap, y_ap, mk_out, [qc], bias_col=None, scale=1.0)
        st["outdone"] = env.sem("tt").n
        wr_state["oc"] = -1


EMITTERS = {
    "l1f": emit_l1f,
    "l0f": emit_l0f,
    "l0b": emit_l0b,
    "l0a": emit_l0a,
    "l1a": emit_l1a,
    "l1b": emit_l1b,
}


def build_program(sublayers, dev_arrays, dump=(), serial=False,
                  race_detect=True):
    nc = bass.Bass("TRN2", target_bir_lowering=False,
                   detect_race_conditions=race_detect)
    env = Env()
    env.plan.self_sems = {"vector": env.sems["vv"], "scalar": env.sems["ss"]}
    if serial:
        env.plan.serial_sems = env.sems
    env.nc = nc

    x_in = nc.dram_tensor("x", [S, D], F32, kind="ExternalInput").ap()
    y_out = nc.dram_tensor("y", [S, D], F32, kind="ExternalOutput").ap()
    xs = [nc.dram_tensor(f"xs{i}", [S, D], F32, kind="Internal").ap()
          for i in range(2)]
    env.dram["attn_oT"] = nc.dram_tensor("attn_oT", [D, S], BF16,
                                         kind="Internal").ap()
    env.dram["vtok_d"] = nc.dram_tensor("vtok_d", [S, D], BF16,
                                        kind="Internal").ap()

    for name, arr in dev_arrays.items():
        dt = {np.dtype(np.float32): F32, np.dtype(np.int32): I32}[arr.dtype]
        env.dram[name] = nc.dram_tensor(name, list(arr.shape), dt,
                                        kind="ExternalInput").ap()

    es = ExitStack()
    with es:
        alloc_common(env, nc, es)
        for s in env.sems.values():
            s.h = es.enter_context(nc.semaphore(s.name))
        block = es.enter_context(nc.Block())

        p = env.plan
        for nm in ("xnT", "arena1", "arena2", "wstage", "xf", "yf", "fscr",
                   "cols", "acols"):
            p.emit("vector", "memset", env.t[nm][:], 0.0)
        v_init = p.emit("vector", "engine_nop", inc=env.sem("vv"))
        p.wait("sync", (env.sem("vv"), v_init))
        p.wait("gpsimd", (env.sem("vv"), v_init))
        p.dma("gpsimd", env.t["ident"][:], env.dram["ident"],
              inc=env.sem("ic"))
        p.dma("sync", env.t["identf"][:], env.dram["ident"],
              inc=env.sem("ic2"))
        p.wait("tensor", (env.sem("ic"), 16), (env.sem("ic2"), 16))
        p.emit("vector", "memset", env.t["epsc"][:], LN_EPS,
               inc=env.sem("vv"))

        cur = x_in
        for idx, sl in enumerate(sublayers):
            last = idx == len(sublayers) - 1
            dst = y_out if last else xs[idx % 2]
            sub_reset(env)
            EMITTERS[SUB_ORDER[sl]](env, cur, dst)
            cur = dst
            if not last:
                barrier(env)

        for dname in dump:
            th = env.t[dname]
            sh = list(th.shape) if hasattr(th, "shape") else None
            dout = nc.dram_tensor("dump_" + dname, sh,
                                  th[:].dtype, kind="ExternalOutput").ap()
            barrier(env)
            p.dma("sync", dout, th[:], inc=env.sem("oo"))

        p.wait("sync", (env.sem("oo"), env.sem("oo").n),
               (env.sem("oob"), env.sem("oob").n))

        for ename in Plan.ENGINES:
            def runner(eng, _ops=env.plan.ops[ename]):
                for f in _ops:
                    f(eng)
            getattr(block, ename)(runner)

    return nc


GROUPS = [list(range(8))]


def kernel(**inputs):
    dev = host_prep(inputs)
    x = np.asarray(inputs["L0_attn_a_x"])
    cur = [np.ascontiguousarray(x[c]) for c in range(8)]
    for g in GROUPS:
        nc = build_program(g, dev)
        in_maps = []
        for c in range(8):
            m = dict(dev)
            m["x"] = cur[c]
            in_maps.append(m)
        res = run_bass_kernel_spmd(nc, in_maps, core_ids=list(range(8)))
        cur = [np.ascontiguousarray(r["y"]) for r in res.results]
    return np.stack(cur).astype(np.float32)


# version 10
# speedup vs baseline: 1.1764x; 1.1764x over previous
import numpy as np
import ml_dtypes
from contextlib import ExitStack

import concourse.bass as bass
import concourse.mybir as mybir
from concourse.bass_utils import run_bass_kernel_spmd

F32 = mybir.dt.float32
BF16 = mybir.dt.bfloat16
I32 = mybir.dt.int32
AF = mybir.ActivationFunctionType
ALU = mybir.AluOpType
AX = mybir.AxisListType

S = 4096
D = 1024
NT = S // 128
NF = D // 128
TQ = 512
NQ = S // TQ
H = 16
DK = 64
LN_EPS = 1e-5

SUB_ORDER = ["l0a", "l0f", "l0b", "l0f", "l1a", "l1f", "l1b", "l1f"]


class Sem:
    def __init__(self, name):
        self.name = name
        self.n = 0
        self.h = None

    def bump(self, amt):
        self.n += amt
        return self.n


class Plan:
    ENGINES = ("sync", "gpsimd", "vector", "scalar", "tensor")

    def __init__(self):
        self.ops = {e: [] for e in self.ENGINES}
        self.serial_sems = None
        self.self_sems = {}

    def raw(self, engine, fn):
        self.ops[engine].append(fn)

    def emit(self, engine, method, *args, waits=(), inc=None, inc_amt=1, **kwargs):
        if self.serial_sems is not None:
            waits = tuple(waits) + tuple(
                (s, s.n) for s in self.serial_sems.values() if s.n > 0)
        ssem = self.self_sems.get(engine)
        if ssem is not None:
            if ssem.n > 0:
                waits = tuple(waits) + ((ssem, ssem.n),)
            if inc is None:
                inc = ssem
                inc_amt = 1
        waits = tuple(waits)

        def f(eng):
            for s, v in waits:
                eng.wait_ge(s.h, v)
            ins = getattr(eng, method)(*args, **kwargs)
            if inc is not None:
                ins.then_inc(inc.h, inc_amt)
        self.ops[engine].append(f)
        if inc is not None:
            return inc.bump(inc_amt)
        return None

    def dma(self, queue, out, in_, waits=(), inc=None, **kwargs):
        return self.emit(queue, "dma_start", out=out, in_=in_, waits=waits,
                         inc=inc, inc_amt=16, **kwargs)

    def wait(self, engine, *waits):
        waits = tuple(waits)

        def f(eng):
            for s, v in waits:
                eng.wait_ge(s.h, v)
        self.ops[engine].append(f)


class Env:
    def __init__(self):
        self.plan = Plan()
        self.sems = {}
        self.t = {}
        self.dram = {}
        self._state = {}
        for name in ("ld", "ldb", "lo", "wt", "wt0", "wt1", "wt2", "wt3",
                     "et", "tt", "vv", "ss", "oo", "oob", "ic", "ic2"):
            self.sems[name] = Sem(name)

    def sem(self, name):
        return self.sems[name]

    def state(self, key, default):
        if key not in self._state:
            self._state[key] = default
        return self._state[key]


def barrier(env):
    p = env.plan
    marks = {}
    for e, sname in (("tensor", "tt"), ("vector", "vv"), ("scalar", "ss")):
        s = env.sem(sname)
        v = s.bump(1)

        def mk(sem=s):
            def f(eng):
                eng.nop().then_inc(sem.h, 1)
            return f
        p.raw(e, mk())
        marks[e] = (s, v)
    allm = list(marks.values())
    for e in ("tensor", "vector", "scalar"):
        p.wait(e, *[m for ee, m in marks.items() if ee != e])
    p.wait("gpsimd", *allm)
    p.wait("sync", *allm)


def sub_reset(env):
    env._state = {}


def host_prep(inputs):
    dev = {}
    A = lambda a: np.ascontiguousarray(np.asarray(a))

    def tile_asc(wT, nh):
        a = wT.shape[0] // 128
        return np.ascontiguousarray(
            wT.reshape(a, 128, nh, 128).transpose(2, 1, 0, 3).reshape(
                nh, 128, a * 128))
    dev["l1f_wgT"] = tile_asc(np.asarray(inputs["L1_ffn_w_gate"]).T, 24)
    dev["l1f_wuT"] = tile_asc(np.asarray(inputs["L1_ffn_w_up"]).T, 24)
    dev["l1f_wdT"] = tile_asc(np.asarray(inputs["L1_ffn_w_down"]).T, 8)

    qkv = np.asarray(inputs["L0_attn_a_qkv_w"]).reshape(16, 192, 1024)
    qkv_b = np.asarray(inputs["L0_attn_a_qkv_b"]).reshape(16, 192)
    dev["l0a_wqT"] = tile_asc(A(qkv[:, 0:64, :].reshape(1024, 1024).T), 8)
    dev["l0a_wkT"] = tile_asc(A(qkv[:, 64:128, :].reshape(1024, 1024).T), 8)
    dev["l0a_wvT"] = tile_asc(A(qkv[:, 128:192, :].reshape(1024, 1024).T), 8)
    dev["l0a_bq"] = A(qkv_b[:, 0:64].reshape(1024).reshape(8, 128).T)
    dev["l0a_bk"] = A(qkv_b[:, 64:128].reshape(1024).reshape(8, 128).T)
    dev["l0a_bv"] = A(qkv_b[:, 128:192].reshape(1024).reshape(8, 128).T)
    dev["l0a_woT"] = tile_asc(A(np.asarray(inputs["L0_attn_a_out_w"]).T), 8)
    dev["l0a_bo"] = A(np.asarray(inputs["L0_attn_a_out_b"]).reshape(8, 128).T)
    slopes = 0.5 ** np.arange(16)
    p = np.arange(128, dtype=np.float64)
    f = np.arange(512, dtype=np.float64)
    bias = np.zeros((128, 16 * 57), dtype=np.float64)
    for h in range(16):
        sl = slopes[h]
        for m in range(1, 29):
            bias[:, 57 * h + m] = sl * (p - 64) - sl * (128 * m - 64)
        for mp in range(1, 29):
            bias[:, 57 * h + 28 + mp] = -sl * p - sl * (128 * mp - 127)
    dev["l0a_bias"] = A(bias.astype(np.float32))
    eb = np.zeros((16, 2, 128, 512), dtype=np.float64)
    for h in range(16):
        sl = slopes[h]
        eb[h, 0] = np.exp(-sl * f)[None, :]
        eb[h, 1] = np.exp(-sl * (511.0 - f))[None, :]
    dev["l0a_eb"] = A(eb.astype(np.float32))
    T = np.zeros((16, 4, 128, 512), dtype=np.float64)
    for h in range(16):
        for o in range(4):
            T[h, o] = -8.0 * slopes[h] * np.abs(f[None, :] - p[:, None] - 128.0 * o)
    dev["l0a_T"] = A(T.astype(np.float32))

    dev["l0b_wqT"] = tile_asc(A(np.asarray(inputs["L0_attn_b_Wq"]).T), 8)
    dev["l0b_wkT"] = tile_asc(A(np.asarray(inputs["L0_attn_b_Wk"]).T), 8)
    dev["l0b_wvT"] = tile_asc(A(np.asarray(inputs["L0_attn_b_Wv"]).T), 8)
    dev["l0b_woT"] = tile_asc(A(np.asarray(inputs["L0_attn_b_Wout"]).T), 8)
    dev["l0b_bout"] = A(np.asarray(inputs["L0_attn_b_bout"]).reshape(8, 128).T)

    dev["l1a_wqT"] = tile_asc(A(np.asarray(inputs["L1_attn_a_Wq_w"]).T), 8)
    dev["l1a_wkT"] = tile_asc(A(np.asarray(inputs["L1_attn_a_Wk_w"]).T), 8)
    dev["l1a_wvT"] = tile_asc(A(np.asarray(inputs["L1_attn_a_Wv_w"]).T), 8)
    dev["l1a_woT"] = tile_asc(A(np.asarray(inputs["L1_attn_a_Wo_w"]).T), 8)
    dev["l1a_bq"] = A(np.asarray(inputs["L1_attn_a_Wq_b"]).reshape(8, 128).T)
    dev["l1a_bk"] = A(np.asarray(inputs["L1_attn_a_Wk_b"]).reshape(8, 128).T)
    dev["l1a_bv"] = A(np.asarray(inputs["L1_attn_a_Wv_b"]).reshape(8, 128).T)
    dev["l1a_bo"] = A(np.asarray(inputs["L1_attn_a_Wo_b"]).reshape(8, 128).T)
    pp = np.arange(128)[:, None]
    ff = np.arange(512)[None, :]
    cm = np.zeros((4, 128, 512), dtype=np.float32)
    for o in range(4):
        cm[o] = (128 * o + pp <= ff).astype(np.float32)
    dev["l1a_cmask"] = cm
    cmT = np.zeros((4, 128, 512), dtype=np.float32)
    for qs in range(4):
        cmT[qs] = (ff <= 128 * qs + pp).astype(np.float32)
    dev["l1a_cmaskT"] = cmT

    dev["l0f_gate"] = A(inputs["L0_ffn_gate_w"])
    sel8 = np.zeros((8, 1024), dtype=np.float32)
    for e in range(8):
        sel8[e, 128 * e:128 * (e + 1)] = 1.0
    dev["sel8"] = sel8
    dev["l0f_wu"] = A(inputs["L0_ffn_Wu"])
    dev["l0f_wv"] = A(inputs["L0_ffn_Wv"])
    dev["l0f_wo"] = A(inputs["L0_ffn_Wo"])

    dev["l1b_wwT"] = tile_asc(A(np.asarray(inputs["L1_attn_b_W_write"]).T), 8)
    dev["l1b_wrT"] = tile_asc(A(np.asarray(inputs["L1_attn_b_W_read"]).T), 8)
    dev["l1b_decay"] = A(np.asarray(inputs["L1_attn_b_decay"]).reshape(1, 1))
    dev["l1b_la"] = A(np.asarray(inputs["L1_attn_b_log_alpha"]).reshape(1, 1))
    pp = np.arange(128, dtype=np.float64)[:, None]
    ff = np.arange(512, dtype=np.float64)[None, :]
    dev["l1b_iotaE"] = A(np.broadcast_to(ff - 64.0, (128, 512)).astype(np.float32))
    it = np.zeros((4, 128, 512), dtype=np.float32)
    tm = np.zeros((4, 128, 512), dtype=np.float32)
    for o in range(4):
        it[o] = (ff - pp - 128.0 * o - 1.0).astype(np.float32)
        tm[o] = ((ff - pp - 128.0 * o) >= 1.0).astype(np.float32)
    dev["l1b_iotaT"] = it
    dev["l1b_tmask"] = tm
    cm = np.zeros((128, 28), dtype=np.float64)
    for m in range(1, 29):
        cm[:, m - 1] = 128.0 * m + 63.0 - pp[:, 0]
    dev["l1b_colm"] = A(cm.astype(np.float32))

    dev["ident"] = np.eye(128, dtype=np.float32)
    return dev


def alloc_common(env, nc, es):
    t = env.t
    sb = lambda name, shape, dt: es.enter_context(nc.sbuf_tensor(name, shape, dt))
    t["xnT"] = sb("xnT", [128, NF * S], BF16)
    t["arena1"] = sb("arena1", [128, 32768], BF16)
    t["arena2"] = sb("arena2", [128, 16384], BF16)
    t["wstage"] = sb("wstage", [128, 8192], BF16)
    t["xf"] = sb("xf", [128, 4096], F32)
    t["yf"] = sb("yf", [128, 2048], F32)
    t["fscr"] = sb("fscr", [128, 640], F32)
    t["acols"] = sb("acols", [128, 912], F32)
    t["cols"] = sb("cols", [128, 256], F32)
    t["epsc"] = sb("epsc", [128, 1], F32)
    t["ident"] = sb("sb_ident", [128, 128], BF16)
    t["identf"] = sb("sb_identf", [128, 128], F32)
    t["psum"] = es.enter_context(nc.psum_tensor("psum", [128, 4096], F32))


def pbank(env, i):
    return env.t["psum"][:, 512 * i:512 * (i + 1)]


def pbank_bf16(env, i):
    return env.t["psum"][:, 512 * i:512 * (i + 1)].bitcast(BF16)


def wslot(env, i):
    return env.t["wstage"][:, 2048 * i:2048 * (i + 1)]


def emit_prologue(env, x_ap):
    p, t = env.plan, env.t
    tt, vv, ss = env.sem("tt"), env.sem("vv"), env.sem("ss")
    lds = [env.sem("ld"), env.sem("ldb")]
    xf = t["xf"]
    stats = t["cols"]
    xn_stage = t["arena2"]

    apply_done = {}
    tr_done = {}
    evac_done = {}

    for i in range(NT):
        buf = i % 2
        xt = xf[:, 1024 * buf:1024 * (buf + 1)]
        waits = [(vv, apply_done[i - 2])] if i >= 2 else []
        v_ld = p.dma("sync", xt, x_ap[128 * i:128 * (i + 1), :], waits=waits,
                     inc=lds[buf])
        p.wait("vector", (lds[buf], v_ld))
        p.emit("vector", "bn_stats", stats[:, 12 * buf:12 * buf + 6], xt[:, 0:512])
        p.emit("vector", "bn_stats", stats[:, 12 * buf + 6:12 * buf + 12],
               xt[:, 512:1024])
        v_ag = p.emit("vector", "bn_aggr", stats[:, 24 + 2 * buf:26 + 2 * buf],
                      stats[:, 12 * buf:12 * buf + 12], inc=vv)
        v_sq = p.emit("scalar", "activation", stats[:, 28 + buf:29 + buf],
                      stats[:, 25 + 2 * buf:26 + 2 * buf], AF.Sqrt,
                      bias=t["epsc"][:], scale=1.0,
                      waits=[(vv, v_ag)], inc=ss)
        v_rc = p.emit("vector", "reciprocal", stats[:, 30 + buf:31 + buf],
                      stats[:, 28 + buf:29 + buf], waits=[(ss, v_sq)], inc=vv)
        if getattr(env, "DEBUG_STATS", False):
            p.emit("vector", "tensor_copy", t["fscr"][:, 3 * i:3 * i + 2],
                   stats[:, 24 + 2 * buf:26 + 2 * buf])
            p.emit("vector", "tensor_copy", t["fscr"][:, 3 * i + 2:3 * i + 3],
                   stats[:, 30 + buf:31 + buf], waits=[(vv, v_rc)], inc=vv)
        xnt = xn_stage[:, 1024 * buf:1024 * (buf + 1)]
        aw = [(tt, tr_done[i - 2])] if i >= 2 else []
        aw.append((vv, v_rc))
        apply_done[i] = p.emit(
            "vector", "tensor_scalar", out=xnt, in0=xt,
            scalar1=stats[:, 24 + 2 * buf:25 + 2 * buf],
            scalar2=stats[:, 30 + buf:31 + buf],
            op0=ALU.subtract, op1=ALU.mult, waits=aw, inc=vv)
        bank = 6 + (i % 2)
        pb = pbank_bf16(env, bank)
        tw = [(vv, apply_done[i])]
        if i >= 2:
            tw.append((vv, evac_done[i - 2]))
        for fc in range(NF):
            r = p.emit("tensor", "transpose", pb[:, 128 * fc:128 * (fc + 1)],
                       xnt[:, 128 * fc:128 * (fc + 1)], t["ident"][:],
                       waits=(tw if fc == 0 else ()),
                       inc=(tt if fc == NF - 1 else None))
        tr_done[i] = r
        dst = t["xnT"][:].rearrange("p (f s) -> p f s", f=NF)[:, :,
                                                             128 * i:128 * (i + 1)]
        src = pb.rearrange("p (f s) -> p f s", f=NF)
        evac_done[i] = p.emit("vector", "tensor_copy", dst, src,
                              waits=[(tt, tr_done[i])], inc=vv)


def emit_out_phase(env, x_ap, y_ap, mk_psum_tile, tcs, bias_col=None, scale=1.0):
    p, t = env.plan, env.t
    ld, tt, vv, ss = (env.sem("ld"), env.sem("tt"), env.sem("vv"),
                      env.sem("ss"))
    oos = [env.sem("oo"), env.sem("oob")]
    stage = t["arena2"]
    so = 8192
    xf, yf = t["xf"], t["yf"]
    st = env.state("out_phase", {"bank_free": {0: 0, 1: 0},
                                 "tbank_free": {0: 0, 1: 0},
                                 "prev_add": env.sem("vv").n,
                                 "ydma": {0: 0, 1: 0}})

    for tc in tcs:
        xw = [(vv, st["prev_add"])] if st["prev_add"] else []
        v_ld = p.dma("sync", xf[:].rearrange("p (a b) -> p a b", a=4),
                     x_ap[512 * tc:512 * (tc + 1), :].rearrange(
                         "(a p) b -> p a b", p=128), waits=xw, inc=ld)
        last_ev = 0
        for oc in range(NF):
            bank = oc % 2
            bf = st["bank_free"][bank]
            war = [] if bf == 0 else [(env.sem(bf[0]), bf[1])]
            sem, val = mk_psum_tile(oc, tc, bank, war)
            dstg = stage[:, so + 512 * oc: so + 512 * (oc + 1)]
            if bias_col is not None:
                last_ev = p.emit("vector", "tensor_scalar", out=dstg,
                                 in0=pbank(env, bank), scalar1=scale,
                                 scalar2=bias_col[:, oc:oc + 1],
                                 op0=ALU.mult, op1=ALU.add,
                                 waits=[(sem, val)], inc=vv)
                st["bank_free"][bank] = ("vv", last_ev)
            else:
                last_ev = p.emit("scalar", "activation", dstg, pbank(env, bank),
                                 AF.Copy, scale=scale, waits=[(sem, val)],
                                 inc=ss)
                st["bank_free"][bank] = ("ss", last_ev)
        lsem = env.sem("vv") if bias_col is not None else env.sem("ss")
        for sub in range(4):
            bank = 2 + (sub % 2)
            pb = pbank_bf16(env, bank)
            tw = [(lsem, last_ev)]
            if st["tbank_free"][sub % 2]:
                tw.append((vv, st["tbank_free"][sub % 2]))
            for oc in range(NF):
                v_tr = p.emit("tensor", "transpose",
                              pb[:, 128 * oc:128 * (oc + 1)],
                              stage[:, so + 512 * oc + 128 * sub:
                                    so + 512 * oc + 128 * (sub + 1)],
                              t["ident"][:],
                              waits=(tw if oc == 0 else ()),
                              inc=(tt if oc == NF - 1 else None))
            ysl = yf[:, 1024 * (sub % 2):1024 * (sub % 2 + 1)]
            yw = [(tt, v_tr), (ld, v_ld)]
            if st["ydma"][sub % 2]:
                yw.append((oos[sub % 2], st["ydma"][sub % 2]))
            v_add = p.emit("vector", "tensor_add", ysl,
                           xf[:, 1024 * sub:1024 * (sub + 1)], pb[:, 0:1024],
                           waits=yw, inc=vv)
            st["tbank_free"][sub % 2] = v_add
            st["ydma"][sub % 2] = p.dma(
                "sync",
                y_ap[512 * tc + 128 * sub:512 * tc + 128 * (sub + 1), :],
                ysl, waits=[(vv, v_add)], inc=oos[sub % 2])
        st["prev_add"] = v_add


def emit_l1f(env, x_ap, y_ap):
    p, t = env.plan, env.t
    tt, vv, ss = env.sem("tt"), env.sem("vv"), env.sem("ss")
    wts_ = [env.sem("wt0"), env.sem("wt1"), env.sem("wt2"), env.sem("wt3")]
    HID = 3072
    NH = HID // 128
    TCB = 1024
    NSUB = TCB // 512
    wgT, wuT, wdT = env.dram["l1f_wgT"], env.dram["l1f_wuT"], env.dram["l1f_wdT"]

    emit_prologue(env, x_ap)
    pro_vv = env.sem("vv").n
    xnT = t["xnT"][:].rearrange("p (f s) -> p f s", f=NF)

    h = t["arena1"]
    ws_done = {}
    gfree = [0]
    ufree = [0]
    h_free = [0]
    sgt_free = {0: 0, 1: 0}

    def load_w(src, hc, slot):
        v = p.dma("gpsimd", wslot(env, slot)[:, 0:1024], src[hc],
                  waits=[(tt, ws_done.get(slot, 0))], inc=wts_[slot])
        return (wts_[slot], v)

    for tcb in range(S // TCB):
        tok0 = TCB * tcb
        for hc in range(NH):
            sg = (2 * hc) % 4
            su = (2 * hc + 1) % 4
            vg = load_w(wgT, hc, sg)
            vu = load_w(wuT, hc, su)
            for tsub in range(NSUB):
                q0 = tok0 + 512 * tsub
                for ic in range(NF):
                    wts = []
                    if ic == 0:
                        wts = [vg, (vv, pro_vv), (ss, gfree[0])]
                    v_g = p.emit("tensor", "matmul", pbank(env, 4),
                                 wslot(env, sg)[:, 128 * ic:128 * (ic + 1)],
                                 xnT[:, ic, q0:q0 + 512],
                                 start=(ic == 0), stop=(ic == NF - 1),
                                 waits=wts, inc=(tt if ic == NF - 1 else None))
                for ic in range(NF):
                    wts = []
                    if ic == 0:
                        wts = [vu, (vv, ufree[0])]
                    v_u = p.emit("tensor", "matmul", pbank(env, 5),
                                 wslot(env, su)[:, 128 * ic:128 * (ic + 1)],
                                 xnT[:, ic, q0:q0 + 512],
                                 start=(ic == 0), stop=(ic == NF - 1),
                                 waits=wts, inc=(tt if ic == NF - 1 else None))
                sgi = tsub % 2
                sg_t = t["arena2"][:, 512 * sgi:512 * (sgi + 1)]
                v_si = p.emit("scalar", "activation", sg_t, pbank(env, 4),
                              AF.Silu, scale=1.0 / 127.0,
                              waits=[(tt, v_g), (vv, sgt_free[sgi])], inc=ss)
                gfree[0] = v_si
                hsl = h[:, hc * TCB + 512 * tsub: hc * TCB + 512 * (tsub + 1)]
                hw = [(ss, v_si), (tt, v_u)]
                if tcb > 0 and hc == 0 and tsub == 0:
                    hw.append((tt, h_free[0]))
                v_h = p.emit("vector", "tensor_mul", hsl, sg_t, pbank(env, 5),
                             waits=hw, inc=vv)
                ufree[0] = v_h
                sgt_free[sgi] = v_h
            ws_done[sg] = tt.n
            ws_done[su] = tt.n
        v_h_all = vv.n

        wd_state = {"oc": -1, "v": 0}

        def mk_down(oc, tc, bank, war, _tok0=tok0, _v_h=v_h_all, _wd=wd_state):
            if _wd["oc"] != oc:
                _wd["v"] = p.dma(
                    "gpsimd", t["wstage"][:, 0:NH * 128], wdT[oc],
                    waits=[(tt, tt.n)], inc=wts_[0])
                _wd["oc"] = oc
            off = 512 * tc - _tok0
            for hc in range(NH):
                wts = []
                if hc == 0:
                    wts = list(war) + [(wts_[0], _wd["v"]), (vv, _v_h)]
                r = p.emit("tensor", "matmul", pbank(env, bank),
                           t["wstage"][:, 128 * hc:128 * (hc + 1)],
                           h[:, hc * TCB + off: hc * TCB + off + 512],
                           start=(hc == 0), stop=(hc == NH - 1),
                           waits=wts, inc=(tt if hc == NH - 1 else None))
            return tt, r

        emit_out_phase(env, x_ap, y_ap, mk_down,
                       range(NSUB * tcb, NSUB * (tcb + 1)),
                       bias_col=None, scale=1.0 / (127.0 * 127.0))
        h_free[0] = tt.n
        for sl in range(4):
            ws_done[sl] = tt.n


def emit_l0f(env, x_ap, y_ap):
    p, t = env.plan, env.t
    tt, vv, ss = env.sem("tt"), env.sem("vv"), env.sem("ss")
    icsem = env.sem("ic")
    wts_ = [env.sem("wt0"), env.sem("wt1"), env.sem("wt2"), env.sem("wt3")]
    NE, R, NR = 8, 256, 2
    TCB = 2048
    NSUB = TCB // 512

    emit_prologue(env, x_ap)
    pro_vv = env.sem("vv").n
    xnT = t["xnT"][:].rearrange("p (f s) -> p f s", f=NF)

    gstage = t["wstage"][:, 8064:8128]
    p.dma("gpsimd", gstage.rearrange("p (a e) -> p a e", a=8),
          env.dram["l0f_gate"].rearrange("(a p) e -> p a e", p=128), inc=icsem)
    sel8 = t["arena2"][0:8, 4608:5632]
    p.dma("gpsimd", sel8, env.dram["sel8"], inc=icsem)
    v_consts = icsem.n
    logitsT = t["xf"]
    lt = t["cols"]
    wrows = t["arena2"][0:8, 0:4096]

    for tc in range(NQ):
        for ic in range(NF):
            wts = []
            if ic == 0:
                wts = [(env.sem("ic"), v_consts), (vv, pro_vv)]
                if tc >= 1:
                    wts.append((vv, env._state.get("l0f_lg_evac", 0)))
            v_lg = p.emit("tensor", "matmul", pbank(env, 4)[0:8, :],
                          gstage[:, 8 * ic:8 * ic + 8],
                          xnT[:, ic, 512 * tc:512 * (tc + 1)],
                          start=(ic == 0), stop=(ic == NF - 1),
                          waits=wts, inc=(tt if ic == NF - 1 else None))
        env._state["l0f_lg_evac"] = p.emit(
            "vector", "tensor_copy", logitsT[0:8, 512 * tc:512 * (tc + 1)],
            pbank(env, 4)[0:8, :], waits=[(tt, v_lg)], inc=vv)
    v_lg_all = env.sem("vv").n

    BIGN = -1.0e9
    g_evac = {"fwd": 0, "bwd": 0}
    for i in range(NT):
        tw = [(vv, v_lg_all)] if i == 0 else [(vv, g_evac["fwd"])]
        v_tr = p.emit("tensor", "transpose", pbank(env, 5)[:, 0:8],
                      logitsT[0:8, 128 * i:128 * (i + 1)], t["identf"][0:8, 0:8],
                      waits=tw, inc=tt)
        l8 = lt[:, 8 * i:8 * i + 8]
        g_evac["fwd"] = p.emit("vector", "tensor_copy", l8, pbank(env, 5)[:, 0:8],
                               waits=[(tt, v_tr)], inc=vv)
        c = t["fscr"]
        p.emit("vector", "reduce_max", c[:, 0:1], l8, axis=AX.X)
        p.emit("vector", "tensor_scalar", out=c[:, 8:16], in0=l8,
               scalar1=c[:, 0:1], scalar2=None, op0=ALU.is_equal)
        p.emit("vector", "scalar_tensor_tensor", out=c[:, 16:24], in0=c[:, 8:16],
               scalar=BIGN, in1=l8, op0=ALU.mult, op1=ALU.add)
        p.emit("vector", "reduce_max", c[:, 1:2], c[:, 16:24], axis=AX.X)
        p.emit("vector", "tensor_scalar", out=c[:, 24:32], in0=l8,
               scalar1=c[:, 1:2], scalar2=None, op0=ALU.is_ge)
        v_nm = p.emit("vector", "tensor_scalar", out=c[:, 2:3], in0=c[:, 0:1],
                      scalar1=-1.0, scalar2=None, op0=ALU.mult, inc=vv)
        v_ex = p.emit("scalar", "activation", c[:, 32:40], l8, AF.Exp,
                      bias=c[:, 2:3], scale=1.0, waits=[(vv, v_nm)], inc=ss)
        p.emit("vector", "tensor_mul", c[:, 40:48], c[:, 32:40], c[:, 24:32],
               waits=[(ss, v_ex)])
        p.emit("vector", "reduce_sum", c[:, 3:4], c[:, 40:48], axis=AX.X)
        p.emit("vector", "reciprocal", c[:, 4:5], c[:, 3:4])
        v_wf = p.emit("vector", "tensor_scalar", out=l8, in0=c[:, 40:48],
                      scalar1=c[:, 4:5], scalar2=None, op0=ALU.mult, inc=vv)
        bw = [(vv, v_wf)]
        if g_evac["bwd"]:
            bw.append((vv, g_evac["bwd"]))
        v_tb = p.emit("tensor", "transpose", pbank(env, 5)[0:8, 128:256],
                      l8, t["identf"][:], waits=bw, inc=tt)
        g_evac["bwd"] = p.emit("vector", "tensor_copy",
                               wrows[:, 128 * i:128 * (i + 1)],
                               pbank(env, 5)[0:8, 128:256],
                               waits=[(tt, v_tb)], inc=vv)
    v_gate_done = p.emit("vector", "engine_nop", inc=vv)
    g_tt_floor = env.sem("tt").n

    sarena = t["arena1"]
    ws_done = {}
    ufree = [0]
    vfree = [0]
    bfree = [0]
    sg_free = {0: 0, 1: 0}
    s_free = [0]

    def load_uv(which, e, rc, slot):
        src = env.dram["l0f_wu" if which == "u" else "l0f_wv"]
        ap = src[e].rearrange("(a p) r -> p a r", p=128)[:, :, 128 * rc:128 * (rc + 1)]
        v = p.dma("gpsimd",
                  wslot(env, slot)[:, 0:1024].rearrange("p (a h) -> p a h", a=8),
                  ap, waits=[(tt, ws_done.get(slot, g_tt_floor))],
                  inc=wts_[slot])
        return (wts_[slot], v)

    for tcb in range(S // TCB):
        tok0 = TCB * tcb
        for e in range(NE):
            vlds = []
            for rc in range(NR):
                vlds.append(load_uv("u", e, rc, 2 * rc))
                vlds.append(load_uv("v", e, rc, 2 * rc + 1))
            for tsub in range(NSUB):
                q0 = tok0 + 512 * tsub
                bcw = [(vv, bfree[0])] if bfree[0] else [(vv, v_gate_done),
                                                          (icsem, v_consts)]
                v_bc = p.emit("tensor", "matmul", pbank(env, 6),
                              sel8[:, 128 * e:128 * (e + 1)],
                              wrows[:, q0:q0 + 512],
                              start=True, stop=True, waits=bcw, inc=tt)
                for rc in range(NR):
                    for ic in range(NF):
                        wts = []
                        if ic == 0:
                            wts = [vlds[2 * rc], (ss, ufree[0])]
                        v_u = p.emit("tensor", "matmul", pbank(env, 4),
                                     wslot(env, 2 * rc)[:, 128 * ic:128 * (ic + 1)],
                                     xnT[:, ic, q0:q0 + 512],
                                     start=(ic == 0), stop=(ic == NF - 1),
                                     waits=wts, inc=(tt if ic == NF - 1 else None))
                    for ic in range(NF):
                        wts = []
                        if ic == 0:
                            wts = [vlds[2 * rc + 1], (vv, vfree[0])]
                        v_v = p.emit("tensor", "matmul", pbank(env, 5),
                                     wslot(env, 2 * rc + 1)[:, 128 * ic:128 * (ic + 1)],
                                     xnT[:, ic, q0:q0 + 512],
                                     start=(ic == 0), stop=(ic == NF - 1),
                                     waits=wts, inc=(tt if ic == NF - 1 else None))
                    sgi = tsub % 2
                    sg_t = t["arena2"][:, 5632 + 512 * sgi:5632 + 512 * (sgi + 1)]
                    v_si = p.emit("scalar", "activation", sg_t, pbank(env, 4),
                                  AF.Silu, scale=1.0,
                                  waits=[(tt, v_u), (vv, sg_free[sgi])], inc=ss)
                    ufree[0] = v_si
                    stmp = t["arena2"][:, 6656 + 512 * sgi:6656 + 512 * (sgi + 1)]
                    v_s = p.emit("vector", "tensor_mul", stmp, sg_t, pbank(env, 5),
                                 waits=[(ss, v_si), (tt, v_v)], inc=vv)
                    vfree[0] = v_s
                    ssl = sarena[:, (e * NR + rc) * 2048 + 512 * tsub:
                                 (e * NR + rc) * 2048 + 512 * (tsub + 1)]
                    swt = [(vv, v_s), (tt, v_bc)]
                    if tcb > 0 and e == 0 and rc == 0 and tsub == 0:
                        swt.append((tt, s_free[0]))
                    v_sp = p.emit("vector", "tensor_mul", ssl, stmp,
                                  pbank(env, 6), waits=swt, inc=vv)
                    sg_free[sgi] = v_sp
                    if rc == NR - 1:
                        bfree[0] = v_sp
            for sl in range(4):
                ws_done[sl] = tt.n
        v_s_all = env.sem("vv").n

        wo_state = {"oc": -1, "v": 0}

        def mk_down(oc, tc, bank, war, _tok0=tok0, _v_s=v_s_all, _wo=wo_state):
            if _wo["oc"] != oc:
                ap = env.dram["l0f_wo"].rearrange(
                    "e (a p) d -> p (e a) d", p=128)[:, :, 128 * oc:128 * (oc + 1)]
                _wo["v"] = p.dma(
                    "gpsimd",
                    t["wstage"][:, 0:16 * 128].rearrange("p (a h) -> p a h", a=16),
                    ap, waits=[(tt, tt.n)], inc=wts_[0])
                _wo["oc"] = oc
            off = 512 * tc - _tok0
            for k in range(16):
                wts = []
                if k == 0:
                    wts = list(war) + [(wts_[0], _wo["v"]), (vv, _v_s)]
                r = p.emit("tensor", "matmul", pbank(env, bank),
                           t["wstage"][:, 128 * k:128 * (k + 1)],
                           sarena[:, k * 2048 + off: k * 2048 + off + 512],
                           start=(k == 0), stop=(k == 15),
                           waits=wts, inc=(tt if k == 15 else None))
            return tt, r

        emit_out_phase(env, x_ap, y_ap, mk_down,
                       range(NSUB * tcb, NSUB * (tcb + 1)),
                       bias_col=None, scale=1.0)
        s_free[0] = tt.n
        for sl in range(4):
            ws_done[sl] = tt.n


def emit_l0b(env, x_ap, y_ap):
    p, t = env.plan, env.t
    tt, vv, ss, ld = (env.sem("tt"), env.sem("vv"), env.sem("ss"),
                      env.sem("ld"))
    oos = [env.sem("oo"), env.sem("oob")]
    lo, ic2 = env.sem("lo"), env.sem("ic2")
    wts_ = [env.sem("wt0"), env.sem("wt1"), env.sem("wt2"), env.sem("wt3")]

    emit_prologue(env, x_ap)
    pro_vv = env.sem("vv").n
    xnT = t["xnT"][:].rearrange("p (f s) -> p f s", f=NF)
    attn_oT = env.dram["attn_oT"]

    a2 = t["arena2"]
    qT = a2[:, 0:4096]
    kT = a2[:, 4096:8192]
    vbuf = a2[:, 8192:12352]
    otst = a2[:, 12352:13376]
    bcast = a2[:, 13376:15424].bitcast(F32)
    probs = t["arena1"]
    fscr = t["fscr"]

    v_ones = p.emit("vector", "memset", fscr[0:1, 512:640], 1.0, inc=vv)
    p.emit("vector", "memset", fscr[64:65, 512:640], 1.0, inc=vv)
    v_bc0 = p.dma("sync", t["cols"][:, 240:248], env.dram["l0b_bout"], inc=ic2)

    accst = t["xf"][:, 0:1024]
    st_probs_free = [0, 0]
    st_pair_state = {}
    pend = []

    def flush_pend():
        while pend:
            pend.pop(0)()

    def stream_w(which, pr, slot):
        src = env.dram[which]
        v = p.dma("gpsimd", wslot(env, slot)[:, 0:1024], src[pr],
                  waits=[(tt, st_pair_state.get(("wfree", slot), 0))],
                  inc=wts_[slot])
        return (wts_[slot], v)

    for pr in range(8):
        flush_pend()
        v_wq = stream_w("l0b_wqT", pr, 0)
        v_wk = stream_w("l0b_wkT", pr, 1)
        v_wv = stream_w("l0b_wvT", pr, 2)
        prev_evac = st_pair_state.get("proj_evac", 0)
        for dst, wv, slot, tag in ((qT, v_wq, 0, "q"), (kT, v_wk, 1, "k")):
            for tc in range(NQ):
                bank = tc % 2
                for ic in range(NF):
                    wts = []
                    if ic == 0:
                        wts = [wv, (ss, st_pair_state.get(("pe", bank), 0))]
                        if tc == 0 and tag == "q":
                            wts.append((vv, pro_vv))
                            wts.append((vv, st_pair_state.get("norm_read", 0)))
                    v_mm = p.emit("tensor", "matmul", pbank(env, bank),
                                  wslot(env, slot)[:, 128 * ic:128 * (ic + 1)],
                                  xnT[:, ic, 512 * tc:512 * (tc + 1)],
                                  start=(ic == 0), stop=(ic == NF - 1),
                                  waits=wts, inc=(tt if ic == NF - 1 else None))
                st_pair_state[("pe", bank)] = p.emit(
                    "scalar", "activation", dst[:, 512 * tc:512 * (tc + 1)],
                    pbank(env, bank), AF.Copy, scale=1.0,
                    waits=[(tt, v_mm)], inc=ss)
        v_ms = p.emit("vector", "memset",
                      vbuf[:].rearrange("p (a c) -> p a c", c=65)[:, :, 64:65],
                      1.0, inc=vv)
        for tc in range(NQ):
            bank = tc % 2
            for ic in range(NF):
                wts = []
                if ic == 0:
                    wts = [v_wv, (ss, st_pair_state.get(("pe", bank), 0))]
                v_mm = p.emit("tensor", "matmul", pbank(env, bank),
                              wslot(env, 2)[:, 128 * ic:128 * (ic + 1)],
                              xnT[:, ic, 512 * tc:512 * (tc + 1)],
                              start=(ic == 0), stop=(ic == NF - 1),
                              waits=wts, inc=(tt if ic == NF - 1 else None))
            st_pair_state[("pe", bank)] = p.emit(
                "scalar", "activation", otst[:, 0:512], pbank(env, bank),
                AF.Copy, scale=1.0,
                waits=[(tt, v_mm), (vv, st_pair_state.get("vt_done", 0)),
                       (oos[0], st_pair_state.get(("odma", 0), 0)),
                       (oos[1], st_pair_state.get(("odma", 1), 0))],
                inc=ss)
            v_ev = st_pair_state[("pe", bank)]
            pb = pbank_bf16(env, 6)
            tw = [(ss, v_ev), (vv, st_pair_state.get("vt_done", 0)),
                  (vv, v_ms), (ss, st_pair_state.get(("bce", 0), 0)),
                  (vv, st_pair_state.get(("bcw", 0), 0))]
            for b4 in range(4):
                v_tr = p.emit("tensor", "transpose",
                              pb[:, 128 * b4:128 * (b4 + 1)],
                              otst[:, 128 * b4:128 * (b4 + 1)], t["ident"][:],
                              waits=(tw if b4 == 0 else ()),
                              inc=(tt if b4 == 3 else None))
            kc0 = tc * 4
            srcA = pb[:, 0:512].rearrange("p (a c) -> p a c", c=128)[:, :, 0:64]
            srcB = pb[:, 0:512].rearrange("p (a c) -> p a c", c=128)[:, :, 64:128]
            dstA = vbuf[:, 65 * kc0:65 * (kc0 + 4)].rearrange(
                "p (a c) -> p a c", c=65)[:, :, 0:64]
            dstB = vbuf[:, 2080 + 65 * kc0:2080 + 65 * (kc0 + 4)].rearrange(
                "p (a c) -> p a c", c=65)[:, :, 0:64]
            p.emit("vector", "tensor_copy", dstA, srcA, waits=[(tt, v_tr)])
            st_pair_state["vt_done"] = p.emit(
                "vector", "tensor_copy", dstB, srcB, inc=vv)
        v_proj_done = env.sem("vv").n

        for qc in range(NQ):
            q0 = 512 * qc
            exp_done = {}

            def emit_av(kcc, hh):
                wts = [(ss, exp_done[kcc])]
                if kcc == 0:
                    wts.append((ss, st_pair_state.get(("accfree", hh), 0)))
                return p.emit("tensor", "matmul",
                              pbank(env, 4 + hh)[0:65, :],
                              vbuf[:, 2080 * hh + 65 * kcc:
                                   2080 * hh + 65 * (kcc + 1)],
                              probs[:, 16384 * hh + 512 * kcc:
                                    16384 * hh + 512 * (kcc + 1)],
                              start=(kcc == 0), stop=(kcc == 31),
                              skip_group_check=True, waits=wts,
                              inc=(tt if kcc == 31 else None))

            for kc in range(32):
                bA = kc % 2
                bB = 2 + kc % 2
                wts = []
                if kc < 2:
                    wts = [(vv, v_proj_done),
                           (ss, st_pair_state.get("exp_last", 0))]
                else:
                    wts = [(ss, exp_done[kc - 2])]
                p.emit("tensor", "matmul", pbank(env, bA),
                       kT[0:64, 128 * kc:128 * (kc + 1)],
                       qT[0:64, q0:q0 + 512], start=True, stop=True,
                       tile_position=(0, 0), skip_group_check=True, waits=wts)
                v_s = p.emit("tensor", "matmul", pbank(env, bB),
                             kT[64:128, 128 * kc:128 * (kc + 1)],
                             qT[64:128, q0:q0 + 512], start=True, stop=True,
                             tile_position=(64, 0), skip_group_check=True,
                             inc=tt)
                if kc == 6:
                    flush_pend()
                if kc % 2 == 1:
                    p.emit("scalar", "activation",
                           probs[:, 512 * (kc - 1):512 * (kc + 1)],
                           t["psum"][:, 0:1024], AF.Exp, scale=0.125,
                           waits=[(tt, v_s)], inc=ss)
                    exp_done[kc - 1] = exp_done[kc] = p.emit(
                        "scalar", "activation",
                        probs[:, 16384 + 512 * (kc - 1):16384 + 512 * (kc + 1)],
                        t["psum"][:, 1024:2048], AF.Exp, scale=0.125, inc=ss)
                    if kc >= 3:
                        for kcc in (kc - 3, kc - 2):
                            emit_av(kcc, 0)
                            emit_av(kcc, 1)
            v_avh = {}
            for kcc in (30, 31):
                for hh in range(2):
                    v_avh[hh] = emit_av(kcc, hh)
            st_pair_state["exp_last"] = exp_done[31]
            v_rcs = {}
            for hh in range(2):
                v_ae = p.emit("scalar", "activation",
                              accst[0:65, 512 * hh:512 * (hh + 1)],
                              pbank(env, 4 + hh)[0:65, :], AF.Copy, scale=1.0,
                              waits=[(tt, v_avh[hh]),
                                     (vv, st_pair_state.get(("accrd", hh), 0))],
                              inc=ss)
                st_pair_state[("accfree", hh)] = v_ae
                v_rcs[hh] = p.emit("vector", "reciprocal",
                                   accst[64:65, 512 * hh:512 * (hh + 1)],
                                   accst[64:65, 512 * hh:512 * (hh + 1)],
                                   waits=[(ss, v_ae)], inc=vv)

            def norm_tail(v_rcs=dict(v_rcs), pr_=pr, q0_=q0):
                for hh in range(2):
                    v_bm = p.emit("tensor", "matmul", pbank(env, 6 + hh),
                                  fscr[64:65, 512:640],
                                  accst[64:65, 512 * hh:512 * (hh + 1)],
                                  start=True, stop=True,
                                  skip_group_check=True,
                                  waits=[(vv, v_rcs[hh]),
                                         (ss, st_pair_state.get(("bce", hh), 0))],
                                  inc=tt)
                    st_pair_state[("bm", hh)] = v_bm
                    v_be = p.emit("scalar", "activation",
                                  bcast[:, 512 * hh:512 * (hh + 1)],
                                  pbank(env, 6 + hh), AF.Copy, scale=1.0,
                                  waits=[(tt, v_bm),
                                         (vv, st_pair_state.get(("bcw", hh), 0))],
                                  inc=ss)
                    st_pair_state[("bce", hh)] = v_be
                    v_nm = p.emit("vector", "tensor_mul",
                                  otst[:, 512 * hh:512 * (hh + 1)][0:64, :],
                                  accst[0:64, 512 * hh:512 * (hh + 1)],
                                  bcast[0:64, 512 * hh:512 * (hh + 1)],
                                  waits=[(ss, v_be),
                                         (oos[hh],
                                          st_pair_state.get(("odma", hh), 0))],
                                  inc=vv)
                    st_pair_state[("bcw", hh)] = v_nm
                    st_pair_state[("accrd", hh)] = v_nm
                    st_pair_state["norm_read"] = v_nm
                    st_pair_state[("odma", hh)] = p.dma(
                        "sync",
                        attn_oT[128 * pr_ + 64 * hh:128 * pr_ + 64 * (hh + 1),
                                q0_:q0_ + 512],
                        otst[:, 512 * hh:512 * (hh + 1)][0:64, :],
                        waits=[(vv, v_nm)], inc=oos[hh])

            pend.append(norm_tail)
            st_probs_free[0] = env.sem("tt").n
        st_pair_state[("wfree", 0)] = env.sem("tt").n
        st_pair_state[("wfree", 1)] = env.sem("tt").n
        st_pair_state[("wfree", 2)] = env.sem("tt").n
        st_pair_state["proj_evac"] = env.sem("ss").n

    flush_pend()
    v_attn_all = [env.sem("oo").n, env.sem("oob").n]

    ost = t["arena1"]
    wo_state = {"oc": -1, "v": 0}
    ld_state = {"tc": -1, "v": 0}

    def mk_out(oc, tc, bank, war):
        if ld_state["tc"] != tc:
            ld_state["v"] = p.dma(
                "sync", ost[:, 0:4096].rearrange("p (a c) -> p a c", a=8),
                attn_oT.rearrange("(a p) s -> p a s", p=128)[:, :,
                                                            512 * tc:512 * (tc + 1)],
                waits=[(oos[0], v_attn_all[0]), (oos[1], v_attn_all[1]),
                       (tt, tt.n)], inc=lo)
            ld_state["tc"] = tc
        if wo_state["oc"] != oc:
            wo_state["v"] = p.dma(
                "gpsimd", t["wstage"][:, 0:1024], env.dram["l0b_woT"][oc],
                waits=[(tt, tt.n)], inc=wts_[0])
            wo_state["oc"] = oc
        for hc in range(NF):
            wts = []
            if hc == 0:
                wts = list(war) + [(wts_[0], wo_state["v"]), (lo, ld_state["v"])]
            r = p.emit("tensor", "matmul", pbank(env, bank),
                       t["wstage"][:, 128 * hc:128 * (hc + 1)],
                       ost[:, 512 * hc:512 * (hc + 1)],
                       start=(hc == 0), stop=(hc == NF - 1),
                       waits=wts, inc=(tt if hc == NF - 1 else None))
        return tt, r

    p.wait("scalar", (ic2, v_bc0))
    p.wait("vector", (ic2, v_bc0))
    emit_out_phase(env, x_ap, y_ap, mk_out, range(NQ),
                   bias_col=t["cols"][:, 240:248], scale=1.0)


def _l0a_regions(qc):
    out = []
    for kc in range(32):
        r = kc - 4 * qc
        if 0 <= r <= 3:
            out.append(("str", r))
        elif r < 0:
            out.append(("low", -r))
        else:
            out.append(("high", r - 3))
    return out


def emit_l0a(env, x_ap, y_ap):
    p, t = env.plan, env.t
    tt, vv, ss, ld = (env.sem("tt"), env.sem("vv"), env.sem("ss"),
                      env.sem("ld"))
    oos = [env.sem("oo"), env.sem("oob")]
    lo, ic2, et = env.sem("lo"), env.sem("ic2"), env.sem("et")
    wts_ = [env.sem("wt0"), env.sem("wt1"), env.sem("wt2"), env.sem("wt3")]

    emit_prologue(env, x_ap)
    pro_vv = env.sem("vv").n
    xnT = t["xnT"][:].rearrange("p (f s) -> p f s", f=NF)
    attn_oT = env.dram["attn_oT"]

    a2 = t["arena2"]
    qT = a2[:, 0:4096]
    kT = a2[:, 4096:8192]
    vbuf = a2[:, 8192:12352]
    otst = a2[:, 12352:13376]
    comb = a2[:, 13376:15424].bitcast(F32)
    probs = t["arena1"]
    fscr = t["fscr"]
    acols = t["acols"]
    etile = t["wstage"][:, 4096:6144]
    ttA = t["wstage"][:, 6144:8192]

    M_skip = [int(np.floor((112.2 / (0.5 ** h) + 127) / 128)) for h in range(16)]

    p.dma("sync", acols[:].rearrange("p (a c) -> p a c", a=4),
          env.dram["l0a_bias"].rearrange("p (a c) -> p a c", a=4), inc=ic2)
    p.dma("sync", t["cols"][:, 240:248], env.dram["l0a_bo"], inc=ic2)
    p.dma("sync", t["cols"][:, 216:224], env.dram["l0a_bq"], inc=ic2)
    p.dma("sync", t["cols"][:, 224:232], env.dram["l0a_bk"], inc=ic2)
    v_bo = p.dma("sync", t["cols"][:, 232:240], env.dram["l0a_bv"], inc=ic2)
    p.wait("scalar", (ic2, v_bo))
    p.wait("vector", (ic2, v_bo))
    p.emit("vector", "memset", fscr[64:65, 512:640], 1.0)
    v_ones = p.emit("vector", "memset", fscr[0:1, 512:640], 1.0, inc=vv)

    st = {}

    def stream_w(which, pr, slot):
        src = env.dram[which]
        v = p.dma("gpsimd", wslot(env, slot)[:, 0:1024], src[pr],
                  waits=[(tt, st.get(("wfree", slot), 0))], inc=wts_[slot])
        return (wts_[slot], v)

    PAIRS = getattr(env, "L0A_PAIRS", range(8))
    pend = []

    def flush_pend():
        while pend:
            pend.pop(0)()

    for pr in PAIRS:
        flush_pend()
        v_wq = stream_w("l0a_wqT", pr, 0)
        v_wk = stream_w("l0a_wkT", pr, 1)
        v_wv = stream_w("l0a_wvT", pr, 2)

        for which, wv, slot, dst, bcol0, bank in (
                ("q", v_wq, 0, qT, 216, 0), ("k", v_wk, 1, kT, 224, 1)):
            for tc in range(NQ):
                for ic in range(NF):
                    wts = []
                    if ic == 0:
                        wts = [wv, (vv, st.get(("pe", bank), 0))]
                        if tc == 0 and which == "q":
                            wts.append((vv, st.get(("nmdone", 0), pro_vv)))
                    v_mm = p.emit("tensor", "matmul", pbank(env, bank),
                                  wslot(env, slot)[:, 128 * ic:128 * (ic + 1)],
                                  xnT[:, ic, 512 * tc:512 * (tc + 1)],
                                  start=(ic == 0), stop=(ic == NF - 1),
                                  waits=wts, inc=(tt if ic == NF - 1 else None))
                st[("pe", bank)] = p.emit(
                    "vector", "tensor_scalar",
                    out=dst[:, 512 * tc:512 * (tc + 1)], in0=pbank(env, bank),
                    scalar1=1.0, scalar2=t["cols"][:, bcol0 + pr:bcol0 + pr + 1],
                    op0=ALU.mult, op1=ALU.add,
                    waits=[(tt, v_mm)], inc=vv)
        v_ms = p.emit("vector", "memset",
                      vbuf[:].rearrange("p (a c) -> p a c", c=65)[:, :, 64:65],
                      1.0, inc=vv)
        for tc in range(NQ):
            bank = tc % 2
            for ic in range(NF):
                wts = []
                if ic == 0:
                    wts = [v_wv, (vv, st.get(("pe", bank), 0))]
                v_mm = p.emit("tensor", "matmul", pbank(env, bank),
                              wslot(env, 2)[:, 128 * ic:128 * (ic + 1)],
                              xnT[:, ic, 512 * tc:512 * (tc + 1)],
                              start=(ic == 0), stop=(ic == NF - 1),
                              waits=wts, inc=(tt if ic == NF - 1 else None))
            st[("pe", bank)] = p.emit(
                "vector", "tensor_scalar", out=otst[:, 0:512],
                in0=pbank(env, bank), scalar1=1.0,
                scalar2=t["cols"][:, 232 + pr:233 + pr],
                op0=ALU.mult, op1=ALU.add,
                waits=[(tt, v_mm),
                       (vv, st.get("vt_done", 0)),
                       (oos[0], st.get(("odma", 0), 0)),
                       (oos[1], st.get(("odma", 1), 0))], inc=vv)
            v_ev = st[("pe", bank)]
            pb = pbank_bf16(env, 6)
            tw = [(vv, v_ev), (vv, st.get("vt_done", 0)), (vv, v_ms),
                  (vv, st.get("cfree_all", 0))]
            for b4 in range(4):
                v_tr = p.emit("tensor", "transpose",
                              pb[:, 128 * b4:128 * (b4 + 1)],
                              otst[:, 128 * b4:128 * (b4 + 1)], t["ident"][:],
                              waits=(tw if b4 == 0 else ()),
                              inc=(tt if b4 == 3 else None))
            kc0 = tc * 4
            srcA = pb[:, 0:512].rearrange("p (a c) -> p a c", c=128)[:, :, 0:64]
            srcB = pb[:, 0:512].rearrange("p (a c) -> p a c", c=128)[:, :, 64:128]
            dstA = vbuf[:, 65 * kc0:65 * (kc0 + 4)].rearrange(
                "p (a c) -> p a c", c=65)[:, :, 0:64]
            dstB = vbuf[:, 2080 + 65 * kc0:2080 + 65 * (kc0 + 4)].rearrange(
                "p (a c) -> p a c", c=65)[:, :, 0:64]
            p.emit("vector", "tensor_copy", dstA, srcA, waits=[(tt, v_tr)])
            st["vt_done"] = p.emit("vector", "tensor_copy", dstB, srcB, inc=vv)
        v_proj_done = env.sem("vv").n
        v_proj_tt = env.sem("tt").n

        for hh in range(2):
            for side in range(2):
                p.dma("gpsimd",
                      etile[:, 512 * (2 * hh + side):512 * (2 * hh + side + 1)],
                      env.dram["l0a_eb"][2 * pr + hh, side],
                      waits=[(tt, v_proj_tt), (vv, st.get("cfree_all", 0))],
                      inc=et)
        for o in range(4):
            p.dma("gpsimd", ttA[:, 512 * o:512 * (o + 1)],
                  env.dram["l0a_T"][2 * pr, o],
                  waits=[(tt, v_proj_tt),
                         (vv, st.get("str_add_last", 0))], inc=et)
            p.dma("gpsimd", t["xf"][:, 256 * o:256 * (o + 1)].bitcast(BF16),
                  env.dram["l0a_T"][2 * pr + 1, o],
                  waits=[(vv, st.get("str_add_last", pro_vv))], inc=et)
        v_et_all = et.n

        if getattr(env, "L0A_PROJ_ONLY", False):
            continue
        for qc in range(NQ):
            q0 = 512 * qc
            regions = _l0a_regions(qc)
            comp = {0: [], 1: []}
            for hh in range(2):
                h = 2 * pr + hh
                for kc, (reg, m) in enumerate(regions):
                    if reg in ("low", "high") and m > M_skip[h]:
                        continue
                    comp[hh].append((kc, reg, m))
            bmap = {(0, "low"): 2, (0, "high"): 3, (0, "str"): 4,
                    (1, "low"): 5, (1, "high"): 6, (1, "str"): 7}
            for hh in range(2):
                h = 2 * pr + hh
                base_k = 64 * hh
                pbuf = 16384 * hh
                exp_done = {}
                seen = {}
                for (kc, reg, m) in comp[hh]:
                    seen.setdefault(reg, []).append(kc)
                first_in_reg = {reg: kcs[0] for reg, kcs in seen.items()}
                last_in_reg = {reg: kcs[-1] for reg, kcs in seen.items()}
                ncomp = len(comp[hh])

                def emit_av(j2):
                    kc2, reg2, _m2 = comp[hh][j2]
                    bank2 = bmap[(hh, reg2)]
                    wts2 = [(ss, exp_done[j2])]
                    if kc2 == first_in_reg[reg2]:
                        wts2.append((vv, st.get(("cfree", hh), 0)))
                        if reg2 == "str":
                            wts2.append((vv, st.get(("nmdone", hh), 0)))
                    return p.emit("tensor", "matmul",
                                  pbank(env, bank2)[0:65, :],
                                  vbuf[:, 2080 * hh + 65 * kc2:
                                       2080 * hh + 65 * (kc2 + 1)],
                                  probs[:, pbuf + 512 * kc2:
                                        pbuf + 512 * (kc2 + 1)],
                                  start=(kc2 == first_in_reg[reg2]),
                                  stop=(kc2 == last_in_reg[reg2]),
                                  skip_group_check=True,
                                  waits=wts2,
                                  inc=(tt if kc2 == last_in_reg[reg2]
                                       else None))

                for j, (kc, reg, m) in enumerate(comp[hh]):
                    sbank = j % 2
                    wts = []
                    if j < 2:
                        wts = [(vv, v_proj_done), (tt, st.get("avdone", 0)),
                               (ss, st.get("exp_last", 0))]
                    else:
                        wts = [(ss, exp_done[j - 2])]
                    v_s = p.emit("tensor", "matmul", pbank(env, sbank),
                                 kT[base_k:base_k + 64, 128 * kc:128 * (kc + 1)],
                                 qT[base_k:base_k + 64, q0:q0 + 512],
                                 start=True, stop=True,
                                 tile_position=(base_k, 0),
                                 skip_group_check=True,
                                 waits=wts, inc=tt)
                    ew = [(tt, v_s)]
                    esrc = pbank(env, sbank)
                    if reg == "str":
                        tsl = (ttA[:, 512 * m:512 * (m + 1)] if hh == 0 else
                               t["xf"][:, 256 * m:256 * (m + 1)].bitcast(BF16))
                        stg = t["yf"][:, 512:1024]
                        v_s2 = p.emit("vector", "tensor_add", stg,
                                      pbank(env, sbank), tsl,
                                      waits=[(tt, v_s), (et, v_et_all),
                                             (ss, env.sem("ss").n)],
                                      inc=vv)
                        st["str_add_last"] = v_s2
                        ew = [(vv, v_s2)]
                        esrc = stg
                        bcol = acols[:, 57 * h:57 * h + 1]
                    elif reg == "low":
                        bcol = acols[:, 57 * h + m:57 * h + m + 1]
                    else:
                        bcol = acols[:, 57 * h + 28 + m:57 * h + 28 + m + 1]
                    exp_done[j] = p.emit(
                        "scalar", "activation",
                        probs[:, pbuf + 512 * kc:pbuf + 512 * (kc + 1)],
                        esrc, AF.Exp, bias=bcol, scale=0.125,
                        waits=ew, inc=ss)
                    if hh == 0 and j == min(1, ncomp - 1):
                        flush_pend()
                    if j >= 1:
                        emit_av(j - 1)
                emit_av(ncomp - 1)
                st["avdone"] = env.sem("tt").n
                st["exp_last"] = exp_done[ncomp - 1]

                creg = comb[:, 512 * hh:512 * (hh + 1)]
                have = set(r for (_, r, _) in comp[hh])
                el = etile[0:65, 512 * 2 * hh:512 * (2 * hh + 1)]
                eh = etile[0:65, 512 * (2 * hh + 1):512 * (2 * hh + 2)]
                scr = t["yf"][:, 0:512]
                v_c = p.emit("vector", "tensor_copy", creg[0:65, :],
                             pbank(env, bmap[(hh, "str")])[0:65, :],
                             waits=[(tt, st["avdone"]),
                                    (vv, st.get(("nmdone", hh), 0))], inc=vv)
                if "low" in have:
                    p.emit("vector", "tensor_mul", scr[0:65, :],
                           pbank(env, bmap[(hh, "low")])[0:65, :], el,
                           waits=[(et, v_et_all)])
                    v_c = p.emit("vector", "tensor_add", creg[0:65, :],
                                 creg[0:65, :], scr[0:65, :], inc=vv)
                if "high" in have:
                    p.emit("vector", "tensor_mul", scr[0:65, :],
                           pbank(env, bmap[(hh, "high")])[0:65, :], eh,
                           waits=[(et, v_et_all)])
                    v_c = p.emit("vector", "tensor_add", creg[0:65, :],
                                 creg[0:65, :], scr[0:65, :], inc=vv)
                st[("cfree", hh)] = v_c
                st["cfree_all"] = v_c
                v_rc = p.emit("vector", "reciprocal",
                              creg[64:65, :], creg[64:65, :],
                              waits=[(vv, v_c)], inc=vv)

                def c_tail(v_rc=v_rc, hh=hh, pr_=pr, q0_=q0, creg=creg,
                           sb_=bmap[(hh, "str")]):
                    v_bm = p.emit("tensor", "matmul", pbank(env, sb_),
                                  fscr[64:65, 512:640],
                                  creg[64:65, :],
                                  start=True, stop=True,
                                  skip_group_check=True,
                                  waits=[(vv, v_rc), (vv, v_ones)], inc=tt)
                    st[("bm", hh)] = v_bm
                    v_nm = p.emit("vector", "tensor_mul",
                                  otst[:, 512 * hh:512 * (hh + 1)][0:64, :],
                                  creg[0:64, :], pbank(env, sb_)[0:64, :],
                                  waits=[(tt, v_bm),
                                         (oos[hh],
                                          st.get(("odma", hh), 0))],
                                  inc=vv)
                    st[("nmdone", hh)] = v_nm
                    st[("odma", hh)] = p.dma(
                        "sync",
                        attn_oT[128 * pr_ + 64 * hh:128 * pr_ + 64 * (hh + 1),
                                q0_:q0_ + 512],
                        otst[:, 512 * hh:512 * (hh + 1)][0:64, :],
                        waits=[(vv, v_nm)], inc=oos[hh])

                pend.append(c_tail)
        st[("wfree", 0)] = env.sem("tt").n
        st[("wfree", 1)] = env.sem("tt").n
        st[("wfree", 2)] = env.sem("tt").n

    flush_pend()
    v_attn_all = [env.sem("oo").n, env.sem("oob").n]

    ost = t["arena1"]
    wo_state = {"oc": -1, "v": 0}
    ld_state = {"tc": -1, "v": 0}

    def mk_out(oc, tc, bank, war):
        if ld_state["tc"] != tc:
            ld_state["v"] = p.dma(
                "sync", ost[:, 0:4096].rearrange("p (a c) -> p a c", a=8),
                attn_oT.rearrange("(a p) s -> p a s", p=128)[:, :,
                                                            512 * tc:512 * (tc + 1)],
                waits=[(oos[0], v_attn_all[0]), (oos[1], v_attn_all[1]),
                       (tt, tt.n)], inc=lo)
            ld_state["tc"] = tc
        if wo_state["oc"] != oc:
            wo_state["v"] = p.dma(
                "gpsimd", t["wstage"][:, 0:1024], env.dram["l0a_woT"][oc],
                waits=[(tt, tt.n)], inc=wts_[0])
            wo_state["oc"] = oc
        for hc in range(NF):
            wts = []
            if hc == 0:
                wts = list(war) + [(wts_[0], wo_state["v"]), (lo, ld_state["v"])]
            r = p.emit("tensor", "matmul", pbank(env, bank),
                       t["wstage"][:, 128 * hc:128 * (hc + 1)],
                       ost[:, 512 * hc:512 * (hc + 1)],
                       start=(hc == 0), stop=(hc == NF - 1),
                       waits=wts, inc=(tt if hc == NF - 1 else None))
        return tt, r

    if getattr(env, "L0A_SKIP_OUT", False):
        return
    p.wait("scalar", (ic2, v_bo))
    p.wait("vector", (ic2, v_bo))
    emit_out_phase(env, x_ap, y_ap, mk_out, range(NQ),
                   bias_col=t["cols"][:, 240:248], scale=1.0)


def emit_l1a(env, x_ap, y_ap):
    p, t = env.plan, env.t
    tt, vv, ss = env.sem("tt"), env.sem("vv"), env.sem("ss")
    oos = [env.sem("oo"), env.sem("oob")]
    lo, ic2, et = env.sem("lo"), env.sem("ic2"), env.sem("et")
    wts_ = [env.sem("wt0"), env.sem("wt1"), env.sem("wt2"), env.sem("wt3")]

    emit_prologue(env, x_ap)
    pro_vv = env.sem("vv").n
    xnT = t["xnT"][:].rearrange("p (f s) -> p f s", f=NF)
    attn_oT = env.dram["attn_oT"]

    a2 = t["arena2"]
    qT = a2[:, 0:4096]
    kT = a2[:, 4096:8192]
    vbuf = a2[:, 8192:12352]
    otst = a2[:, 12352:13376]
    probsK = a2[:, 13376:15424]
    pk1 = [t["xf"][:, 2560 + 256 * i:2816 + 256 * i].bitcast(BF16)
           for i in range(4)]
    ptmp4 = {(0, 0): a2[:, 15424:15936],
             (0, 1): t["xf"][:, 1280:1536].bitcast(BF16),
             (1, 0): t["xf"][:, 1024:1280].bitcast(BF16),
             (1, 1): t["xf"][:, 3584:3840].bitcast(BF16)}
    creg_all = t["xf"][:, 1536:2560]
    probs = t["arena1"]
    fscr = t["fscr"]
    cmask = t["wstage"][:, 6144:8192]

    p.dma("sync", t["cols"][:, 240:248], env.dram["l1a_bo"], inc=ic2)
    p.dma("sync", t["cols"][:, 216:224], env.dram["l1a_bq"], inc=ic2)
    p.dma("sync", t["cols"][:, 224:232], env.dram["l1a_bk"], inc=ic2)
    v_b = p.dma("sync", t["cols"][:, 232:240], env.dram["l1a_bv"], inc=ic2)
    p.wait("scalar", (ic2, v_b))
    p.wait("vector", (ic2, v_b))
    acols = t["acols"]
    ones2 = [acols[0:1, 0:128], acols[64:65, 0:128]]
    onesb = a2[:, 16383:16384]
    p.emit("vector", "memset", ones2[0], 1.0)
    p.emit("vector", "memset", ones2[1], 1.0)
    v_ones = p.emit("vector", "memset", onesb, 1.0, inc=vv)
    st = {}
    pend = []

    def flush_pend():
        while pend:
            pend.pop(0)()

    def stream_w(which, pr, slot):
        src = env.dram[which]
        v = p.dma("gpsimd", wslot(env, slot)[:, 0:1024], src[pr],
                  waits=[(tt, st.get(("wfree", slot), 0))], inc=wts_[slot])
        return (wts_[slot], v)

    for pr in range(getattr(env, "L1A_PAIRS", 8)):
        flush_pend()
        v_wq = stream_w("l1a_wqT", pr, 0)
        v_wk = stream_w("l1a_wkT", pr, 1)
        v_wv = stream_w("l1a_wvT", pr, 2)
        if pr == 0:
            for o in range(4):
                p.dma("gpsimd", cmask[:, 512 * o:512 * (o + 1)],
                      env.dram["l1a_cmaskT"][o], waits=[(tt, tt.n)], inc=et)
            v_cm = et.n

        for which, wv, slot, dst, bcol0, bank in (
                ("q", v_wq, 0, qT, 216, 0), ("k", v_wk, 1, kT, 224, 1)):
            for tc in range(NQ):
                for ic in range(NF):
                    wts = []
                    if ic == 0:
                        wts = [wv, (vv, st.get(("pe", bank), 0)),
                               (ss, st.get(("sb", bank), 0))]
                        if tc == 0 and which == "q":
                            wts.append((vv, st.get("norm_done", pro_vv)))
                    v_mm = p.emit("tensor", "matmul", pbank(env, bank),
                                  wslot(env, slot)[:, 128 * ic:128 * (ic + 1)],
                                  xnT[:, ic, 512 * tc:512 * (tc + 1)],
                                  start=(ic == 0), stop=(ic == NF - 1),
                                  waits=wts, inc=(tt if ic == NF - 1 else None))
                st[("pe", bank)] = p.emit(
                    "vector", "tensor_scalar",
                    out=dst[:, 512 * tc:512 * (tc + 1)], in0=pbank(env, bank),
                    scalar1=1.0, scalar2=t["cols"][:, bcol0 + pr:bcol0 + pr + 1],
                    op0=ALU.mult, op1=ALU.add,
                    waits=[(tt, v_mm)], inc=vv)
        v_ms = p.emit("vector", "memset",
                      vbuf[:].rearrange("p (a c) -> p a c", c=65)[:, :, 64:65],
                      1.0, inc=vv)
        for tc in range(NQ):
            bank = tc % 2
            for ic in range(NF):
                wts = []
                if ic == 0:
                    wts = [v_wv, (vv, st.get(("pe", bank), 0)),
                           (ss, st.get(("sb", bank), 0))]
                v_mm = p.emit("tensor", "matmul", pbank(env, bank),
                              wslot(env, 2)[:, 128 * ic:128 * (ic + 1)],
                              xnT[:, ic, 512 * tc:512 * (tc + 1)],
                              start=(ic == 0), stop=(ic == NF - 1),
                              waits=wts, inc=(tt if ic == NF - 1 else None))
            st[("pe", bank)] = p.emit(
                "vector", "tensor_scalar", out=otst[:, 0:512],
                in0=pbank(env, bank), scalar1=1.0,
                scalar2=t["cols"][:, 232 + pr:233 + pr],
                op0=ALU.mult, op1=ALU.add,
                waits=[(tt, v_mm), (vv, st.get("vt_done", 0)),
                       (oos[0], st.get(("odma", 0), 0)),
                       (oos[1], st.get(("odma", 1), 0))], inc=vv)
            v_ev = st[("pe", bank)]
            pb = pbank_bf16(env, 6)
            tw = [(vv, v_ev), (vv, st.get("vt_done", 0)), (vv, v_ms),
                  (vv, st.get("b6free", 0)), (vv, st.get(("tpe", 0), 0)),
                  (vv, st.get(("cnorm", 0), 0))]
            for b4 in range(4):
                v_tr = p.emit("tensor", "transpose",
                              pb[:, 128 * b4:128 * (b4 + 1)],
                              otst[:, 128 * b4:128 * (b4 + 1)], t["ident"][:],
                              waits=(tw if b4 == 0 else ()),
                              inc=(tt if b4 == 3 else None))
            kc0 = tc * 4
            srcA = pb[:, 0:512].rearrange("p (a c) -> p a c", c=128)[:, :, 0:64]
            srcB = pb[:, 0:512].rearrange("p (a c) -> p a c", c=128)[:, :, 64:128]
            dstA = vbuf[:, 65 * kc0:65 * (kc0 + 4)].rearrange(
                "p (a c) -> p a c", c=65)[:, :, 0:64]
            dstB = vbuf[:, 2080 + 65 * kc0:2080 + 65 * (kc0 + 4)].rearrange(
                "p (a c) -> p a c", c=65)[:, :, 0:64]
            p.emit("vector", "tensor_copy", dstA, srcA, waits=[(tt, v_tr)])
            st["vt_done"] = p.emit("vector", "tensor_copy", dstB, srcB, inc=vv)
        v_proj_done = env.sem("vv").n
        if getattr(env, "L1A_DBG", None) == pr:
            dbg = env.nc.dram_tensor("dbg_vbuf", [128, 4160], BF16,
                                     kind="ExternalOutput").ap()
            p.dma("sync", dbg, vbuf[:], waits=[(vv, v_proj_done)],
                  inc=env.sem("oo"))

        suf = {0: acols[0:65, 128:136], 1: acols[0:65, 136:144]}
        bstage = acols[0:65, 144:152]
        for hh in range(2):
            for kb in range(8):
                for i4 in range(4):
                    kc = 4 * kb + i4
                    wts = []
                    if kb == 0 and i4 == 0:
                        wts = [(vv, v_proj_done), (vv, st.get("b6free", 0)),
                               (vv, st.get("suf_done", 0)), (vv, v_ones),
                               (vv, st.get(("tpe", 0), 0)),
                               (vv, st.get(("cnorm", 0), 0))]
                    v_bs = p.emit("tensor", "matmul",
                                  pbank(env, 6)[0:65, kb:kb + 1],
                                  vbuf[:, 2080 * hh + 65 * kc:
                                       2080 * hh + 65 * (kc + 1)],
                                  onesb,
                                  start=(i4 == 0), stop=(i4 == 3),
                                  waits=wts,
                                  inc=(tt if i4 == 3 else None),
                                  skip_group_check=True)
            v_bse = p.emit("vector", "tensor_copy", bstage,
                           pbank(env, 6)[0:65, 0:8], waits=[(tt, v_bs)], inc=vv)
            st["b6free"] = v_bse
            p.emit("vector", "memset", suf[hh][:, 7:8], 0.0)
            for kb in range(6, -1, -1):
                p.emit("vector", "tensor_add", suf[hh][:, kb:kb + 1],
                       suf[hh][:, kb + 1:kb + 2],
                       bstage[:, kb + 1:kb + 2])
            st["suf_done"] = p.emit("vector", "engine_nop", inc=vv)
            if getattr(env, "L1A_DBG", None) == pr and hh == 1:
                dbg2 = env.nc.dram_tensor("dbg_suf", [128, 32], F32,
                                          kind="ExternalOutput").ap()
                p.dma("sync", dbg2, acols[:, 128:160],
                      waits=[(vv, st["suf_done"])], inc=env.sem("oo"))

        for qc in range(getattr(env, "L1A_MAXQC", NQ)):
            q0 = 512 * qc
            nkb = qc + 1

            def chunk_ap(hh, qs, kb):
                off = 16384 * hh + 4096 * qs + 512 * kb
                return probs[:, off:off + 512]

            def emit_A(hh, kb, qc=qc):
                base_k = 64 * hh
                for qs in range(4):
                    c2 = 32 * hh + 4 * kb + qs
                    b = 2 * hh + qs % 2
                    wts = [(ss, st.get(("sb", b), 0))]
                    if kb == 0 and qs == 0:
                        wts.append((vv, v_proj_done))
                    v_s = p.emit("tensor", "matmul", pbank(env, b),
                                 qT[base_k:base_k + 64,
                                    q0 + 128 * qs:q0 + 128 * (qs + 1)],
                                 kT[base_k:base_k + 64,
                                    512 * kb:512 * (kb + 1)],
                                 start=True, stop=True,
                                 tile_position=(base_k, 0),
                                 skip_group_check=True,
                                 waits=wts, inc=tt)
                    v_e1 = p.emit("scalar", "activation",
                                  chunk_ap(hh, qs, kb), pbank(env, b),
                                  AF.Exp, scale=0.125,
                                  waits=[(tt, v_s)], inc=ss)
                    st[("sb", b)] = v_e1
                    if kb == qc:
                        p.emit("vector", "tensor_mul",
                               chunk_ap(hh, qs, kb), chunk_ap(hh, qs, kb),
                               cmask[:, 512 * qs:512 * (qs + 1)],
                               waits=[(ss, v_e1), (et, v_cm)], inc=vv)
                        p.emit("vector", "bn_stats",
                               acols[:, 160 + 6 * c2:166 + 6 * c2],
                               chunk_ap(hh, qs, kb), inc=vv)
                    else:
                        p.emit("vector", "bn_stats",
                               acols[:, 160 + 6 * c2:166 + 6 * c2],
                               chunk_ap(hh, qs, kb),
                               waits=[(ss, v_e1)], inc=vv)

            def emit_B_both(kb, qc=qc, nkb=nkb):
                v_rcs = {}
                for hh in range(2):
                    base = 32 * hh + 4 * kb
                    for qs in range(4):
                        c2 = base + qs
                        p.emit("vector", "bn_aggr",
                               acols[:, 544 + 2 * c2:546 + 2 * c2],
                               acols[:, 160 + 6 * c2:166 + 6 * c2])
                    rr = acols[:, 672 + base:676 + base]
                    mv = acols[:, 544 + 2 * base:552 + 2 * base].rearrange(
                        "p (a c) -> p a c", c=2)[:, :, 0:1]
                    wmax = max(st.get(("rcread", base + q), 0)
                               for q in range(4))
                    p.emit("vector", "tensor_scalar_mul", rr, mv, 512.0,
                           waits=[(ss, wmax)])
                    v_rc4 = p.emit("vector", "reciprocal", rr, rr, inc=vv)
                    for qs in range(4):
                        v_rcs[(hh, qs)] = v_rc4
                ev_last = {}
                for qs in range(4):
                    for hh in range(2):
                        c2 = 32 * hh + 4 * kb + qs
                        pk = (hh, qs % 2)
                        rcol = acols[:, 672 + c2:673 + c2]
                        v_e2 = p.emit("scalar", "activation", ptmp4[pk],
                                      chunk_ap(hh, qs, kb), AF.Exp,
                                      scale=rcol,
                                      waits=[(vv, v_rcs[(hh, qs)]),
                                             (tt, st.get(("tpf", pk), 0))],
                                      inc=ss)
                        st[("rcread", c2)] = v_e2
                        pb = pbank_bf16(env, 6 + hh)
                        for i4 in range(4):
                            v_tr = p.emit(
                                "tensor", "transpose",
                                pb[:, 128 * i4:128 * (i4 + 1)],
                                ptmp4[pk][:, 128 * i4:128 * (i4 + 1)],
                                t["ident"][:],
                                waits=([(ss, v_e2),
                                        (vv, st.get(("tpe", hh), 0)),
                                        (vv, st.get(("cnorm", hh), 0))]
                                       if i4 == 0 else ()),
                                inc=(tt if i4 == 3 else None))
                        st[("tpf", pk)] = v_tr
                        ew = [(tt, v_tr), (tt, st.get(("avl", hh), 0))]
                        if hh == 0:
                            v_ev = p.emit(
                                "vector", "tensor_copy",
                                probsK.rearrange("p (a c) -> p a c",
                                                 c=512)[:, :, 128 * qs:
                                                        128 * (qs + 1)],
                                pb[:, 0:512].rearrange("p (a c) -> p a c",
                                                       c=128),
                                waits=ew, inc=vv)
                        else:
                            for i4 in range(4):
                                v_ev = p.emit(
                                    "vector", "tensor_copy",
                                    pk1[i4][:, 128 * qs:128 * (qs + 1)],
                                    pb[:, 128 * i4:128 * (i4 + 1)],
                                    waits=(ew if i4 == 0 else ()),
                                    inc=(vv if i4 == 3 else None))
                        st[("tpe", hh)] = v_ev
                        ev_last[hh] = v_ev
                for hh in range(2):
                    for i4 in range(4):
                        kc = 4 * kb + i4
                        wts = [(vv, ev_last[hh])] if i4 == 0 else []
                        if not st[("avst", hh)]:
                            wts.append((vv, st.get(("accfree", hh), 0)))
                        rhs = (probsK[:, 512 * i4:512 * (i4 + 1)] if hh == 0
                               else pk1[i4][:])
                        p.emit("tensor", "matmul",
                               pbank(env, 4 + hh)[0:65, :],
                               vbuf[:, 2080 * hh + 65 * kc:
                                    2080 * hh + 65 * (kc + 1)],
                               rhs,
                               start=(not st[("avst", hh)]),
                               stop=(kb == nkb - 1 and i4 == 3),
                               skip_group_check=True, waits=wts,
                               inc=(tt if (kb == nkb - 1 and i4 == 3)
                                    else None))
                        st[("avst", hh)] = True
                    st[("avl", hh)] = env.sem("tt").n

            st[("avst", 0)] = False
            st[("avst", 1)] = False
            for kb in range(nkb):
                for hh in range(2):
                    emit_A(hh, kb)
                if kb == 0:
                    flush_pend()
                if kb > 0:
                    emit_B_both(kb - 1)
            emit_B_both(nkb - 1)

            for hh in range(2):
                creg = creg_all[:, 512 * hh:512 * (hh + 1)]
                v_c = p.emit("vector", "tensor_scalar", out=creg[0:65, :],
                             in0=pbank(env, 4 + hh)[0:65, :],
                             scalar1=suf[hh][:, qc:qc + 1], scalar2=None,
                             op0=ALU.add,
                             waits=[(tt, st[("avl", hh)]),
                                    (vv, st.get(("cregfree", hh), 0))],
                             inc=vv)
                st[("accfree", hh)] = v_c
                v_rc2 = p.emit("vector", "reciprocal",
                               creg[64:65, :], creg[64:65, :], inc=vv)

                def c_tail(v_rc2=v_rc2, hh=hh, pr_=pr, q0_=q0, creg=creg):
                    v_bm = p.emit("tensor", "matmul", pbank(env, 6 + hh),
                                  ones2[1], creg[64:65, :],
                                  start=True, stop=True,
                                  skip_group_check=True,
                                  waits=[(vv, v_rc2),
                                         (vv, st.get(("tpe", hh), 0))],
                                  inc=tt)
                    st[("bm", hh)] = v_bm
                    v_nm = p.emit("vector", "tensor_mul",
                                  otst[:, 512 * hh:512 * (hh + 1)][0:64, :],
                                  creg[0:64, :],
                                  pbank(env, 6 + hh)[0:64, :],
                                  waits=[(tt, v_bm),
                                         (oos[hh],
                                          st.get(("odma", hh), 0))],
                                  inc=vv)
                    st[("cnorm", hh)] = v_nm
                    st[("cregfree", hh)] = v_nm
                    st[("odma", hh)] = p.dma(
                        "sync",
                        attn_oT[128 * pr_ + 64 * hh:
                                128 * pr_ + 64 * (hh + 1), q0_:q0_ + 512],
                        otst[:, 512 * hh:512 * (hh + 1)][0:64, :],
                        waits=[(vv, v_nm)], inc=oos[hh])

                pend.append(c_tail)

        st[("wfree", 0)] = env.sem("tt").n
        st[("wfree", 1)] = env.sem("tt").n
        st[("wfree", 2)] = env.sem("tt").n

    flush_pend()
    v_attn_all = [env.sem("oo").n, env.sem("oob").n]

    ost = t["arena1"]
    wo_state = {"oc": -1, "v": 0}
    ld_state = {"tc": -1, "v": 0}

    def mk_out(oc, tc, bank, war):
        if ld_state["tc"] != tc:
            ld_state["v"] = p.dma(
                "sync", ost[:, 0:4096].rearrange("p (a c) -> p a c", a=8),
                attn_oT.rearrange("(a p) s -> p a s", p=128)[:, :,
                                                            512 * tc:512 * (tc + 1)],
                waits=[(oos[0], v_attn_all[0]), (oos[1], v_attn_all[1]),
                       (tt, tt.n)], inc=lo)
            ld_state["tc"] = tc
        if wo_state["oc"] != oc:
            wo_state["v"] = p.dma(
                "gpsimd", t["wstage"][:, 0:1024], env.dram["l1a_woT"][oc],
                waits=[(tt, tt.n)], inc=wts_[0])
            wo_state["oc"] = oc
        for hc in range(NF):
            wts = []
            if hc == 0:
                wts = list(war) + [(wts_[0], wo_state["v"]), (lo, ld_state["v"])]
            r = p.emit("tensor", "matmul", pbank(env, bank),
                       t["wstage"][:, 128 * hc:128 * (hc + 1)],
                       ost[:, 512 * hc:512 * (hc + 1)],
                       start=(hc == 0), stop=(hc == NF - 1),
                       waits=wts, inc=(tt if hc == NF - 1 else None))
        return tt, r

    emit_out_phase(env, x_ap, y_ap, mk_out, range(NQ),
                   bias_col=t["cols"][:, 240:248], scale=1.0)


def emit_l1b(env, x_ap, y_ap):
    p, t = env.plan, env.t
    tt, vv, ss = env.sem("tt"), env.sem("vv"), env.sem("ss")
    lo, ic2, et = env.sem("lo"), env.sem("ic2"), env.sem("et")
    wts_ = [env.sem("wt0"), env.sem("wt1"), env.sem("wt2"), env.sem("wt3")]

    emit_prologue(env, x_ap)
    pro_vv = env.sem("vv").n
    xnT = t["xnT"][:].rearrange("p (f s) -> p f s", f=NF)
    acols = t["acols"]
    fscr = t["fscr"]
    probs = t["arena1"]
    vstg = t["arena1"][:, 16384:32768]
    a2 = t["arena2"]
    readsq = a2[:, 0:4096]
    vbuild = a2[:, 4096:8192]
    zsh0 = a2[:, 14400:15424]
    comb = a2[:, 13376:15424].bitcast(F32)
    Etile = t["wstage"][:, 6144:7168].bitcast(F32)
    Ttile = [t["wstage"][:, 2048 + 1024 * o:3072 + 1024 * o].bitcast(F32)
             for o in range(4)]
    mstage = t["wstage"][:, 7168:7680]
    vtok_d = env.dram["vtok_d"]

    p.dma("sync", fscr[0:1, 0:1], env.dram["l1b_decay"],
          allow_slow_non_contiguous=True, inc=ic2)
    p.dma("sync", fscr[0:1, 1:2], env.dram["l1b_la"],
          allow_slow_non_contiguous=True, inc=ic2)
    p.dma("sync", acols[:, 0:512], env.dram["l1b_iotaE"], inc=ic2)
    v_c0 = p.dma("sync", acols[:, 512:540], env.dram["l1b_colm"], inc=ic2)
    p.wait("scalar", (ic2, v_c0))
    p.wait("vector", (ic2, v_c0))
    v_ones = p.emit("vector", "memset", acols[0:1, 768:896], 1.0, inc=vv)
    v_g = p.emit("scalar", "activation", fscr[0:1, 2:3], fscr[0:1, 0:1],
                 AF.Sigmoid, scale=1.0, inc=ss)
    v_lg = p.emit("scalar", "activation", fscr[0:1, 3:4], fscr[0:1, 2:3],
                  AF.Ln, scale=1.0, waits=[(ss, v_g)], inc=ss)
    v_bm = p.emit("tensor", "matmul", pbank(env, 7)[:, 0:1],
                  acols[0:1, 768:896], fscr[0:1, 3:4],
                  start=True, stop=True, waits=[(vv, v_ones), (ss, v_lg)],
                  inc=tt)
    v_bm2 = p.emit("tensor", "matmul", pbank(env, 7)[:, 1:2],
                   acols[0:1, 768:896], fscr[0:1, 1:2],
                   start=True, stop=True, inc=tt)
    lgc = fscr[:, 8:9]
    lac = fscr[:, 9:10]
    v_ev = p.emit("vector", "tensor_copy", fscr[:, 8:10], pbank(env, 7)[:, 0:2],
                  waits=[(tt, v_bm2)], inc=vv)
    v_E = p.emit("scalar", "activation", Etile[:], acols[:, 0:512], AF.Exp,
                 bias=lac, scale=lgc, waits=[(vv, v_ev)], inc=ss)
    v_cf = p.emit("scalar", "activation", fscr[:, 16:44], acols[:, 512:540],
                  AF.Exp, scale=lgc, inc=ss)
    v_T = {}
    prev_tm = 0
    for o in range(4):
        v_i = p.dma("sync", acols[:, 0:512], env.dram["l1b_iotaT"][o],
                    waits=[(ss, v_T[o - 1] if o else v_cf)], inc=ic2)
        v_m = p.dma("gpsimd", mstage, env.dram["l1b_tmask"][o],
                    waits=([(vv, prev_tm)] if o else ()), inc=et)
        v_e1 = p.emit("scalar", "activation", Ttile[o][:], acols[:, 0:512],
                      AF.Exp, bias=lac, scale=lgc,
                      waits=[(ic2, v_i)], inc=ss)
        v_T[o] = v_e1
        prev_tm = p.emit("vector", "tensor_mul", Ttile[o][:], Ttile[o][:],
                         mstage, waits=[(ss, v_e1), (et, v_m)], inc=vv)
    v_setup_v = env.sem("vv").n

    for icx in range(NF):
        p.emit("vector", "memset", zsh0[:, 128 * icx:128 * icx + 1], 0.0)
        p.emit("vector", "tensor_copy",
               zsh0[:, 128 * icx + 1:128 * (icx + 1)],
               xnT[:, icx, 0:127], waits=[(vv, pro_vv)])
    v_zsh = p.emit("vector", "engine_nop", inc=vv)

    st = {}

    for dc in range(NF):
        v_w = p.dma("gpsimd", wslot(env, 0)[:, 0:1024],
                    env.dram["l1b_wwT"][dc],
                    waits=[(tt, st.get("w0free", 0))], inc=wts_[0])
        for tc in range(NQ):
            bank = tc % 2
            for icx in range(NF):
                wts = []
                if icx == 0:
                    wts = [(wts_[0], v_w), (vv, pro_vv),
                           (ss, st.get(("ve", bank), 0))]
                v_mm = p.emit("tensor", "matmul", pbank(env, bank),
                              wslot(env, 0)[:, 128 * icx:128 * (icx + 1)],
                              xnT[:, icx, 512 * tc:512 * (tc + 1)],
                              start=(icx == 0), stop=(icx == NF - 1),
                              waits=wts, inc=(tt if icx == NF - 1 else None))
            st[("ve", bank)] = p.emit(
                "scalar", "activation",
                a2[:, 9216 + 512 * tc:9216 + 512 * (tc + 1)],
                pbank(env, bank), AF.Copy, scale=1.0,
                waits=[(tt, v_mm), (tt, st.get("vtr", 0)),
                       (env.sem("oo"), st.get("vout", 0))], inc=ss)
        v_vs = env.sem("ss").n
        for tc4 in range(NQ):
            pb = pbank_bf16(env, 6)
            tw = [(ss, v_vs), (vv, st.get("vtev", 0))]
            for b4 in range(4):
                v_tr = p.emit("tensor", "transpose",
                              pb[:, 128 * b4:128 * (b4 + 1)],
                              a2[:, 9216 + 512 * tc4 + 128 * b4:
                                 9216 + 512 * tc4 + 128 * (b4 + 1)],
                              t["ident"][:],
                              waits=(tw if b4 == 0 else ()),
                              inc=(tt if b4 == 3 else None))
            st["vtr"] = v_tr
            kc0 = 4 * tc4
            src4 = pb[:, 0:512].rearrange("p (a c) -> p a c", c=128)
            dst4 = vbuild[:].rearrange("p (k c) -> p k c", k=32)[:, kc0:kc0 + 4, :]
            st["vtev"] = p.emit("vector", "tensor_copy", dst4, src4,
                                waits=[(tt, v_tr)], inc=vv)
        st["w0free"] = env.sem("tt").n
        st["vout"] = p.dma(
            "sync",
            vtok_d.rearrange("(kc pp) d -> pp kc d", pp=128)[:, :,
                                                            128 * dc:128 * (dc + 1)],
            vbuild[:].rearrange("p (k c) -> p k c", k=32),
            waits=[(vv, st["vtev"])], inc=env.sem("oo"))
    v_vtok_oo = env.sem("oo").n

    wr_state = {"oc": -1, "v": 0}
    for qc in range(getattr(env, "L1B_MAXQC", NQ)):
        q0 = 512 * qc
        nkc = 4 * qc + 4
        for kc in range(nkc):
            sbank = kc % 2
            for icx in range(NF):
                wts = []
                if icx == 0:
                    pf = st.get(("pfree", sbank))
                    wts = ([pf] if pf else []) + [(vv, v_zsh)]
                lhs = (zsh0[:, 128 * icx:128 * (icx + 1)] if kc == 0 else
                       xnT[:, icx, 128 * kc - 1:128 * kc + 127])
                v_s = p.emit("tensor", "matmul", pbank(env, sbank), lhs,
                             xnT[:, icx, q0:q0 + 512],
                             start=(icx == 0), stop=(icx == NF - 1),
                             waits=wts, inc=(tt if icx == NF - 1 else None))
            o = kc - 4 * qc
            pw = [(tt, v_s), (tt, st.get(("pused", kc), 0))]
            if o >= 0:
                v_pe = p.emit("vector", "tensor_mul",
                              probs[:, 512 * kc:512 * (kc + 1)],
                              pbank(env, sbank), Ttile[o][:],
                              waits=pw + [(vv, v_setup_v)], inc=vv)
            else:
                m = -o
                v_pe = p.emit("vector", "tensor_scalar",
                              out=probs[:, 512 * kc:512 * (kc + 1)],
                              in0=pbank(env, sbank),
                              scalar1=fscr[:, 15 + m:16 + m], scalar2=None,
                              op0=ALU.mult,
                              waits=pw + [(ss, v_cf)], inc=vv)
            st[("pfree", sbank)] = (vv, v_pe)
        v_p_all = env.sem("vv").n
        nh = (nkc + 15) // 16
        first_write = {dc: True for dc in range(NF)}
        for hno in range(nh):
            klo, khi = 16 * hno, min(16 * (hno + 1), nkc)
            v_vl = p.dma(
                "sync",
                vstg[:].rearrange("p (k d) -> p k d", k=16)[:, 0:khi - klo, :],
                vtok_d.rearrange("(kc pp) d -> pp kc d", pp=128)[:, klo:khi, :],
                waits=[(env.sem("oo"), v_vtok_oo),
                       (tt, st.get("vstg_free", 0))], inc=lo)
            for dc in range(NF):
                lows = [k for k in range(klo, min(khi, 4 * qc))]
                strs = [k for k in range(max(klo, 4 * qc), khi)]
                av = {}
                for b, kcs in ((4, lows), (5, strs)):
                    for k_i, kc in enumerate(kcs):
                        wts = []
                        if k_i == 0:
                            wts = [(lo, v_vl), (vv, v_p_all),
                                   (vv, st.get("combfree", 0)),
                                   (tt, st.get("outdone", 0))]
                        v_av = p.emit(
                            "tensor", "matmul", pbank(env, b),
                            vstg[:].rearrange("p (k d) -> p k d", k=16)[
                                :, kc - klo, 128 * dc:128 * (dc + 1)],
                            probs[:, 512 * kc:512 * (kc + 1)],
                            start=(k_i == 0), stop=(k_i == len(kcs) - 1),
                            waits=wts, inc=(tt if k_i == len(kcs) - 1
                                            else None))
                    if kcs:
                        av[b] = tt.n
                rq = readsq[:, 512 * dc:512 * (dc + 1)]
                v_cb = None
                if 4 in av:
                    p.emit("vector", "tensor_mul", comb[:, 0:512],
                           pbank(env, 4), Etile[:],
                           waits=[(tt, av[4]), (ss, v_E)])
                    if first_write[dc]:
                        v_cb = p.emit("vector", "tensor_copy", rq,
                                      comb[:, 0:512], inc=vv)
                    else:
                        v_cb = p.emit("vector", "tensor_add", rq, rq,
                                      comb[:, 0:512], inc=vv)
                    first_write[dc] = False
                if 5 in av:
                    if first_write[dc]:
                        v_cb = p.emit("vector", "tensor_copy", rq,
                                      pbank(env, 5),
                                      waits=[(tt, av[5])], inc=vv)
                    else:
                        v_cb = p.emit("vector", "tensor_add", rq, rq,
                                      pbank(env, 5),
                                      waits=[(tt, av[5])], inc=vv)
                    first_write[dc] = False
                st["combfree"] = v_cb
            st["vstg_free"] = env.sem("tt").n
        for kc in range(nkc):
            st[("pused", kc)] = env.sem("tt").n
        v_reads = env.sem("vv").n

        def mk_out(oc, tc, bank, war, _vr=v_reads, _wr=wr_state):
            if _wr["oc"] != oc:
                _wr["v"] = p.dma(
                    "gpsimd", wslot(env, 0)[:, 0:1024],
                    env.dram["l1b_wrT"][oc],
                    waits=[(tt, tt.n)], inc=wts_[0])
                _wr["oc"] = oc
            for ec in range(NF):
                wts = []
                if ec == 0:
                    wts = list(war) + [(wts_[0], _wr["v"]), (vv, _vr)]
                r = p.emit("tensor", "matmul", pbank(env, bank),
                           wslot(env, 0)[:, 128 * ec:128 * (ec + 1)],
                           readsq[:, 512 * ec:512 * (ec + 1)],
                           start=(ec == 0), stop=(ec == NF - 1),
                           waits=wts, inc=(tt if ec == NF - 1 else None))
            return tt, r

        emit_out_phase(env, x_ap, y_ap, mk_out, [qc], bias_col=None, scale=1.0)
        st["outdone"] = env.sem("tt").n
        wr_state["oc"] = -1


EMITTERS = {
    "l1f": emit_l1f,
    "l0f": emit_l0f,
    "l0b": emit_l0b,
    "l0a": emit_l0a,
    "l1a": emit_l1a,
    "l1b": emit_l1b,
}


def build_program(sublayers, dev_arrays, dump=(), serial=False,
                  race_detect=True):
    nc = bass.Bass("TRN2", target_bir_lowering=False,
                   detect_race_conditions=race_detect)
    env = Env()
    env.plan.self_sems = {"vector": env.sems["vv"], "scalar": env.sems["ss"]}
    if serial:
        env.plan.serial_sems = env.sems
    env.nc = nc

    x_in = nc.dram_tensor("x", [S, D], F32, kind="ExternalInput").ap()
    y_out = nc.dram_tensor("y", [S, D], F32, kind="ExternalOutput").ap()
    xs = [nc.dram_tensor(f"xs{i}", [S, D], F32, kind="Internal").ap()
          for i in range(2)]
    env.dram["attn_oT"] = nc.dram_tensor("attn_oT", [D, S], BF16,
                                         kind="Internal").ap()
    env.dram["vtok_d"] = nc.dram_tensor("vtok_d", [S, D], BF16,
                                        kind="Internal").ap()

    for name, arr in dev_arrays.items():
        dt = {np.dtype(np.float32): F32, np.dtype(np.int32): I32}[arr.dtype]
        env.dram[name] = nc.dram_tensor(name, list(arr.shape), dt,
                                        kind="ExternalInput").ap()

    es = ExitStack()
    with es:
        alloc_common(env, nc, es)
        for s in env.sems.values():
            s.h = es.enter_context(nc.semaphore(s.name))
        block = es.enter_context(nc.Block())

        p = env.plan
        for nm in ("xnT", "arena1", "arena2", "wstage", "xf", "yf", "fscr",
                   "cols", "acols"):
            p.emit("vector", "memset", env.t[nm][:], 0.0)
        v_init = p.emit("vector", "engine_nop", inc=env.sem("vv"))
        p.wait("sync", (env.sem("vv"), v_init))
        p.wait("gpsimd", (env.sem("vv"), v_init))
        p.dma("gpsimd", env.t["ident"][:], env.dram["ident"],
              inc=env.sem("ic"))
        p.dma("sync", env.t["identf"][:], env.dram["ident"],
              inc=env.sem("ic2"))
        p.wait("tensor", (env.sem("ic"), 16), (env.sem("ic2"), 16))
        p.emit("vector", "memset", env.t["epsc"][:], LN_EPS,
               inc=env.sem("vv"))

        cur = x_in
        for idx, sl in enumerate(sublayers):
            last = idx == len(sublayers) - 1
            dst = y_out if last else xs[idx % 2]
            sub_reset(env)
            EMITTERS[SUB_ORDER[sl]](env, cur, dst)
            cur = dst
            if not last:
                barrier(env)

        for dname in dump:
            th = env.t[dname]
            sh = list(th.shape) if hasattr(th, "shape") else None
            dout = nc.dram_tensor("dump_" + dname, sh,
                                  th[:].dtype, kind="ExternalOutput").ap()
            barrier(env)
            p.dma("sync", dout, th[:], inc=env.sem("oo"))

        p.wait("sync", (env.sem("oo"), env.sem("oo").n),
               (env.sem("oob"), env.sem("oob").n))

        for ename in Plan.ENGINES:
            def runner(eng, _ops=env.plan.ops[ename]):
                for f in _ops:
                    f(eng)
            getattr(block, ename)(runner)

    return nc


GROUPS = [list(range(8))]


def kernel(**inputs):
    dev = host_prep(inputs)
    x = np.asarray(inputs["L0_attn_a_x"])
    cur = [np.ascontiguousarray(x[c]) for c in range(8)]
    for g in GROUPS:
        nc = build_program(g, dev)
        in_maps = []
        for c in range(8):
            m = dict(dev)
            m["x"] = cur[c]
            in_maps.append(m)
        res = run_bass_kernel_spmd(nc, in_maps, core_ids=list(range(8)))
        cur = [np.ascontiguousarray(r["y"]) for r in res.results]
    return np.stack(cur).astype(np.float32)


# version 11 (final)
# speedup vs baseline: 1.1935x; 1.0145x over previous
import numpy as np
import ml_dtypes
from contextlib import ExitStack

import concourse.bass as bass
import concourse.mybir as mybir
from concourse.bass_utils import run_bass_kernel_spmd

F32 = mybir.dt.float32
BF16 = mybir.dt.bfloat16
I32 = mybir.dt.int32
AF = mybir.ActivationFunctionType
ALU = mybir.AluOpType
AX = mybir.AxisListType

S = 4096
D = 1024
NT = S // 128
NF = D // 128
TQ = 512
NQ = S // TQ
H = 16
DK = 64
LN_EPS = 1e-5

SUB_ORDER = ["l0a", "l0f", "l0b", "l0f", "l1a", "l1f", "l1b", "l1f"]


class Sem:
    def __init__(self, name):
        self.name = name
        self.n = 0
        self.h = None

    def bump(self, amt):
        self.n += amt
        return self.n


class Plan:
    ENGINES = ("sync", "gpsimd", "vector", "scalar", "tensor")

    def __init__(self):
        self.ops = {e: [] for e in self.ENGINES}
        self.serial_sems = None
        self.self_sems = {}

    def raw(self, engine, fn):
        self.ops[engine].append(fn)

    def emit(self, engine, method, *args, waits=(), inc=None, inc_amt=1, **kwargs):
        if self.serial_sems is not None:
            waits = tuple(waits) + tuple(
                (s, s.n) for s in self.serial_sems.values() if s.n > 0)
        ssem = self.self_sems.get(engine)
        if ssem is not None:
            if ssem.n > 0:
                waits = tuple(waits) + ((ssem, ssem.n),)
            if inc is None:
                inc = ssem
                inc_amt = 1
        waits = tuple(waits)

        def f(eng):
            for s, v in waits:
                eng.wait_ge(s.h, v)
            ins = getattr(eng, method)(*args, **kwargs)
            if inc is not None:
                ins.then_inc(inc.h, inc_amt)
        self.ops[engine].append(f)
        if inc is not None:
            return inc.bump(inc_amt)
        return None

    def dma(self, queue, out, in_, waits=(), inc=None, **kwargs):
        return self.emit(queue, "dma_start", out=out, in_=in_, waits=waits,
                         inc=inc, inc_amt=16, **kwargs)

    def wait(self, engine, *waits):
        waits = tuple(waits)

        def f(eng):
            for s, v in waits:
                eng.wait_ge(s.h, v)
        self.ops[engine].append(f)


class Env:
    def __init__(self):
        self.plan = Plan()
        self.sems = {}
        self.t = {}
        self.dram = {}
        self._state = {}
        for name in ("ld", "ldb", "lo", "wt", "wt0", "wt1", "wt2", "wt3",
                     "et", "tt", "vv", "ss", "oo", "oob", "ic", "ic2"):
            self.sems[name] = Sem(name)

    def sem(self, name):
        return self.sems[name]

    def state(self, key, default):
        if key not in self._state:
            self._state[key] = default
        return self._state[key]


def barrier(env):
    p = env.plan
    marks = {}
    for e, sname in (("tensor", "tt"), ("vector", "vv"), ("scalar", "ss")):
        s = env.sem(sname)
        v = s.bump(1)

        def mk(sem=s):
            def f(eng):
                eng.nop().then_inc(sem.h, 1)
            return f
        p.raw(e, mk())
        marks[e] = (s, v)
    allm = list(marks.values())
    for e in ("tensor", "vector", "scalar"):
        p.wait(e, *[m for ee, m in marks.items() if ee != e])
    p.wait("gpsimd", *allm)
    p.wait("sync", *allm)


def sub_reset(env):
    env._state = {}


def host_prep(inputs):
    dev = {}
    A = lambda a: np.ascontiguousarray(np.asarray(a))

    def tile_asc(wT, nh):
        a = wT.shape[0] // 128
        return np.ascontiguousarray(
            wT.reshape(a, 128, nh, 128).transpose(2, 1, 0, 3).reshape(
                nh, 128, a * 128))
    dev["l1f_wgT"] = tile_asc(np.asarray(inputs["L1_ffn_w_gate"]).T, 24)
    dev["l1f_wuT"] = tile_asc(np.asarray(inputs["L1_ffn_w_up"]).T, 24)
    dev["l1f_wdT"] = tile_asc(np.asarray(inputs["L1_ffn_w_down"]).T, 8)

    qkv = np.asarray(inputs["L0_attn_a_qkv_w"]).reshape(16, 192, 1024)
    qkv_b = np.asarray(inputs["L0_attn_a_qkv_b"]).reshape(16, 192)
    dev["l0a_wqT"] = tile_asc(A(qkv[:, 0:64, :].reshape(1024, 1024).T), 8)
    dev["l0a_wkT"] = tile_asc(A(qkv[:, 64:128, :].reshape(1024, 1024).T), 8)
    dev["l0a_wvT"] = tile_asc(A(qkv[:, 128:192, :].reshape(1024, 1024).T), 8)
    dev["l0a_bq"] = A(qkv_b[:, 0:64].reshape(1024).reshape(8, 128).T)
    dev["l0a_bk"] = A(qkv_b[:, 64:128].reshape(1024).reshape(8, 128).T)
    dev["l0a_bv"] = A(qkv_b[:, 128:192].reshape(1024).reshape(8, 128).T)
    dev["l0a_woT"] = tile_asc(A(np.asarray(inputs["L0_attn_a_out_w"]).T), 8)
    dev["l0a_bo"] = A(np.asarray(inputs["L0_attn_a_out_b"]).reshape(8, 128).T)
    slopes = 0.5 ** np.arange(16)
    p = np.arange(128, dtype=np.float64)
    f = np.arange(512, dtype=np.float64)
    bias = np.zeros((128, 16 * 57), dtype=np.float64)
    for h in range(16):
        sl = slopes[h]
        for m in range(1, 29):
            bias[:, 57 * h + m] = sl * (p - 64) - sl * (128 * m - 64)
        for mp in range(1, 29):
            bias[:, 57 * h + 28 + mp] = -sl * p - sl * (128 * mp - 127)
    dev["l0a_bias"] = A(bias.astype(np.float32))
    eb = np.zeros((16, 2, 128, 512), dtype=np.float64)
    for h in range(16):
        sl = slopes[h]
        eb[h, 0] = np.exp(-sl * f)[None, :]
        eb[h, 1] = np.exp(-sl * (511.0 - f))[None, :]
    dev["l0a_eb"] = A(eb.astype(np.float32))
    T = np.zeros((16, 4, 128, 512), dtype=np.float64)
    for h in range(16):
        for o in range(4):
            T[h, o] = -8.0 * slopes[h] * np.abs(f[None, :] - p[:, None] - 128.0 * o)
    dev["l0a_T"] = A(T.astype(np.float32))

    dev["l0b_wqT"] = tile_asc(A(np.asarray(inputs["L0_attn_b_Wq"]).T), 8)
    dev["l0b_wkT"] = tile_asc(A(np.asarray(inputs["L0_attn_b_Wk"]).T), 8)
    dev["l0b_wvT"] = tile_asc(A(np.asarray(inputs["L0_attn_b_Wv"]).T), 8)
    dev["l0b_woT"] = tile_asc(A(np.asarray(inputs["L0_attn_b_Wout"]).T), 8)
    dev["l0b_bout"] = A(np.asarray(inputs["L0_attn_b_bout"]).reshape(8, 128).T)

    dev["l1a_wqT"] = tile_asc(A(np.asarray(inputs["L1_attn_a_Wq_w"]).T), 8)
    dev["l1a_wkT"] = tile_asc(A(np.asarray(inputs["L1_attn_a_Wk_w"]).T), 8)
    dev["l1a_wvT"] = tile_asc(A(np.asarray(inputs["L1_attn_a_Wv_w"]).T), 8)
    dev["l1a_woT"] = tile_asc(A(np.asarray(inputs["L1_attn_a_Wo_w"]).T), 8)
    dev["l1a_bq"] = A(np.asarray(inputs["L1_attn_a_Wq_b"]).reshape(8, 128).T)
    dev["l1a_bk"] = A(np.asarray(inputs["L1_attn_a_Wk_b"]).reshape(8, 128).T)
    dev["l1a_bv"] = A(np.asarray(inputs["L1_attn_a_Wv_b"]).reshape(8, 128).T)
    dev["l1a_bo"] = A(np.asarray(inputs["L1_attn_a_Wo_b"]).reshape(8, 128).T)
    pp = np.arange(128)[:, None]
    ff = np.arange(512)[None, :]
    cm = np.zeros((4, 128, 512), dtype=np.float32)
    for o in range(4):
        cm[o] = (128 * o + pp <= ff).astype(np.float32)
    dev["l1a_cmask"] = cm
    cmT = np.zeros((4, 128, 512), dtype=np.float32)
    for qs in range(4):
        cmT[qs] = (ff <= 128 * qs + pp).astype(np.float32)
    dev["l1a_cmaskT"] = cmT

    dev["l0f_gate"] = A(inputs["L0_ffn_gate_w"])
    sel8 = np.zeros((8, 1024), dtype=np.float32)
    for e in range(8):
        sel8[e, 128 * e:128 * (e + 1)] = 1.0
    dev["sel8"] = sel8
    def tile_uv(w):
        return np.ascontiguousarray(
            np.asarray(w, dtype=np.float32).reshape(8, 8, 128, 2, 128)
            .transpose(0, 3, 2, 1, 4).reshape(16, 128, 1024))
    dev["l0f_wu"] = tile_uv(inputs["L0_ffn_Wu"])
    dev["l0f_wv"] = tile_uv(inputs["L0_ffn_Wv"])
    dev["l0f_wo"] = np.ascontiguousarray(
        np.asarray(inputs["L0_ffn_Wo"], dtype=np.float32)
        .reshape(8, 2, 128, 8, 128).transpose(3, 2, 0, 1, 4)
        .reshape(8, 128, 2048))

    dev["l1b_wwT"] = tile_asc(A(np.asarray(inputs["L1_attn_b_W_write"]).T), 8)
    dev["l1b_wrT"] = tile_asc(A(np.asarray(inputs["L1_attn_b_W_read"]).T), 8)
    dev["l1b_decay"] = A(np.asarray(inputs["L1_attn_b_decay"]).reshape(1, 1))
    dev["l1b_la"] = A(np.asarray(inputs["L1_attn_b_log_alpha"]).reshape(1, 1))
    pp = np.arange(128, dtype=np.float64)[:, None]
    ff = np.arange(512, dtype=np.float64)[None, :]
    dev["l1b_iotaE"] = A(np.broadcast_to(ff - 64.0, (128, 512)).astype(np.float32))
    it = np.zeros((4, 128, 512), dtype=np.float32)
    tm = np.zeros((4, 128, 512), dtype=np.float32)
    for o in range(4):
        it[o] = (ff - pp - 128.0 * o - 1.0).astype(np.float32)
        tm[o] = ((ff - pp - 128.0 * o) >= 1.0).astype(np.float32)
    dev["l1b_iotaT"] = it
    dev["l1b_tmask"] = tm
    cm = np.zeros((128, 28), dtype=np.float64)
    for m in range(1, 29):
        cm[:, m - 1] = 128.0 * m + 63.0 - pp[:, 0]
    dev["l1b_colm"] = A(cm.astype(np.float32))

    dev["ident"] = np.eye(128, dtype=np.float32)
    return dev


def alloc_common(env, nc, es):
    t = env.t
    sb = lambda name, shape, dt: es.enter_context(nc.sbuf_tensor(name, shape, dt))
    t["xnT"] = sb("xnT", [128, NF * S], BF16)
    t["arena1"] = sb("arena1", [128, 32768], BF16)
    t["arena2"] = sb("arena2", [128, 16384], BF16)
    t["wstage"] = sb("wstage", [128, 8192], BF16)
    t["xf"] = sb("xf", [128, 4096], F32)
    t["yf"] = sb("yf", [128, 2048], F32)
    t["fscr"] = sb("fscr", [128, 640], F32)
    t["acols"] = sb("acols", [128, 912], F32)
    t["cols"] = sb("cols", [128, 256], F32)
    t["epsc"] = sb("epsc", [128, 1], F32)
    t["ident"] = sb("sb_ident", [128, 128], BF16)
    t["identf"] = sb("sb_identf", [128, 128], F32)
    t["psum"] = es.enter_context(nc.psum_tensor("psum", [128, 4096], F32))


def pbank(env, i):
    return env.t["psum"][:, 512 * i:512 * (i + 1)]


def pbank_bf16(env, i):
    return env.t["psum"][:, 512 * i:512 * (i + 1)].bitcast(BF16)


def wslot(env, i):
    return env.t["wstage"][:, 2048 * i:2048 * (i + 1)]


def emit_prologue(env, x_ap):
    p, t = env.plan, env.t
    tt, vv, ss = env.sem("tt"), env.sem("vv"), env.sem("ss")
    lds = [env.sem("ld"), env.sem("ldb")]
    xf = t["xf"]
    stats = t["cols"]
    xn_stage = t["arena2"]

    apply_done = {}
    tr_done = {}
    evac_done = {}

    for i in range(NT):
        buf = i % 2
        xt = xf[:, 1024 * buf:1024 * (buf + 1)]
        waits = [(vv, apply_done[i - 2])] if i >= 2 else []
        v_ld = p.dma("sync", xt, x_ap[128 * i:128 * (i + 1), :], waits=waits,
                     inc=lds[buf])
        p.wait("vector", (lds[buf], v_ld))
        p.emit("vector", "bn_stats", stats[:, 12 * buf:12 * buf + 6], xt[:, 0:512])
        p.emit("vector", "bn_stats", stats[:, 12 * buf + 6:12 * buf + 12],
               xt[:, 512:1024])
        v_ag = p.emit("vector", "bn_aggr", stats[:, 24 + 2 * buf:26 + 2 * buf],
                      stats[:, 12 * buf:12 * buf + 12], inc=vv)
        v_sq = p.emit("scalar", "activation", stats[:, 28 + buf:29 + buf],
                      stats[:, 25 + 2 * buf:26 + 2 * buf], AF.Sqrt,
                      bias=t["epsc"][:], scale=1.0,
                      waits=[(vv, v_ag)], inc=ss)
        v_rc = p.emit("vector", "reciprocal", stats[:, 30 + buf:31 + buf],
                      stats[:, 28 + buf:29 + buf], waits=[(ss, v_sq)], inc=vv)
        if getattr(env, "DEBUG_STATS", False):
            p.emit("vector", "tensor_copy", t["fscr"][:, 3 * i:3 * i + 2],
                   stats[:, 24 + 2 * buf:26 + 2 * buf])
            p.emit("vector", "tensor_copy", t["fscr"][:, 3 * i + 2:3 * i + 3],
                   stats[:, 30 + buf:31 + buf], waits=[(vv, v_rc)], inc=vv)
        xnt = xn_stage[:, 1024 * buf:1024 * (buf + 1)]
        aw = [(tt, tr_done[i - 2])] if i >= 2 else []
        aw.append((vv, v_rc))
        apply_done[i] = p.emit(
            "vector", "tensor_scalar", out=xnt, in0=xt,
            scalar1=stats[:, 24 + 2 * buf:25 + 2 * buf],
            scalar2=stats[:, 30 + buf:31 + buf],
            op0=ALU.subtract, op1=ALU.mult, waits=aw, inc=vv)
        bank = 6 + (i % 2)
        pb = pbank_bf16(env, bank)
        tw = [(vv, apply_done[i])]
        if i >= 2:
            tw.append((vv, evac_done[i - 2]))
        for fc in range(NF):
            r = p.emit("tensor", "transpose", pb[:, 128 * fc:128 * (fc + 1)],
                       xnt[:, 128 * fc:128 * (fc + 1)], t["ident"][:],
                       waits=(tw if fc == 0 else ()),
                       inc=(tt if fc == NF - 1 else None))
        tr_done[i] = r
        dst = t["xnT"][:].rearrange("p (f s) -> p f s", f=NF)[:, :,
                                                             128 * i:128 * (i + 1)]
        src = pb.rearrange("p (f s) -> p f s", f=NF)
        evac_done[i] = p.emit("vector", "tensor_copy", dst, src,
                              waits=[(tt, tr_done[i])], inc=vv)


def emit_out_phase(env, x_ap, y_ap, mk_psum_tile, tcs, bias_col=None, scale=1.0):
    p, t = env.plan, env.t
    ld, tt, vv, ss = (env.sem("ld"), env.sem("tt"), env.sem("vv"),
                      env.sem("ss"))
    oos = [env.sem("oo"), env.sem("oob")]
    stage = t["arena2"]
    so = 8192
    xf, yf = t["xf"], t["yf"]
    st = env.state("out_phase", {"bank_free": {0: 0, 1: 0},
                                 "tbank_free": {0: 0, 1: 0},
                                 "prev_add": env.sem("vv").n,
                                 "ydma": {0: 0, 1: 0}})

    for tc in tcs:
        xw = [(vv, st["prev_add"])] if st["prev_add"] else []
        v_ld = p.dma("sync", xf[:].rearrange("p (a b) -> p a b", a=4),
                     x_ap[512 * tc:512 * (tc + 1), :].rearrange(
                         "(a p) b -> p a b", p=128), waits=xw, inc=ld)
        last_ev = 0
        for oc in range(NF):
            bank = oc % 2
            bf = st["bank_free"][bank]
            war = [] if bf == 0 else [(env.sem(bf[0]), bf[1])]
            sem, val = mk_psum_tile(oc, tc, bank, war)
            dstg = stage[:, so + 512 * oc: so + 512 * (oc + 1)]
            if bias_col is not None:
                last_ev = p.emit("vector", "tensor_scalar", out=dstg,
                                 in0=pbank(env, bank), scalar1=scale,
                                 scalar2=bias_col[:, oc:oc + 1],
                                 op0=ALU.mult, op1=ALU.add,
                                 waits=[(sem, val)], inc=vv)
                st["bank_free"][bank] = ("vv", last_ev)
            else:
                last_ev = p.emit("scalar", "activation", dstg, pbank(env, bank),
                                 AF.Copy, scale=scale, waits=[(sem, val)],
                                 inc=ss)
                st["bank_free"][bank] = ("ss", last_ev)
        lsem = env.sem("vv") if bias_col is not None else env.sem("ss")
        for sub in range(4):
            bank = 2 + (sub % 2)
            pb = pbank_bf16(env, bank)
            tw = [(lsem, last_ev)]
            if st["tbank_free"][sub % 2]:
                tw.append((vv, st["tbank_free"][sub % 2]))
            for oc in range(NF):
                v_tr = p.emit("tensor", "transpose",
                              pb[:, 128 * oc:128 * (oc + 1)],
                              stage[:, so + 512 * oc + 128 * sub:
                                    so + 512 * oc + 128 * (sub + 1)],
                              t["ident"][:],
                              waits=(tw if oc == 0 else ()),
                              inc=(tt if oc == NF - 1 else None))
            ysl = yf[:, 1024 * (sub % 2):1024 * (sub % 2 + 1)]
            yw = [(tt, v_tr), (ld, v_ld)]
            if st["ydma"][sub % 2]:
                yw.append((oos[sub % 2], st["ydma"][sub % 2]))
            v_add = p.emit("vector", "tensor_add", ysl,
                           xf[:, 1024 * sub:1024 * (sub + 1)], pb[:, 0:1024],
                           waits=yw, inc=vv)
            st["tbank_free"][sub % 2] = v_add
            st["ydma"][sub % 2] = p.dma(
                "sync",
                y_ap[512 * tc + 128 * sub:512 * tc + 128 * (sub + 1), :],
                ysl, waits=[(vv, v_add)], inc=oos[sub % 2])
        st["prev_add"] = v_add


def emit_l1f(env, x_ap, y_ap):
    p, t = env.plan, env.t
    tt, vv, ss = env.sem("tt"), env.sem("vv"), env.sem("ss")
    wts_ = [env.sem("wt0"), env.sem("wt1"), env.sem("wt2"), env.sem("wt3")]
    HID = 3072
    NH = HID // 128
    TCB = 1024
    NSUB = TCB // 512
    wgT, wuT, wdT = env.dram["l1f_wgT"], env.dram["l1f_wuT"], env.dram["l1f_wdT"]

    emit_prologue(env, x_ap)
    pro_vv = env.sem("vv").n
    xnT = t["xnT"][:].rearrange("p (f s) -> p f s", f=NF)

    h = t["arena1"]
    ws_done = {}
    gfree = [0]
    ufree = [0]
    h_free = [0]
    sgt_free = {0: 0, 1: 0}

    def load_w(src, hc, slot):
        v = p.dma("gpsimd", wslot(env, slot)[:, 0:1024], src[hc],
                  waits=[(tt, ws_done.get(slot, 0))], inc=wts_[slot])
        return (wts_[slot], v)

    for tcb in range(S // TCB):
        tok0 = TCB * tcb
        for hc in range(NH):
            sg = (2 * hc) % 4
            su = (2 * hc + 1) % 4
            vg = load_w(wgT, hc, sg)
            vu = load_w(wuT, hc, su)
            for tsub in range(NSUB):
                q0 = tok0 + 512 * tsub
                for ic in range(NF):
                    wts = []
                    if ic == 0:
                        wts = [vg, (vv, pro_vv), (ss, gfree[0])]
                    v_g = p.emit("tensor", "matmul", pbank(env, 4),
                                 wslot(env, sg)[:, 128 * ic:128 * (ic + 1)],
                                 xnT[:, ic, q0:q0 + 512],
                                 start=(ic == 0), stop=(ic == NF - 1),
                                 waits=wts, inc=(tt if ic == NF - 1 else None))
                for ic in range(NF):
                    wts = []
                    if ic == 0:
                        wts = [vu, (vv, ufree[0])]
                    v_u = p.emit("tensor", "matmul", pbank(env, 5),
                                 wslot(env, su)[:, 128 * ic:128 * (ic + 1)],
                                 xnT[:, ic, q0:q0 + 512],
                                 start=(ic == 0), stop=(ic == NF - 1),
                                 waits=wts, inc=(tt if ic == NF - 1 else None))
                sgi = tsub % 2
                sg_t = t["arena2"][:, 512 * sgi:512 * (sgi + 1)]
                v_si = p.emit("scalar", "activation", sg_t, pbank(env, 4),
                              AF.Silu, scale=1.0 / 127.0,
                              waits=[(tt, v_g), (vv, sgt_free[sgi])], inc=ss)
                gfree[0] = v_si
                hsl = h[:, hc * TCB + 512 * tsub: hc * TCB + 512 * (tsub + 1)]
                hw = [(ss, v_si), (tt, v_u)]
                if tcb > 0 and hc == 0 and tsub == 0:
                    hw.append((tt, h_free[0]))
                v_h = p.emit("vector", "tensor_mul", hsl, sg_t, pbank(env, 5),
                             waits=hw, inc=vv)
                ufree[0] = v_h
                sgt_free[sgi] = v_h
            ws_done[sg] = tt.n
            ws_done[su] = tt.n
        v_h_all = vv.n

        wd_state = {"oc": -1, "v": 0}

        def mk_down(oc, tc, bank, war, _tok0=tok0, _v_h=v_h_all, _wd=wd_state):
            if _wd["oc"] != oc:
                _wd["v"] = p.dma(
                    "gpsimd", t["wstage"][:, 0:NH * 128], wdT[oc],
                    waits=[(tt, tt.n)], inc=wts_[0])
                _wd["oc"] = oc
            off = 512 * tc - _tok0
            for hc in range(NH):
                wts = []
                if hc == 0:
                    wts = list(war) + [(wts_[0], _wd["v"]), (vv, _v_h)]
                r = p.emit("tensor", "matmul", pbank(env, bank),
                           t["wstage"][:, 128 * hc:128 * (hc + 1)],
                           h[:, hc * TCB + off: hc * TCB + off + 512],
                           start=(hc == 0), stop=(hc == NH - 1),
                           waits=wts, inc=(tt if hc == NH - 1 else None))
            return tt, r

        emit_out_phase(env, x_ap, y_ap, mk_down,
                       range(NSUB * tcb, NSUB * (tcb + 1)),
                       bias_col=None, scale=1.0 / (127.0 * 127.0))
        h_free[0] = tt.n
        for sl in range(4):
            ws_done[sl] = tt.n


def emit_l0f(env, x_ap, y_ap):
    p, t = env.plan, env.t
    tt, vv, ss = env.sem("tt"), env.sem("vv"), env.sem("ss")
    icsem = env.sem("ic")
    wts_ = [env.sem("wt0"), env.sem("wt1"), env.sem("wt2"), env.sem("wt3")]
    NE, R, NR = 8, 256, 2
    TCB = 2048
    NSUB = TCB // 512

    emit_prologue(env, x_ap)
    pro_vv = env.sem("vv").n
    xnT = t["xnT"][:].rearrange("p (f s) -> p f s", f=NF)

    gstage = t["wstage"][:, 8064:8128]
    p.dma("gpsimd", gstage.rearrange("p (a e) -> p a e", a=8),
          env.dram["l0f_gate"].rearrange("(a p) e -> p a e", p=128), inc=icsem)
    sel8 = t["arena2"][0:8, 4608:5632]
    p.dma("gpsimd", sel8, env.dram["sel8"], inc=icsem)
    v_consts = icsem.n
    logitsT = t["xf"]
    lt = t["cols"]
    wrows = t["arena2"][0:8, 0:4096]

    for tc in range(NQ):
        for ic in range(NF):
            wts = []
            if ic == 0:
                wts = [(env.sem("ic"), v_consts), (vv, pro_vv)]
                if tc >= 1:
                    wts.append((vv, env._state.get("l0f_lg_evac", 0)))
            v_lg = p.emit("tensor", "matmul", pbank(env, 4)[0:8, :],
                          gstage[:, 8 * ic:8 * ic + 8],
                          xnT[:, ic, 512 * tc:512 * (tc + 1)],
                          start=(ic == 0), stop=(ic == NF - 1),
                          waits=wts, inc=(tt if ic == NF - 1 else None))
        env._state["l0f_lg_evac"] = p.emit(
            "vector", "tensor_copy", logitsT[0:8, 512 * tc:512 * (tc + 1)],
            pbank(env, 4)[0:8, :], waits=[(tt, v_lg)], inc=vv)
    v_lg_all = env.sem("vv").n

    BIGN = -1.0e9
    g_evac = {"fwd": 0, "bwd": 0}
    for i in range(NT):
        tw = [(vv, v_lg_all)] if i == 0 else [(vv, g_evac["fwd"])]
        v_tr = p.emit("tensor", "transpose", pbank(env, 5)[:, 0:8],
                      logitsT[0:8, 128 * i:128 * (i + 1)], t["identf"][0:8, 0:8],
                      waits=tw, inc=tt)
        l8 = lt[:, 8 * i:8 * i + 8]
        g_evac["fwd"] = p.emit("vector", "tensor_copy", l8, pbank(env, 5)[:, 0:8],
                               waits=[(tt, v_tr)], inc=vv)
        c = t["fscr"]
        p.emit("vector", "reduce_max", c[:, 0:1], l8, axis=AX.X)
        p.emit("vector", "tensor_scalar", out=c[:, 8:16], in0=l8,
               scalar1=c[:, 0:1], scalar2=None, op0=ALU.is_equal)
        p.emit("vector", "scalar_tensor_tensor", out=c[:, 16:24], in0=c[:, 8:16],
               scalar=BIGN, in1=l8, op0=ALU.mult, op1=ALU.add)
        p.emit("vector", "reduce_max", c[:, 1:2], c[:, 16:24], axis=AX.X)
        p.emit("vector", "tensor_scalar", out=c[:, 24:32], in0=l8,
               scalar1=c[:, 1:2], scalar2=None, op0=ALU.is_ge)
        v_nm = p.emit("vector", "tensor_scalar", out=c[:, 2:3], in0=c[:, 0:1],
                      scalar1=-1.0, scalar2=None, op0=ALU.mult, inc=vv)
        v_ex = p.emit("scalar", "activation", c[:, 32:40], l8, AF.Exp,
                      bias=c[:, 2:3], scale=1.0, waits=[(vv, v_nm)], inc=ss)
        p.emit("vector", "tensor_mul", c[:, 40:48], c[:, 32:40], c[:, 24:32],
               waits=[(ss, v_ex)])
        p.emit("vector", "reduce_sum", c[:, 3:4], c[:, 40:48], axis=AX.X)
        p.emit("vector", "reciprocal", c[:, 4:5], c[:, 3:4])
        v_wf = p.emit("vector", "tensor_scalar", out=l8, in0=c[:, 40:48],
                      scalar1=c[:, 4:5], scalar2=None, op0=ALU.mult, inc=vv)
        bw = [(vv, v_wf)]
        if g_evac["bwd"]:
            bw.append((vv, g_evac["bwd"]))
        v_tb = p.emit("tensor", "transpose", pbank(env, 5)[0:8, 128:256],
                      l8, t["identf"][:], waits=bw, inc=tt)
        g_evac["bwd"] = p.emit("vector", "tensor_copy",
                               wrows[:, 128 * i:128 * (i + 1)],
                               pbank(env, 5)[0:8, 128:256],
                               waits=[(tt, v_tb)], inc=vv)
    v_gate_done = p.emit("vector", "engine_nop", inc=vv)
    g_tt_floor = env.sem("tt").n

    sarena = t["arena1"]
    ws_done = {}
    ufree = [0]
    vfree = [0]
    bfree = [0]
    sg_free = {0: 0, 1: 0}
    s_free = [0]

    def load_uv(which, e, rc, slot):
        src = env.dram["l0f_wu" if which == "u" else "l0f_wv"]
        v = p.dma("gpsimd", wslot(env, slot)[:, 0:1024], src[2 * e + rc],
                  waits=[(tt, ws_done.get(slot, g_tt_floor))],
                  inc=wts_[slot])
        return (wts_[slot], v)

    for tcb in range(S // TCB):
        tok0 = TCB * tcb
        for e in range(NE):
            vlds = []
            for rc in range(NR):
                vlds.append(load_uv("u", e, rc, 2 * rc))
                vlds.append(load_uv("v", e, rc, 2 * rc + 1))
            for tsub in range(NSUB):
                q0 = tok0 + 512 * tsub
                bcw = [(vv, bfree[0])] if bfree[0] else [(vv, v_gate_done),
                                                          (icsem, v_consts)]
                v_bc = p.emit("tensor", "matmul", pbank(env, 6),
                              sel8[:, 128 * e:128 * (e + 1)],
                              wrows[:, q0:q0 + 512],
                              start=True, stop=True, waits=bcw, inc=tt)
                for rc in range(NR):
                    for ic in range(NF):
                        wts = []
                        if ic == 0:
                            wts = [vlds[2 * rc], (ss, ufree[0])]
                        v_u = p.emit("tensor", "matmul", pbank(env, 4),
                                     wslot(env, 2 * rc)[:, 128 * ic:128 * (ic + 1)],
                                     xnT[:, ic, q0:q0 + 512],
                                     start=(ic == 0), stop=(ic == NF - 1),
                                     waits=wts, inc=(tt if ic == NF - 1 else None))
                    for ic in range(NF):
                        wts = []
                        if ic == 0:
                            wts = [vlds[2 * rc + 1], (vv, vfree[0])]
                        v_v = p.emit("tensor", "matmul", pbank(env, 5),
                                     wslot(env, 2 * rc + 1)[:, 128 * ic:128 * (ic + 1)],
                                     xnT[:, ic, q0:q0 + 512],
                                     start=(ic == 0), stop=(ic == NF - 1),
                                     waits=wts, inc=(tt if ic == NF - 1 else None))
                    sgi = tsub % 2
                    sg_t = t["arena2"][:, 5632 + 512 * sgi:5632 + 512 * (sgi + 1)]
                    v_si = p.emit("scalar", "activation", sg_t, pbank(env, 4),
                                  AF.Silu, scale=1.0,
                                  waits=[(tt, v_u), (vv, sg_free[sgi])], inc=ss)
                    ufree[0] = v_si
                    stmp = t["arena2"][:, 6656 + 512 * sgi:6656 + 512 * (sgi + 1)]
                    v_s = p.emit("vector", "tensor_mul", stmp, sg_t, pbank(env, 5),
                                 waits=[(ss, v_si), (tt, v_v)], inc=vv)
                    vfree[0] = v_s
                    ssl = sarena[:, (e * NR + rc) * 2048 + 512 * tsub:
                                 (e * NR + rc) * 2048 + 512 * (tsub + 1)]
                    swt = [(vv, v_s), (tt, v_bc)]
                    if tcb > 0 and e == 0 and rc == 0 and tsub == 0:
                        swt.append((tt, s_free[0]))
                    v_sp = p.emit("vector", "tensor_mul", ssl, stmp,
                                  pbank(env, 6), waits=swt, inc=vv)
                    sg_free[sgi] = v_sp
                    if rc == NR - 1:
                        bfree[0] = v_sp
            for sl in range(4):
                ws_done[sl] = tt.n
        v_s_all = env.sem("vv").n

        wo_state = {"oc": -1, "v": 0}

        def mk_down(oc, tc, bank, war, _tok0=tok0, _v_s=v_s_all, _wo=wo_state):
            if _wo["oc"] != oc:
                _wo["v"] = p.dma(
                    "gpsimd", t["wstage"][:, 0:16 * 128],
                    env.dram["l0f_wo"][oc],
                    waits=[(tt, tt.n)], inc=wts_[0])
                _wo["oc"] = oc
            off = 512 * tc - _tok0
            for k in range(16):
                wts = []
                if k == 0:
                    wts = list(war) + [(wts_[0], _wo["v"]), (vv, _v_s)]
                r = p.emit("tensor", "matmul", pbank(env, bank),
                           t["wstage"][:, 128 * k:128 * (k + 1)],
                           sarena[:, k * 2048 + off: k * 2048 + off + 512],
                           start=(k == 0), stop=(k == 15),
                           waits=wts, inc=(tt if k == 15 else None))
            return tt, r

        emit_out_phase(env, x_ap, y_ap, mk_down,
                       range(NSUB * tcb, NSUB * (tcb + 1)),
                       bias_col=None, scale=1.0)
        s_free[0] = tt.n
        for sl in range(4):
            ws_done[sl] = tt.n


def emit_l0b(env, x_ap, y_ap):
    p, t = env.plan, env.t
    tt, vv, ss, ld = (env.sem("tt"), env.sem("vv"), env.sem("ss"),
                      env.sem("ld"))
    oos = [env.sem("oo"), env.sem("oob")]
    lo, ic2 = env.sem("lo"), env.sem("ic2")
    wts_ = [env.sem("wt0"), env.sem("wt1"), env.sem("wt2"), env.sem("wt3")]

    emit_prologue(env, x_ap)
    pro_vv = env.sem("vv").n
    xnT = t["xnT"][:].rearrange("p (f s) -> p f s", f=NF)
    attn_oT = env.dram["attn_oT"]

    a2 = t["arena2"]
    qT = a2[:, 0:4096]
    kT = a2[:, 4096:8192]
    vbuf = a2[:, 8192:12352]
    otst = a2[:, 12352:13376]
    bcast = a2[:, 13376:15424].bitcast(F32)
    probs = t["arena1"]
    fscr = t["fscr"]

    v_ones = p.emit("vector", "memset", fscr[0:1, 512:640], 1.0, inc=vv)
    p.emit("vector", "memset", fscr[64:65, 512:640], 1.0, inc=vv)
    v_bc0 = p.dma("sync", t["cols"][:, 240:248], env.dram["l0b_bout"], inc=ic2)

    accst = t["xf"][:, 0:1024]
    st_probs_free = [0, 0]
    st_pair_state = {}
    pend = []

    def flush_pend():
        while pend:
            pend.pop(0)()

    def stream_w(which, pr, slot):
        src = env.dram[which]
        v = p.dma("gpsimd", wslot(env, slot)[:, 0:1024], src[pr],
                  waits=[(tt, st_pair_state.get(("wfree", slot), 0))],
                  inc=wts_[slot])
        return (wts_[slot], v)

    for pr in range(8):
        flush_pend()
        v_wq = stream_w("l0b_wqT", pr, 0)
        v_wk = stream_w("l0b_wkT", pr, 1)
        v_wv = stream_w("l0b_wvT", pr, 2)
        prev_evac = st_pair_state.get("proj_evac", 0)
        for dst, wv, slot, tag in ((qT, v_wq, 0, "q"), (kT, v_wk, 1, "k")):
            for tc in range(NQ):
                bank = tc % 2
                for ic in range(NF):
                    wts = []
                    if ic == 0:
                        wts = [wv, (ss, st_pair_state.get(("pe", bank), 0))]
                        if tc == 0 and tag == "q":
                            wts.append((vv, pro_vv))
                            wts.append((vv, st_pair_state.get("norm_read", 0)))
                    v_mm = p.emit("tensor", "matmul", pbank(env, bank),
                                  wslot(env, slot)[:, 128 * ic:128 * (ic + 1)],
                                  xnT[:, ic, 512 * tc:512 * (tc + 1)],
                                  start=(ic == 0), stop=(ic == NF - 1),
                                  waits=wts, inc=(tt if ic == NF - 1 else None))
                st_pair_state[("pe", bank)] = p.emit(
                    "scalar", "activation", dst[:, 512 * tc:512 * (tc + 1)],
                    pbank(env, bank), AF.Copy, scale=1.0,
                    waits=[(tt, v_mm)], inc=ss)
        v_ms = p.emit("vector", "memset",
                      vbuf[:].rearrange("p (a c) -> p a c", c=65)[:, :, 64:65],
                      1.0, inc=vv)
        for tc in range(NQ):
            bank = tc % 2
            for ic in range(NF):
                wts = []
                if ic == 0:
                    wts = [v_wv, (ss, st_pair_state.get(("pe", bank), 0))]
                v_mm = p.emit("tensor", "matmul", pbank(env, bank),
                              wslot(env, 2)[:, 128 * ic:128 * (ic + 1)],
                              xnT[:, ic, 512 * tc:512 * (tc + 1)],
                              start=(ic == 0), stop=(ic == NF - 1),
                              waits=wts, inc=(tt if ic == NF - 1 else None))
            st_pair_state[("pe", bank)] = p.emit(
                "scalar", "activation", otst[:, 0:512], pbank(env, bank),
                AF.Copy, scale=1.0,
                waits=[(tt, v_mm), (vv, st_pair_state.get("vt_done", 0)),
                       (oos[0], st_pair_state.get(("odma", 0), 0)),
                       (oos[1], st_pair_state.get(("odma", 1), 0))],
                inc=ss)
            v_ev = st_pair_state[("pe", bank)]
            pb = pbank_bf16(env, 6)
            tw = [(ss, v_ev), (vv, st_pair_state.get("vt_done", 0)),
                  (vv, v_ms), (ss, st_pair_state.get(("bce", 0), 0)),
                  (vv, st_pair_state.get(("bcw", 0), 0))]
            for b4 in range(4):
                v_tr = p.emit("tensor", "transpose",
                              pb[:, 128 * b4:128 * (b4 + 1)],
                              otst[:, 128 * b4:128 * (b4 + 1)], t["ident"][:],
                              waits=(tw if b4 == 0 else ()),
                              inc=(tt if b4 == 3 else None))
            kc0 = tc * 4
            srcA = pb[:, 0:512].rearrange("p (a c) -> p a c", c=128)[:, :, 0:64]
            srcB = pb[:, 0:512].rearrange("p (a c) -> p a c", c=128)[:, :, 64:128]
            dstA = vbuf[:, 65 * kc0:65 * (kc0 + 4)].rearrange(
                "p (a c) -> p a c", c=65)[:, :, 0:64]
            dstB = vbuf[:, 2080 + 65 * kc0:2080 + 65 * (kc0 + 4)].rearrange(
                "p (a c) -> p a c", c=65)[:, :, 0:64]
            p.emit("vector", "tensor_copy", dstA, srcA, waits=[(tt, v_tr)])
            st_pair_state["vt_done"] = p.emit(
                "vector", "tensor_copy", dstB, srcB, inc=vv)
        v_proj_done = env.sem("vv").n

        for qc in range(NQ):
            q0 = 512 * qc
            exp_done = {}

            def emit_av(kcc, hh):
                wts = [(ss, exp_done[kcc])]
                if kcc == 0:
                    wts.append((ss, st_pair_state.get(("accfree", hh), 0)))
                return p.emit("tensor", "matmul",
                              pbank(env, 4 + hh)[0:65, :],
                              vbuf[:, 2080 * hh + 65 * kcc:
                                   2080 * hh + 65 * (kcc + 1)],
                              probs[:, 16384 * hh + 512 * kcc:
                                    16384 * hh + 512 * (kcc + 1)],
                              start=(kcc == 0), stop=(kcc == 31),
                              skip_group_check=True, waits=wts,
                              inc=(tt if kcc == 31 else None))

            for kc in range(32):
                bA = kc % 2
                bB = 2 + kc % 2
                wts = []
                if kc < 2:
                    wts = [(vv, v_proj_done),
                           (ss, st_pair_state.get("exp_last", 0))]
                else:
                    wts = [(ss, exp_done[kc - 2])]
                p.emit("tensor", "matmul", pbank(env, bA),
                       kT[0:64, 128 * kc:128 * (kc + 1)],
                       qT[0:64, q0:q0 + 512], start=True, stop=True,
                       tile_position=(0, 0), skip_group_check=True, waits=wts)
                v_s = p.emit("tensor", "matmul", pbank(env, bB),
                             kT[64:128, 128 * kc:128 * (kc + 1)],
                             qT[64:128, q0:q0 + 512], start=True, stop=True,
                             tile_position=(64, 0), skip_group_check=True,
                             inc=tt)
                if kc == 6:
                    flush_pend()
                if kc % 2 == 1:
                    p.emit("scalar", "activation",
                           probs[:, 512 * (kc - 1):512 * (kc + 1)],
                           t["psum"][:, 0:1024], AF.Exp, scale=0.125,
                           waits=[(tt, v_s)], inc=ss)
                    exp_done[kc - 1] = exp_done[kc] = p.emit(
                        "scalar", "activation",
                        probs[:, 16384 + 512 * (kc - 1):16384 + 512 * (kc + 1)],
                        t["psum"][:, 1024:2048], AF.Exp, scale=0.125, inc=ss)
                    if kc >= 3:
                        for kcc in (kc - 3, kc - 2):
                            emit_av(kcc, 0)
                            emit_av(kcc, 1)
            v_avh = {}
            for kcc in (30, 31):
                for hh in range(2):
                    v_avh[hh] = emit_av(kcc, hh)
            st_pair_state["exp_last"] = exp_done[31]
            v_rcs = {}
            for hh in range(2):
                v_ae = p.emit("scalar", "activation",
                              accst[0:65, 512 * hh:512 * (hh + 1)],
                              pbank(env, 4 + hh)[0:65, :], AF.Copy, scale=1.0,
                              waits=[(tt, v_avh[hh]),
                                     (vv, st_pair_state.get(("accrd", hh), 0))],
                              inc=ss)
                st_pair_state[("accfree", hh)] = v_ae
                v_rcs[hh] = p.emit("vector", "reciprocal",
                                   accst[64:65, 512 * hh:512 * (hh + 1)],
                                   accst[64:65, 512 * hh:512 * (hh + 1)],
                                   waits=[(ss, v_ae)], inc=vv)

            def norm_tail(v_rcs=dict(v_rcs), pr_=pr, q0_=q0):
                for hh in range(2):
                    v_bm = p.emit("tensor", "matmul", pbank(env, 6 + hh),
                                  fscr[64:65, 512:640],
                                  accst[64:65, 512 * hh:512 * (hh + 1)],
                                  start=True, stop=True,
                                  skip_group_check=True,
                                  waits=[(vv, v_rcs[hh]),
                                         (ss, st_pair_state.get(("bce", hh), 0))],
                                  inc=tt)
                    st_pair_state[("bm", hh)] = v_bm
                    v_be = p.emit("scalar", "activation",
                                  bcast[:, 512 * hh:512 * (hh + 1)],
                                  pbank(env, 6 + hh), AF.Copy, scale=1.0,
                                  waits=[(tt, v_bm),
                                         (vv, st_pair_state.get(("bcw", hh), 0))],
                                  inc=ss)
                    st_pair_state[("bce", hh)] = v_be
                    v_nm = p.emit("vector", "tensor_mul",
                                  otst[:, 512 * hh:512 * (hh + 1)][0:64, :],
                                  accst[0:64, 512 * hh:512 * (hh + 1)],
                                  bcast[0:64, 512 * hh:512 * (hh + 1)],
                                  waits=[(ss, v_be),
                                         (oos[hh],
                                          st_pair_state.get(("odma", hh), 0))],
                                  inc=vv)
                    st_pair_state[("bcw", hh)] = v_nm
                    st_pair_state[("accrd", hh)] = v_nm
                    st_pair_state["norm_read"] = v_nm
                    st_pair_state[("odma", hh)] = p.dma(
                        "sync",
                        attn_oT[128 * pr_ + 64 * hh:128 * pr_ + 64 * (hh + 1),
                                q0_:q0_ + 512],
                        otst[:, 512 * hh:512 * (hh + 1)][0:64, :],
                        waits=[(vv, v_nm)], inc=oos[hh])

            pend.append(norm_tail)
            st_probs_free[0] = env.sem("tt").n
        st_pair_state[("wfree", 0)] = env.sem("tt").n
        st_pair_state[("wfree", 1)] = env.sem("tt").n
        st_pair_state[("wfree", 2)] = env.sem("tt").n
        st_pair_state["proj_evac"] = env.sem("ss").n

    flush_pend()
    v_attn_all = [env.sem("oo").n, env.sem("oob").n]

    ost = t["arena1"]
    wo_state = {"oc": -1, "v": 0}
    ld_state = {"tc": -1, "v": 0}

    def mk_out(oc, tc, bank, war):
        if ld_state["tc"] != tc:
            ld_state["v"] = p.dma(
                "sync", ost[:, 0:4096].rearrange("p (a c) -> p a c", a=8),
                attn_oT.rearrange("(a p) s -> p a s", p=128)[:, :,
                                                            512 * tc:512 * (tc + 1)],
                waits=[(oos[0], v_attn_all[0]), (oos[1], v_attn_all[1]),
                       (tt, tt.n)], inc=lo)
            ld_state["tc"] = tc
        if wo_state["oc"] != oc:
            wo_state["v"] = p.dma(
                "gpsimd", t["wstage"][:, 0:1024], env.dram["l0b_woT"][oc],
                waits=[(tt, tt.n)], inc=wts_[0])
            wo_state["oc"] = oc
        for hc in range(NF):
            wts = []
            if hc == 0:
                wts = list(war) + [(wts_[0], wo_state["v"]), (lo, ld_state["v"])]
            r = p.emit("tensor", "matmul", pbank(env, bank),
                       t["wstage"][:, 128 * hc:128 * (hc + 1)],
                       ost[:, 512 * hc:512 * (hc + 1)],
                       start=(hc == 0), stop=(hc == NF - 1),
                       waits=wts, inc=(tt if hc == NF - 1 else None))
        return tt, r

    p.wait("scalar", (ic2, v_bc0))
    p.wait("vector", (ic2, v_bc0))
    emit_out_phase(env, x_ap, y_ap, mk_out, range(NQ),
                   bias_col=t["cols"][:, 240:248], scale=1.0)


def _l0a_regions(qc):
    out = []
    for kc in range(32):
        r = kc - 4 * qc
        if 0 <= r <= 3:
            out.append(("str", r))
        elif r < 0:
            out.append(("low", -r))
        else:
            out.append(("high", r - 3))
    return out


def emit_l0a(env, x_ap, y_ap):
    p, t = env.plan, env.t
    tt, vv, ss, ld = (env.sem("tt"), env.sem("vv"), env.sem("ss"),
                      env.sem("ld"))
    oos = [env.sem("oo"), env.sem("oob")]
    lo, ic2, et = env.sem("lo"), env.sem("ic2"), env.sem("et")
    wts_ = [env.sem("wt0"), env.sem("wt1"), env.sem("wt2"), env.sem("wt3")]

    emit_prologue(env, x_ap)
    pro_vv = env.sem("vv").n
    xnT = t["xnT"][:].rearrange("p (f s) -> p f s", f=NF)
    attn_oT = env.dram["attn_oT"]

    a2 = t["arena2"]
    qT = a2[:, 0:4096]
    kT = a2[:, 4096:8192]
    vbuf = a2[:, 8192:12352]
    otst = a2[:, 12352:13376]
    comb = a2[:, 13376:15424].bitcast(F32)
    probs = t["arena1"]
    fscr = t["fscr"]
    acols = t["acols"]
    etile = t["wstage"][:, 4096:6144]
    ttA = t["wstage"][:, 6144:8192]

    M_skip = [int(np.floor((112.2 / (0.5 ** h) + 127) / 128)) for h in range(16)]

    p.dma("sync", acols[:].rearrange("p (a c) -> p a c", a=4),
          env.dram["l0a_bias"].rearrange("p (a c) -> p a c", a=4), inc=ic2)
    p.dma("sync", t["cols"][:, 240:248], env.dram["l0a_bo"], inc=ic2)
    p.dma("sync", t["cols"][:, 216:224], env.dram["l0a_bq"], inc=ic2)
    p.dma("sync", t["cols"][:, 224:232], env.dram["l0a_bk"], inc=ic2)
    v_bo = p.dma("sync", t["cols"][:, 232:240], env.dram["l0a_bv"], inc=ic2)
    p.wait("scalar", (ic2, v_bo))
    p.wait("vector", (ic2, v_bo))
    p.emit("vector", "memset", fscr[64:65, 512:640], 1.0)
    v_ones = p.emit("vector", "memset", fscr[0:1, 512:640], 1.0, inc=vv)

    st = {}

    def stream_w(which, pr, slot):
        src = env.dram[which]
        v = p.dma("gpsimd", wslot(env, slot)[:, 0:1024], src[pr],
                  waits=[(tt, st.get(("wfree", slot), 0))], inc=wts_[slot])
        return (wts_[slot], v)

    PAIRS = getattr(env, "L0A_PAIRS", range(8))
    pend = []

    def flush_pend():
        while pend:
            pend.pop(0)()

    for pr in PAIRS:
        flush_pend()
        v_wq = stream_w("l0a_wqT", pr, 0)
        v_wk = stream_w("l0a_wkT", pr, 1)
        v_wv = stream_w("l0a_wvT", pr, 2)

        for which, wv, slot, dst, bcol0, bank in (
                ("q", v_wq, 0, qT, 216, 0), ("k", v_wk, 1, kT, 224, 1)):
            for tc in range(NQ):
                for ic in range(NF):
                    wts = []
                    if ic == 0:
                        wts = [wv, (vv, st.get(("pe", bank), 0))]
                        if tc == 0 and which == "q":
                            wts.append((vv, st.get(("nmdone", 0), pro_vv)))
                    v_mm = p.emit("tensor", "matmul", pbank(env, bank),
                                  wslot(env, slot)[:, 128 * ic:128 * (ic + 1)],
                                  xnT[:, ic, 512 * tc:512 * (tc + 1)],
                                  start=(ic == 0), stop=(ic == NF - 1),
                                  waits=wts, inc=(tt if ic == NF - 1 else None))
                st[("pe", bank)] = p.emit(
                    "vector", "tensor_scalar",
                    out=dst[:, 512 * tc:512 * (tc + 1)], in0=pbank(env, bank),
                    scalar1=1.0, scalar2=t["cols"][:, bcol0 + pr:bcol0 + pr + 1],
                    op0=ALU.mult, op1=ALU.add,
                    waits=[(tt, v_mm)], inc=vv)
        v_ms = p.emit("vector", "memset",
                      vbuf[:].rearrange("p (a c) -> p a c", c=65)[:, :, 64:65],
                      1.0, inc=vv)
        for tc in range(NQ):
            bank = tc % 2
            for ic in range(NF):
                wts = []
                if ic == 0:
                    wts = [v_wv, (vv, st.get(("pe", bank), 0))]
                v_mm = p.emit("tensor", "matmul", pbank(env, bank),
                              wslot(env, 2)[:, 128 * ic:128 * (ic + 1)],
                              xnT[:, ic, 512 * tc:512 * (tc + 1)],
                              start=(ic == 0), stop=(ic == NF - 1),
                              waits=wts, inc=(tt if ic == NF - 1 else None))
            st[("pe", bank)] = p.emit(
                "vector", "tensor_scalar", out=otst[:, 0:512],
                in0=pbank(env, bank), scalar1=1.0,
                scalar2=t["cols"][:, 232 + pr:233 + pr],
                op0=ALU.mult, op1=ALU.add,
                waits=[(tt, v_mm),
                       (vv, st.get("vt_done", 0)),
                       (oos[0], st.get(("odma", 0), 0)),
                       (oos[1], st.get(("odma", 1), 0))], inc=vv)
            v_ev = st[("pe", bank)]
            pb = pbank_bf16(env, 6)
            tw = [(vv, v_ev), (vv, st.get("vt_done", 0)), (vv, v_ms),
                  (vv, st.get("cfree_all", 0))]
            for b4 in range(4):
                v_tr = p.emit("tensor", "transpose",
                              pb[:, 128 * b4:128 * (b4 + 1)],
                              otst[:, 128 * b4:128 * (b4 + 1)], t["ident"][:],
                              waits=(tw if b4 == 0 else ()),
                              inc=(tt if b4 == 3 else None))
            kc0 = tc * 4
            srcA = pb[:, 0:512].rearrange("p (a c) -> p a c", c=128)[:, :, 0:64]
            srcB = pb[:, 0:512].rearrange("p (a c) -> p a c", c=128)[:, :, 64:128]
            dstA = vbuf[:, 65 * kc0:65 * (kc0 + 4)].rearrange(
                "p (a c) -> p a c", c=65)[:, :, 0:64]
            dstB = vbuf[:, 2080 + 65 * kc0:2080 + 65 * (kc0 + 4)].rearrange(
                "p (a c) -> p a c", c=65)[:, :, 0:64]
            p.emit("vector", "tensor_copy", dstA, srcA, waits=[(tt, v_tr)])
            st["vt_done"] = p.emit("vector", "tensor_copy", dstB, srcB, inc=vv)
        v_proj_done = env.sem("vv").n
        v_proj_tt = env.sem("tt").n

        for hh in range(2):
            for side in range(2):
                p.dma("gpsimd",
                      etile[:, 512 * (2 * hh + side):512 * (2 * hh + side + 1)],
                      env.dram["l0a_eb"][2 * pr + hh, side],
                      waits=[(tt, v_proj_tt), (vv, st.get("cfree_all", 0))],
                      inc=et)
        for o in range(4):
            p.dma("gpsimd", ttA[:, 512 * o:512 * (o + 1)],
                  env.dram["l0a_T"][2 * pr, o],
                  waits=[(tt, v_proj_tt),
                         (vv, st.get("str_add_last", 0))], inc=et)
            p.dma("gpsimd", t["xf"][:, 256 * o:256 * (o + 1)].bitcast(BF16),
                  env.dram["l0a_T"][2 * pr + 1, o],
                  waits=[(vv, st.get("str_add_last", pro_vv))], inc=et)
        v_et_all = et.n

        if getattr(env, "L0A_PROJ_ONLY", False):
            continue
        for qc in range(NQ):
            q0 = 512 * qc
            regions = _l0a_regions(qc)
            comp = {0: [], 1: []}
            for hh in range(2):
                h = 2 * pr + hh
                for kc, (reg, m) in enumerate(regions):
                    if reg in ("low", "high") and m > M_skip[h]:
                        continue
                    comp[hh].append((kc, reg, m))
            bmap = {(0, "low"): 2, (0, "high"): 3, (0, "str"): 4,
                    (1, "low"): 5, (1, "high"): 6, (1, "str"): 7}
            for hh in range(2):
                h = 2 * pr + hh
                base_k = 64 * hh
                pbuf = 16384 * hh
                exp_done = {}
                seen = {}
                for (kc, reg, m) in comp[hh]:
                    seen.setdefault(reg, []).append(kc)
                first_in_reg = {reg: kcs[0] for reg, kcs in seen.items()}
                last_in_reg = {reg: kcs[-1] for reg, kcs in seen.items()}
                ncomp = len(comp[hh])

                def emit_av(j2):
                    kc2, reg2, _m2 = comp[hh][j2]
                    bank2 = bmap[(hh, reg2)]
                    wts2 = [(ss, exp_done[j2])]
                    if kc2 == first_in_reg[reg2]:
                        wts2.append((vv, st.get(("cfree", hh), 0)))
                        if reg2 == "str":
                            wts2.append((vv, st.get(("nmdone", hh), 0)))
                    return p.emit("tensor", "matmul",
                                  pbank(env, bank2)[0:65, :],
                                  vbuf[:, 2080 * hh + 65 * kc2:
                                       2080 * hh + 65 * (kc2 + 1)],
                                  probs[:, pbuf + 512 * kc2:
                                        pbuf + 512 * (kc2 + 1)],
                                  start=(kc2 == first_in_reg[reg2]),
                                  stop=(kc2 == last_in_reg[reg2]),
                                  skip_group_check=True,
                                  waits=wts2,
                                  inc=(tt if kc2 == last_in_reg[reg2]
                                       else None))

                for j, (kc, reg, m) in enumerate(comp[hh]):
                    sbank = j % 2
                    wts = []
                    if j < 2:
                        wts = [(vv, v_proj_done), (tt, st.get("avdone", 0)),
                               (ss, st.get("exp_last", 0))]
                    else:
                        wts = [(ss, exp_done[j - 2])]
                    v_s = p.emit("tensor", "matmul", pbank(env, sbank),
                                 kT[base_k:base_k + 64, 128 * kc:128 * (kc + 1)],
                                 qT[base_k:base_k + 64, q0:q0 + 512],
                                 start=True, stop=True,
                                 tile_position=(base_k, 0),
                                 skip_group_check=True,
                                 waits=wts, inc=tt)
                    ew = [(tt, v_s)]
                    esrc = pbank(env, sbank)
                    if reg == "str":
                        tsl = (ttA[:, 512 * m:512 * (m + 1)] if hh == 0 else
                               t["xf"][:, 256 * m:256 * (m + 1)].bitcast(BF16))
                        stg = t["yf"][:, 512:1024]
                        v_s2 = p.emit("vector", "tensor_add", stg,
                                      pbank(env, sbank), tsl,
                                      waits=[(tt, v_s), (et, v_et_all),
                                             (ss, env.sem("ss").n)],
                                      inc=vv)
                        st["str_add_last"] = v_s2
                        ew = [(vv, v_s2)]
                        esrc = stg
                        bcol = acols[:, 57 * h:57 * h + 1]
                    elif reg == "low":
                        bcol = acols[:, 57 * h + m:57 * h + m + 1]
                    else:
                        bcol = acols[:, 57 * h + 28 + m:57 * h + 28 + m + 1]
                    exp_done[j] = p.emit(
                        "scalar", "activation",
                        probs[:, pbuf + 512 * kc:pbuf + 512 * (kc + 1)],
                        esrc, AF.Exp, bias=bcol, scale=0.125,
                        waits=ew, inc=ss)
                    if hh == 0 and j == min(1, ncomp - 1):
                        flush_pend()
                    if j >= 1:
                        emit_av(j - 1)
                emit_av(ncomp - 1)
                st["avdone"] = env.sem("tt").n
                st["exp_last"] = exp_done[ncomp - 1]

                creg = comb[:, 512 * hh:512 * (hh + 1)]
                have = set(r for (_, r, _) in comp[hh])
                el = etile[0:65, 512 * 2 * hh:512 * (2 * hh + 1)]
                eh = etile[0:65, 512 * (2 * hh + 1):512 * (2 * hh + 2)]
                scr = t["yf"][:, 0:512]
                v_c = p.emit("vector", "tensor_copy", creg[0:65, :],
                             pbank(env, bmap[(hh, "str")])[0:65, :],
                             waits=[(tt, st["avdone"]),
                                    (vv, st.get(("nmdone", hh), 0))], inc=vv)
                if "low" in have:
                    p.emit("vector", "tensor_mul", scr[0:65, :],
                           pbank(env, bmap[(hh, "low")])[0:65, :], el,
                           waits=[(et, v_et_all)])
                    v_c = p.emit("vector", "tensor_add", creg[0:65, :],
                                 creg[0:65, :], scr[0:65, :], inc=vv)
                if "high" in have:
                    p.emit("vector", "tensor_mul", scr[0:65, :],
                           pbank(env, bmap[(hh, "high")])[0:65, :], eh,
                           waits=[(et, v_et_all)])
                    v_c = p.emit("vector", "tensor_add", creg[0:65, :],
                                 creg[0:65, :], scr[0:65, :], inc=vv)
                st[("cfree", hh)] = v_c
                st["cfree_all"] = v_c
                v_rc = p.emit("vector", "reciprocal",
                              creg[64:65, :], creg[64:65, :],
                              waits=[(vv, v_c)], inc=vv)

                def c_tail(v_rc=v_rc, hh=hh, pr_=pr, q0_=q0, creg=creg,
                           sb_=bmap[(hh, "str")]):
                    v_bm = p.emit("tensor", "matmul", pbank(env, sb_),
                                  fscr[64:65, 512:640],
                                  creg[64:65, :],
                                  start=True, stop=True,
                                  skip_group_check=True,
                                  waits=[(vv, v_rc), (vv, v_ones)], inc=tt)
                    st[("bm", hh)] = v_bm
                    v_nm = p.emit("vector", "tensor_mul",
                                  otst[:, 512 * hh:512 * (hh + 1)][0:64, :],
                                  creg[0:64, :], pbank(env, sb_)[0:64, :],
                                  waits=[(tt, v_bm),
                                         (oos[hh],
                                          st.get(("odma", hh), 0))],
                                  inc=vv)
                    st[("nmdone", hh)] = v_nm
                    st[("odma", hh)] = p.dma(
                        "sync",
                        attn_oT[128 * pr_ + 64 * hh:128 * pr_ + 64 * (hh + 1),
                                q0_:q0_ + 512],
                        otst[:, 512 * hh:512 * (hh + 1)][0:64, :],
                        waits=[(vv, v_nm)], inc=oos[hh])

                pend.append(c_tail)
        st[("wfree", 0)] = env.sem("tt").n
        st[("wfree", 1)] = env.sem("tt").n
        st[("wfree", 2)] = env.sem("tt").n

    flush_pend()
    v_attn_all = [env.sem("oo").n, env.sem("oob").n]

    ost = t["arena1"]
    wo_state = {"oc": -1, "v": 0}
    ld_state = {"tc": -1, "v": 0}

    def mk_out(oc, tc, bank, war):
        if ld_state["tc"] != tc:
            ld_state["v"] = p.dma(
                "sync", ost[:, 0:4096].rearrange("p (a c) -> p a c", a=8),
                attn_oT.rearrange("(a p) s -> p a s", p=128)[:, :,
                                                            512 * tc:512 * (tc + 1)],
                waits=[(oos[0], v_attn_all[0]), (oos[1], v_attn_all[1]),
                       (tt, tt.n)], inc=lo)
            ld_state["tc"] = tc
        if wo_state["oc"] != oc:
            wo_state["v"] = p.dma(
                "gpsimd", t["wstage"][:, 0:1024], env.dram["l0a_woT"][oc],
                waits=[(tt, tt.n)], inc=wts_[0])
            wo_state["oc"] = oc
        for hc in range(NF):
            wts = []
            if hc == 0:
                wts = list(war) + [(wts_[0], wo_state["v"]), (lo, ld_state["v"])]
            r = p.emit("tensor", "matmul", pbank(env, bank),
                       t["wstage"][:, 128 * hc:128 * (hc + 1)],
                       ost[:, 512 * hc:512 * (hc + 1)],
                       start=(hc == 0), stop=(hc == NF - 1),
                       waits=wts, inc=(tt if hc == NF - 1 else None))
        return tt, r

    if getattr(env, "L0A_SKIP_OUT", False):
        return
    p.wait("scalar", (ic2, v_bo))
    p.wait("vector", (ic2, v_bo))
    emit_out_phase(env, x_ap, y_ap, mk_out, range(NQ),
                   bias_col=t["cols"][:, 240:248], scale=1.0)


def emit_l1a(env, x_ap, y_ap):
    p, t = env.plan, env.t
    tt, vv, ss = env.sem("tt"), env.sem("vv"), env.sem("ss")
    oos = [env.sem("oo"), env.sem("oob")]
    lo, ic2, et = env.sem("lo"), env.sem("ic2"), env.sem("et")
    wts_ = [env.sem("wt0"), env.sem("wt1"), env.sem("wt2"), env.sem("wt3")]

    emit_prologue(env, x_ap)
    pro_vv = env.sem("vv").n
    xnT = t["xnT"][:].rearrange("p (f s) -> p f s", f=NF)
    attn_oT = env.dram["attn_oT"]

    a2 = t["arena2"]
    qT = a2[:, 0:4096]
    kT = a2[:, 4096:8192]
    vbuf = a2[:, 8192:12352]
    otst = a2[:, 12352:13376]
    probsK = a2[:, 13376:15424]
    pk1 = [t["xf"][:, 2560 + 256 * i:2816 + 256 * i].bitcast(BF16)
           for i in range(4)]
    ptmp4 = {(0, 0): a2[:, 15424:15936],
             (0, 1): t["xf"][:, 1280:1536].bitcast(BF16),
             (1, 0): t["xf"][:, 1024:1280].bitcast(BF16),
             (1, 1): t["xf"][:, 3584:3840].bitcast(BF16)}
    creg_all = t["xf"][:, 1536:2560]
    probs = t["arena1"]
    fscr = t["fscr"]
    cmask = t["wstage"][:, 6144:8192]

    p.dma("sync", t["cols"][:, 240:248], env.dram["l1a_bo"], inc=ic2)
    p.dma("sync", t["cols"][:, 216:224], env.dram["l1a_bq"], inc=ic2)
    p.dma("sync", t["cols"][:, 224:232], env.dram["l1a_bk"], inc=ic2)
    v_b = p.dma("sync", t["cols"][:, 232:240], env.dram["l1a_bv"], inc=ic2)
    p.wait("scalar", (ic2, v_b))
    p.wait("vector", (ic2, v_b))
    acols = t["acols"]
    ones2 = [acols[0:1, 0:128], acols[64:65, 0:128]]
    onesb = a2[:, 16383:16384]
    p.emit("vector", "memset", ones2[0], 1.0)
    p.emit("vector", "memset", ones2[1], 1.0)
    v_ones = p.emit("vector", "memset", onesb, 1.0, inc=vv)
    st = {}
    pend = []

    def flush_pend():
        while pend:
            pend.pop(0)()

    def stream_w(which, pr, slot):
        src = env.dram[which]
        v = p.dma("gpsimd", wslot(env, slot)[:, 0:1024], src[pr],
                  waits=[(tt, st.get(("wfree", slot), 0))], inc=wts_[slot])
        return (wts_[slot], v)

    for pr in range(getattr(env, "L1A_PAIRS", 8)):
        flush_pend()
        v_wq = stream_w("l1a_wqT", pr, 0)
        v_wk = stream_w("l1a_wkT", pr, 1)
        v_wv = stream_w("l1a_wvT", pr, 2)
        if pr == 0:
            for o in range(4):
                p.dma("gpsimd", cmask[:, 512 * o:512 * (o + 1)],
                      env.dram["l1a_cmaskT"][o], waits=[(tt, tt.n)], inc=et)
            v_cm = et.n

        for which, wv, slot, dst, bcol0, bank in (
                ("q", v_wq, 0, qT, 216, 0), ("k", v_wk, 1, kT, 224, 1)):
            for tc in range(NQ):
                for ic in range(NF):
                    wts = []
                    if ic == 0:
                        wts = [wv, (vv, st.get(("pe", bank), 0)),
                               (ss, st.get(("sb", bank), 0))]
                        if tc == 0 and which == "q":
                            wts.append((vv, st.get("norm_done", pro_vv)))
                    v_mm = p.emit("tensor", "matmul", pbank(env, bank),
                                  wslot(env, slot)[:, 128 * ic:128 * (ic + 1)],
                                  xnT[:, ic, 512 * tc:512 * (tc + 1)],
                                  start=(ic == 0), stop=(ic == NF - 1),
                                  waits=wts, inc=(tt if ic == NF - 1 else None))
                st[("pe", bank)] = p.emit(
                    "vector", "tensor_scalar",
                    out=dst[:, 512 * tc:512 * (tc + 1)], in0=pbank(env, bank),
                    scalar1=1.0, scalar2=t["cols"][:, bcol0 + pr:bcol0 + pr + 1],
                    op0=ALU.mult, op1=ALU.add,
                    waits=[(tt, v_mm)], inc=vv)
        v_ms = p.emit("vector", "memset",
                      vbuf[:].rearrange("p (a c) -> p a c", c=65)[:, :, 64:65],
                      1.0, inc=vv)
        for tc in range(NQ):
            bank = tc % 2
            for ic in range(NF):
                wts = []
                if ic == 0:
                    wts = [v_wv, (vv, st.get(("pe", bank), 0)),
                           (ss, st.get(("sb", bank), 0))]
                v_mm = p.emit("tensor", "matmul", pbank(env, bank),
                              wslot(env, 2)[:, 128 * ic:128 * (ic + 1)],
                              xnT[:, ic, 512 * tc:512 * (tc + 1)],
                              start=(ic == 0), stop=(ic == NF - 1),
                              waits=wts, inc=(tt if ic == NF - 1 else None))
            st[("pe", bank)] = p.emit(
                "vector", "tensor_scalar", out=otst[:, 0:512],
                in0=pbank(env, bank), scalar1=1.0,
                scalar2=t["cols"][:, 232 + pr:233 + pr],
                op0=ALU.mult, op1=ALU.add,
                waits=[(tt, v_mm), (vv, st.get("vt_done", 0)),
                       (oos[0], st.get(("odma", 0), 0)),
                       (oos[1], st.get(("odma", 1), 0))], inc=vv)
            v_ev = st[("pe", bank)]
            pb = pbank_bf16(env, 6)
            tw = [(vv, v_ev), (vv, st.get("vt_done", 0)), (vv, v_ms),
                  (vv, st.get("b6free", 0)), (vv, st.get(("tpe", 0), 0)),
                  (vv, st.get(("cnorm", 0), 0))]
            for b4 in range(4):
                v_tr = p.emit("tensor", "transpose",
                              pb[:, 128 * b4:128 * (b4 + 1)],
                              otst[:, 128 * b4:128 * (b4 + 1)], t["ident"][:],
                              waits=(tw if b4 == 0 else ()),
                              inc=(tt if b4 == 3 else None))
            kc0 = tc * 4
            srcA = pb[:, 0:512].rearrange("p (a c) -> p a c", c=128)[:, :, 0:64]
            srcB = pb[:, 0:512].rearrange("p (a c) -> p a c", c=128)[:, :, 64:128]
            dstA = vbuf[:, 65 * kc0:65 * (kc0 + 4)].rearrange(
                "p (a c) -> p a c", c=65)[:, :, 0:64]
            dstB = vbuf[:, 2080 + 65 * kc0:2080 + 65 * (kc0 + 4)].rearrange(
                "p (a c) -> p a c", c=65)[:, :, 0:64]
            p.emit("vector", "tensor_copy", dstA, srcA, waits=[(tt, v_tr)])
            st["vt_done"] = p.emit("vector", "tensor_copy", dstB, srcB, inc=vv)
        v_proj_done = env.sem("vv").n
        if getattr(env, "L1A_DBG", None) == pr:
            dbg = env.nc.dram_tensor("dbg_vbuf", [128, 4160], BF16,
                                     kind="ExternalOutput").ap()
            p.dma("sync", dbg, vbuf[:], waits=[(vv, v_proj_done)],
                  inc=env.sem("oo"))

        suf = {0: acols[0:65, 128:136], 1: acols[0:65, 136:144]}
        bstage = acols[0:65, 144:152]
        for hh in range(2):
            for kb in range(8):
                for i4 in range(4):
                    kc = 4 * kb + i4
                    wts = []
                    if kb == 0 and i4 == 0:
                        wts = [(vv, v_proj_done), (vv, st.get("b6free", 0)),
                               (vv, st.get("suf_done", 0)), (vv, v_ones),
                               (vv, st.get(("tpe", 0), 0)),
                               (vv, st.get(("cnorm", 0), 0))]
                    v_bs = p.emit("tensor", "matmul",
                                  pbank(env, 6)[0:65, kb:kb + 1],
                                  vbuf[:, 2080 * hh + 65 * kc:
                                       2080 * hh + 65 * (kc + 1)],
                                  onesb,
                                  start=(i4 == 0), stop=(i4 == 3),
                                  waits=wts,
                                  inc=(tt if i4 == 3 else None),
                                  skip_group_check=True)
            v_bse = p.emit("vector", "tensor_copy", bstage,
                           pbank(env, 6)[0:65, 0:8], waits=[(tt, v_bs)], inc=vv)
            st["b6free"] = v_bse
            p.emit("vector", "memset", suf[hh][:, 7:8], 0.0)
            for kb in range(6, -1, -1):
                p.emit("vector", "tensor_add", suf[hh][:, kb:kb + 1],
                       suf[hh][:, kb + 1:kb + 2],
                       bstage[:, kb + 1:kb + 2])
            st["suf_done"] = p.emit("vector", "engine_nop", inc=vv)
            if getattr(env, "L1A_DBG", None) == pr and hh == 1:
                dbg2 = env.nc.dram_tensor("dbg_suf", [128, 32], F32,
                                          kind="ExternalOutput").ap()
                p.dma("sync", dbg2, acols[:, 128:160],
                      waits=[(vv, st["suf_done"])], inc=env.sem("oo"))

        for qc in range(getattr(env, "L1A_MAXQC", NQ)):
            q0 = 512 * qc
            nkb = qc + 1

            def chunk_ap(hh, qs, kb):
                off = 16384 * hh + 4096 * qs + 512 * kb
                return probs[:, off:off + 512]

            def emit_A(hh, kb, qc=qc):
                base_k = 64 * hh
                for qs in range(4):
                    c2 = 32 * hh + 4 * kb + qs
                    b = 2 * hh + qs % 2
                    wts = [(ss, st.get(("sb", b), 0))]
                    if kb == 0 and qs == 0:
                        wts.append((vv, v_proj_done))
                    v_s = p.emit("tensor", "matmul", pbank(env, b),
                                 qT[base_k:base_k + 64,
                                    q0 + 128 * qs:q0 + 128 * (qs + 1)],
                                 kT[base_k:base_k + 64,
                                    512 * kb:512 * (kb + 1)],
                                 start=True, stop=True,
                                 tile_position=(base_k, 0),
                                 skip_group_check=True,
                                 waits=wts, inc=tt)
                    v_e1 = p.emit("scalar", "activation",
                                  chunk_ap(hh, qs, kb), pbank(env, b),
                                  AF.Exp, scale=0.125,
                                  waits=[(tt, v_s)], inc=ss)
                    st[("sb", b)] = v_e1
                    if kb == qc:
                        p.emit("vector", "tensor_mul",
                               chunk_ap(hh, qs, kb), chunk_ap(hh, qs, kb),
                               cmask[:, 512 * qs:512 * (qs + 1)],
                               waits=[(ss, v_e1), (et, v_cm)], inc=vv)
                        p.emit("vector", "bn_stats",
                               acols[:, 160 + 6 * c2:166 + 6 * c2],
                               chunk_ap(hh, qs, kb), inc=vv)
                    else:
                        p.emit("vector", "bn_stats",
                               acols[:, 160 + 6 * c2:166 + 6 * c2],
                               chunk_ap(hh, qs, kb),
                               waits=[(ss, v_e1)], inc=vv)

            def emit_B_both(kb, qc=qc, nkb=nkb):
                v_rcs = {}
                for hh in range(2):
                    base = 32 * hh + 4 * kb
                    for qs in range(4):
                        c2 = base + qs
                        p.emit("vector", "bn_aggr",
                               acols[:, 544 + 2 * c2:546 + 2 * c2],
                               acols[:, 160 + 6 * c2:166 + 6 * c2])
                    rr = acols[:, 672 + base:676 + base]
                    mv = acols[:, 544 + 2 * base:552 + 2 * base].rearrange(
                        "p (a c) -> p a c", c=2)[:, :, 0:1]
                    wmax = max(st.get(("rcread", base + q), 0)
                               for q in range(4))
                    p.emit("vector", "tensor_scalar_mul", rr, mv, 512.0,
                           waits=[(ss, wmax)])
                    v_rc4 = p.emit("vector", "reciprocal", rr, rr, inc=vv)
                    for qs in range(4):
                        v_rcs[(hh, qs)] = v_rc4
                ev_last = {}
                for qs in range(4):
                    for hh in range(2):
                        c2 = 32 * hh + 4 * kb + qs
                        pk = (hh, qs % 2)
                        rcol = acols[:, 672 + c2:673 + c2]
                        v_e2 = p.emit("scalar", "activation", ptmp4[pk],
                                      chunk_ap(hh, qs, kb), AF.Exp,
                                      scale=rcol,
                                      waits=[(vv, v_rcs[(hh, qs)]),
                                             (tt, st.get(("tpf", pk), 0))],
                                      inc=ss)
                        st[("rcread", c2)] = v_e2
                        pb = pbank_bf16(env, 6 + hh)
                        for i4 in range(4):
                            v_tr = p.emit(
                                "tensor", "transpose",
                                pb[:, 128 * i4:128 * (i4 + 1)],
                                ptmp4[pk][:, 128 * i4:128 * (i4 + 1)],
                                t["ident"][:],
                                waits=([(ss, v_e2),
                                        (vv, st.get(("tpe", hh), 0)),
                                        (vv, st.get(("cnorm", hh), 0))]
                                       if i4 == 0 else ()),
                                inc=(tt if i4 == 3 else None))
                        st[("tpf", pk)] = v_tr
                        ew = [(tt, v_tr), (tt, st.get(("avl", hh), 0))]
                        if hh == 0:
                            v_ev = p.emit(
                                "vector", "tensor_copy",
                                probsK.rearrange("p (a c) -> p a c",
                                                 c=512)[:, :, 128 * qs:
                                                        128 * (qs + 1)],
                                pb[:, 0:512].rearrange("p (a c) -> p a c",
                                                       c=128),
                                waits=ew, inc=vv)
                        else:
                            for i4 in range(4):
                                v_ev = p.emit(
                                    "vector", "tensor_copy",
                                    pk1[i4][:, 128 * qs:128 * (qs + 1)],
                                    pb[:, 128 * i4:128 * (i4 + 1)],
                                    waits=(ew if i4 == 0 else ()),
                                    inc=(vv if i4 == 3 else None))
                        st[("tpe", hh)] = v_ev
                        ev_last[hh] = v_ev
                for hh in range(2):
                    for i4 in range(4):
                        kc = 4 * kb + i4
                        wts = [(vv, ev_last[hh])] if i4 == 0 else []
                        if not st[("avst", hh)]:
                            wts.append((vv, st.get(("accfree", hh), 0)))
                        rhs = (probsK[:, 512 * i4:512 * (i4 + 1)] if hh == 0
                               else pk1[i4][:])
                        p.emit("tensor", "matmul",
                               pbank(env, 4 + hh)[0:65, :],
                               vbuf[:, 2080 * hh + 65 * kc:
                                    2080 * hh + 65 * (kc + 1)],
                               rhs,
                               start=(not st[("avst", hh)]),
                               stop=(kb == nkb - 1 and i4 == 3),
                               skip_group_check=True, waits=wts,
                               inc=(tt if (kb == nkb - 1 and i4 == 3)
                                    else None))
                        st[("avst", hh)] = True
                    st[("avl", hh)] = env.sem("tt").n

            st[("avst", 0)] = False
            st[("avst", 1)] = False
            for kb in range(nkb):
                for hh in range(2):
                    emit_A(hh, kb)
                if kb == 0:
                    flush_pend()
                if kb > 0:
                    emit_B_both(kb - 1)
            emit_B_both(nkb - 1)

            for hh in range(2):
                creg = creg_all[:, 512 * hh:512 * (hh + 1)]
                v_c = p.emit("vector", "tensor_scalar", out=creg[0:65, :],
                             in0=pbank(env, 4 + hh)[0:65, :],
                             scalar1=suf[hh][:, qc:qc + 1], scalar2=None,
                             op0=ALU.add,
                             waits=[(tt, st[("avl", hh)]),
                                    (vv, st.get(("cregfree", hh), 0))],
                             inc=vv)
                st[("accfree", hh)] = v_c
                v_rc2 = p.emit("vector", "reciprocal",
                               creg[64:65, :], creg[64:65, :], inc=vv)

                def c_tail(v_rc2=v_rc2, hh=hh, pr_=pr, q0_=q0, creg=creg):
                    v_bm = p.emit("tensor", "matmul", pbank(env, 6 + hh),
                                  ones2[1], creg[64:65, :],
                                  start=True, stop=True,
                                  skip_group_check=True,
                                  waits=[(vv, v_rc2),
                                         (vv, st.get(("tpe", hh), 0))],
                                  inc=tt)
                    st[("bm", hh)] = v_bm
                    v_nm = p.emit("vector", "tensor_mul",
                                  otst[:, 512 * hh:512 * (hh + 1)][0:64, :],
                                  creg[0:64, :],
                                  pbank(env, 6 + hh)[0:64, :],
                                  waits=[(tt, v_bm),
                                         (oos[hh],
                                          st.get(("odma", hh), 0))],
                                  inc=vv)
                    st[("cnorm", hh)] = v_nm
                    st[("cregfree", hh)] = v_nm
                    st[("odma", hh)] = p.dma(
                        "sync",
                        attn_oT[128 * pr_ + 64 * hh:
                                128 * pr_ + 64 * (hh + 1), q0_:q0_ + 512],
                        otst[:, 512 * hh:512 * (hh + 1)][0:64, :],
                        waits=[(vv, v_nm)], inc=oos[hh])

                pend.append(c_tail)

        st[("wfree", 0)] = env.sem("tt").n
        st[("wfree", 1)] = env.sem("tt").n
        st[("wfree", 2)] = env.sem("tt").n

    flush_pend()
    v_attn_all = [env.sem("oo").n, env.sem("oob").n]

    ost = t["arena1"]
    wo_state = {"oc": -1, "v": 0}
    ld_state = {"tc": -1, "v": 0}

    def mk_out(oc, tc, bank, war):
        if ld_state["tc"] != tc:
            ld_state["v"] = p.dma(
                "sync", ost[:, 0:4096].rearrange("p (a c) -> p a c", a=8),
                attn_oT.rearrange("(a p) s -> p a s", p=128)[:, :,
                                                            512 * tc:512 * (tc + 1)],
                waits=[(oos[0], v_attn_all[0]), (oos[1], v_attn_all[1]),
                       (tt, tt.n)], inc=lo)
            ld_state["tc"] = tc
        if wo_state["oc"] != oc:
            wo_state["v"] = p.dma(
                "gpsimd", t["wstage"][:, 0:1024], env.dram["l1a_woT"][oc],
                waits=[(tt, tt.n)], inc=wts_[0])
            wo_state["oc"] = oc
        for hc in range(NF):
            wts = []
            if hc == 0:
                wts = list(war) + [(wts_[0], wo_state["v"]), (lo, ld_state["v"])]
            r = p.emit("tensor", "matmul", pbank(env, bank),
                       t["wstage"][:, 128 * hc:128 * (hc + 1)],
                       ost[:, 512 * hc:512 * (hc + 1)],
                       start=(hc == 0), stop=(hc == NF - 1),
                       waits=wts, inc=(tt if hc == NF - 1 else None))
        return tt, r

    emit_out_phase(env, x_ap, y_ap, mk_out, range(NQ),
                   bias_col=t["cols"][:, 240:248], scale=1.0)


def emit_l1b(env, x_ap, y_ap):
    p, t = env.plan, env.t
    tt, vv, ss = env.sem("tt"), env.sem("vv"), env.sem("ss")
    lo, ic2, et = env.sem("lo"), env.sem("ic2"), env.sem("et")
    wts_ = [env.sem("wt0"), env.sem("wt1"), env.sem("wt2"), env.sem("wt3")]

    emit_prologue(env, x_ap)
    pro_vv = env.sem("vv").n
    xnT = t["xnT"][:].rearrange("p (f s) -> p f s", f=NF)
    acols = t["acols"]
    fscr = t["fscr"]
    probs = t["arena1"]
    vstg = t["arena1"][:, 16384:32768]
    a2 = t["arena2"]
    readsq = a2[:, 0:4096]
    vbuild = a2[:, 4096:8192]
    zsh0 = a2[:, 14400:15424]
    comb = a2[:, 13376:15424].bitcast(F32)
    Etile = t["wstage"][:, 6144:7168].bitcast(F32)
    Ttile = [t["wstage"][:, 2048 + 1024 * o:3072 + 1024 * o].bitcast(F32)
             for o in range(4)]
    mstage = t["wstage"][:, 7168:7680]
    vtok_d = env.dram["vtok_d"]

    p.dma("sync", fscr[0:1, 0:1], env.dram["l1b_decay"],
          allow_slow_non_contiguous=True, inc=ic2)
    p.dma("sync", fscr[0:1, 1:2], env.dram["l1b_la"],
          allow_slow_non_contiguous=True, inc=ic2)
    p.dma("sync", acols[:, 0:512], env.dram["l1b_iotaE"], inc=ic2)
    v_c0 = p.dma("sync", acols[:, 512:540], env.dram["l1b_colm"], inc=ic2)
    p.wait("scalar", (ic2, v_c0))
    p.wait("vector", (ic2, v_c0))
    v_ones = p.emit("vector", "memset", acols[0:1, 768:896], 1.0, inc=vv)
    v_g = p.emit("scalar", "activation", fscr[0:1, 2:3], fscr[0:1, 0:1],
                 AF.Sigmoid, scale=1.0, inc=ss)
    v_lg = p.emit("scalar", "activation", fscr[0:1, 3:4], fscr[0:1, 2:3],
                  AF.Ln, scale=1.0, waits=[(ss, v_g)], inc=ss)
    v_bm = p.emit("tensor", "matmul", pbank(env, 7)[:, 0:1],
                  acols[0:1, 768:896], fscr[0:1, 3:4],
                  start=True, stop=True, waits=[(vv, v_ones), (ss, v_lg)],
                  inc=tt)
    v_bm2 = p.emit("tensor", "matmul", pbank(env, 7)[:, 1:2],
                   acols[0:1, 768:896], fscr[0:1, 1:2],
                   start=True, stop=True, inc=tt)
    lgc = fscr[:, 8:9]
    lac = fscr[:, 9:10]
    v_ev = p.emit("vector", "tensor_copy", fscr[:, 8:10], pbank(env, 7)[:, 0:2],
                  waits=[(tt, v_bm2)], inc=vv)
    v_E = p.emit("scalar", "activation", Etile[:], acols[:, 0:512], AF.Exp,
                 bias=lac, scale=lgc, waits=[(vv, v_ev)], inc=ss)
    v_cf = p.emit("scalar", "activation", fscr[:, 16:44], acols[:, 512:540],
                  AF.Exp, scale=lgc, inc=ss)
    v_T = {}
    prev_tm = 0
    for o in range(4):
        v_i = p.dma("sync", acols[:, 0:512], env.dram["l1b_iotaT"][o],
                    waits=[(ss, v_T[o - 1] if o else v_cf)], inc=ic2)
        v_m = p.dma("gpsimd", mstage, env.dram["l1b_tmask"][o],
                    waits=([(vv, prev_tm)] if o else ()), inc=et)
        v_e1 = p.emit("scalar", "activation", Ttile[o][:], acols[:, 0:512],
                      AF.Exp, bias=lac, scale=lgc,
                      waits=[(ic2, v_i)], inc=ss)
        v_T[o] = v_e1
        prev_tm = p.emit("vector", "tensor_mul", Ttile[o][:], Ttile[o][:],
                         mstage, waits=[(ss, v_e1), (et, v_m)], inc=vv)
    v_setup_v = env.sem("vv").n

    for icx in range(NF):
        p.emit("vector", "memset", zsh0[:, 128 * icx:128 * icx + 1], 0.0)
        p.emit("vector", "tensor_copy",
               zsh0[:, 128 * icx + 1:128 * (icx + 1)],
               xnT[:, icx, 0:127], waits=[(vv, pro_vv)])
    v_zsh = p.emit("vector", "engine_nop", inc=vv)

    st = {}

    for dc in range(NF):
        v_w = p.dma("gpsimd", wslot(env, 0)[:, 0:1024],
                    env.dram["l1b_wwT"][dc],
                    waits=[(tt, st.get("w0free", 0))], inc=wts_[0])
        for tc in range(NQ):
            bank = tc % 2
            for icx in range(NF):
                wts = []
                if icx == 0:
                    wts = [(wts_[0], v_w), (vv, pro_vv),
                           (ss, st.get(("ve", bank), 0))]
                v_mm = p.emit("tensor", "matmul", pbank(env, bank),
                              wslot(env, 0)[:, 128 * icx:128 * (icx + 1)],
                              xnT[:, icx, 512 * tc:512 * (tc + 1)],
                              start=(icx == 0), stop=(icx == NF - 1),
                              waits=wts, inc=(tt if icx == NF - 1 else None))
            st[("ve", bank)] = p.emit(
                "scalar", "activation",
                a2[:, 9216 + 512 * tc:9216 + 512 * (tc + 1)],
                pbank(env, bank), AF.Copy, scale=1.0,
                waits=[(tt, v_mm), (tt, st.get("vtr", 0)),
                       (env.sem("oo"), st.get("vout", 0))], inc=ss)
        v_vs = env.sem("ss").n
        for tc4 in range(NQ):
            pb = pbank_bf16(env, 6)
            tw = [(ss, v_vs), (vv, st.get("vtev", 0))]
            for b4 in range(4):
                v_tr = p.emit("tensor", "transpose",
                              pb[:, 128 * b4:128 * (b4 + 1)],
                              a2[:, 9216 + 512 * tc4 + 128 * b4:
                                 9216 + 512 * tc4 + 128 * (b4 + 1)],
                              t["ident"][:],
                              waits=(tw if b4 == 0 else ()),
                              inc=(tt if b4 == 3 else None))
            st["vtr"] = v_tr
            kc0 = 4 * tc4
            src4 = pb[:, 0:512].rearrange("p (a c) -> p a c", c=128)
            dst4 = vbuild[:].rearrange("p (k c) -> p k c", k=32)[:, kc0:kc0 + 4, :]
            st["vtev"] = p.emit("vector", "tensor_copy", dst4, src4,
                                waits=[(tt, v_tr)], inc=vv)
        st["w0free"] = env.sem("tt").n
        st["vout"] = p.dma(
            "sync",
            vtok_d.rearrange("(kc pp) d -> pp kc d", pp=128)[:, :,
                                                            128 * dc:128 * (dc + 1)],
            vbuild[:].rearrange("p (k c) -> p k c", k=32),
            waits=[(vv, st["vtev"])], inc=env.sem("oo"))
    v_vtok_oo = env.sem("oo").n

    wr_state = {"oc": -1, "v": 0}
    for qc in range(getattr(env, "L1B_MAXQC", NQ)):
        q0 = 512 * qc
        nkc = 4 * qc + 4
        for kc in range(nkc):
            sbank = kc % 2
            for icx in range(NF):
                wts = []
                if icx == 0:
                    pf = st.get(("pfree", sbank))
                    wts = ([pf] if pf else []) + [(vv, v_zsh)]
                lhs = (zsh0[:, 128 * icx:128 * (icx + 1)] if kc == 0 else
                       xnT[:, icx, 128 * kc - 1:128 * kc + 127])
                v_s = p.emit("tensor", "matmul", pbank(env, sbank), lhs,
                             xnT[:, icx, q0:q0 + 512],
                             start=(icx == 0), stop=(icx == NF - 1),
                             waits=wts, inc=(tt if icx == NF - 1 else None))
            o = kc - 4 * qc
            pw = [(tt, v_s), (tt, st.get(("pused", kc), 0))]
            if o >= 0:
                v_pe = p.emit("vector", "tensor_mul",
                              probs[:, 512 * kc:512 * (kc + 1)],
                              pbank(env, sbank), Ttile[o][:],
                              waits=pw + [(vv, v_setup_v)], inc=vv)
            else:
                m = -o
                v_pe = p.emit("vector", "tensor_scalar",
                              out=probs[:, 512 * kc:512 * (kc + 1)],
                              in0=pbank(env, sbank),
                              scalar1=fscr[:, 15 + m:16 + m], scalar2=None,
                              op0=ALU.mult,
                              waits=pw + [(ss, v_cf)], inc=vv)
            st[("pfree", sbank)] = (vv, v_pe)
        v_p_all = env.sem("vv").n
        nh = (nkc + 15) // 16
        first_write = {dc: True for dc in range(NF)}
        for hno in range(nh):
            klo, khi = 16 * hno, min(16 * (hno + 1), nkc)
            v_vl = p.dma(
                "sync",
                vstg[:].rearrange("p (k d) -> p k d", k=16)[:, 0:khi - klo, :],
                vtok_d.rearrange("(kc pp) d -> pp kc d", pp=128)[:, klo:khi, :],
                waits=[(env.sem("oo"), v_vtok_oo),
                       (tt, st.get("vstg_free", 0))], inc=lo)
            for dc in range(NF):
                lows = [k for k in range(klo, min(khi, 4 * qc))]
                strs = [k for k in range(max(klo, 4 * qc), khi)]
                av = {}
                for b, kcs in ((4, lows), (5, strs)):
                    for k_i, kc in enumerate(kcs):
                        wts = []
                        if k_i == 0:
                            wts = [(lo, v_vl), (vv, v_p_all),
                                   (vv, st.get("combfree", 0)),
                                   (tt, st.get("outdone", 0))]
                        v_av = p.emit(
                            "tensor", "matmul", pbank(env, b),
                            vstg[:].rearrange("p (k d) -> p k d", k=16)[
                                :, kc - klo, 128 * dc:128 * (dc + 1)],
                            probs[:, 512 * kc:512 * (kc + 1)],
                            start=(k_i == 0), stop=(k_i == len(kcs) - 1),
                            waits=wts, inc=(tt if k_i == len(kcs) - 1
                                            else None))
                    if kcs:
                        av[b] = tt.n
                rq = readsq[:, 512 * dc:512 * (dc + 1)]
                v_cb = None
                if 4 in av:
                    p.emit("vector", "tensor_mul", comb[:, 0:512],
                           pbank(env, 4), Etile[:],
                           waits=[(tt, av[4]), (ss, v_E)])
                    if first_write[dc]:
                        v_cb = p.emit("vector", "tensor_copy", rq,
                                      comb[:, 0:512], inc=vv)
                    else:
                        v_cb = p.emit("vector", "tensor_add", rq, rq,
                                      comb[:, 0:512], inc=vv)
                    first_write[dc] = False
                if 5 in av:
                    if first_write[dc]:
                        v_cb = p.emit("vector", "tensor_copy", rq,
                                      pbank(env, 5),
                                      waits=[(tt, av[5])], inc=vv)
                    else:
                        v_cb = p.emit("vector", "tensor_add", rq, rq,
                                      pbank(env, 5),
                                      waits=[(tt, av[5])], inc=vv)
                    first_write[dc] = False
                st["combfree"] = v_cb
            st["vstg_free"] = env.sem("tt").n
        for kc in range(nkc):
            st[("pused", kc)] = env.sem("tt").n
        v_reads = env.sem("vv").n

        def mk_out(oc, tc, bank, war, _vr=v_reads, _wr=wr_state):
            if _wr["oc"] != oc:
                _wr["v"] = p.dma(
                    "gpsimd", wslot(env, 0)[:, 0:1024],
                    env.dram["l1b_wrT"][oc],
                    waits=[(tt, tt.n)], inc=wts_[0])
                _wr["oc"] = oc
            for ec in range(NF):
                wts = []
                if ec == 0:
                    wts = list(war) + [(wts_[0], _wr["v"]), (vv, _vr)]
                r = p.emit("tensor", "matmul", pbank(env, bank),
                           wslot(env, 0)[:, 128 * ec:128 * (ec + 1)],
                           readsq[:, 512 * ec:512 * (ec + 1)],
                           start=(ec == 0), stop=(ec == NF - 1),
                           waits=wts, inc=(tt if ec == NF - 1 else None))
            return tt, r

        emit_out_phase(env, x_ap, y_ap, mk_out, [qc], bias_col=None, scale=1.0)
        st["outdone"] = env.sem("tt").n
        wr_state["oc"] = -1


EMITTERS = {
    "l1f": emit_l1f,
    "l0f": emit_l0f,
    "l0b": emit_l0b,
    "l0a": emit_l0a,
    "l1a": emit_l1a,
    "l1b": emit_l1b,
}


def build_program(sublayers, dev_arrays, dump=(), serial=False,
                  race_detect=True):
    nc = bass.Bass("TRN2", target_bir_lowering=False,
                   detect_race_conditions=race_detect)
    env = Env()
    env.plan.self_sems = {"vector": env.sems["vv"], "scalar": env.sems["ss"]}
    if serial:
        env.plan.serial_sems = env.sems
    env.nc = nc

    x_in = nc.dram_tensor("x", [S, D], F32, kind="ExternalInput").ap()
    y_out = nc.dram_tensor("y", [S, D], F32, kind="ExternalOutput").ap()
    xs = [nc.dram_tensor(f"xs{i}", [S, D], F32, kind="Internal").ap()
          for i in range(2)]
    env.dram["attn_oT"] = nc.dram_tensor("attn_oT", [D, S], BF16,
                                         kind="Internal").ap()
    env.dram["vtok_d"] = nc.dram_tensor("vtok_d", [S, D], BF16,
                                        kind="Internal").ap()

    for name, arr in dev_arrays.items():
        dt = {np.dtype(np.float32): F32, np.dtype(np.int32): I32}[arr.dtype]
        env.dram[name] = nc.dram_tensor(name, list(arr.shape), dt,
                                        kind="ExternalInput").ap()

    es = ExitStack()
    with es:
        alloc_common(env, nc, es)
        for s in env.sems.values():
            s.h = es.enter_context(nc.semaphore(s.name))
        block = es.enter_context(nc.Block())

        p = env.plan
        for nm in ("xnT", "arena1", "arena2", "wstage", "xf", "yf", "fscr",
                   "cols", "acols"):
            p.emit("vector", "memset", env.t[nm][:], 0.0)
        v_init = p.emit("vector", "engine_nop", inc=env.sem("vv"))
        p.wait("sync", (env.sem("vv"), v_init))
        p.wait("gpsimd", (env.sem("vv"), v_init))
        p.dma("gpsimd", env.t["ident"][:], env.dram["ident"],
              inc=env.sem("ic"))
        p.dma("sync", env.t["identf"][:], env.dram["ident"],
              inc=env.sem("ic2"))
        p.wait("tensor", (env.sem("ic"), 16), (env.sem("ic2"), 16))
        p.emit("vector", "memset", env.t["epsc"][:], LN_EPS,
               inc=env.sem("vv"))

        cur = x_in
        for idx, sl in enumerate(sublayers):
            last = idx == len(sublayers) - 1
            dst = y_out if last else xs[idx % 2]
            sub_reset(env)
            EMITTERS[SUB_ORDER[sl]](env, cur, dst)
            cur = dst
            if not last:
                barrier(env)

        for dname in dump:
            th = env.t[dname]
            sh = list(th.shape) if hasattr(th, "shape") else None
            dout = nc.dram_tensor("dump_" + dname, sh,
                                  th[:].dtype, kind="ExternalOutput").ap()
            barrier(env)
            p.dma("sync", dout, th[:], inc=env.sem("oo"))

        p.wait("sync", (env.sem("oo"), env.sem("oo").n),
               (env.sem("oob"), env.sem("oob").n))

        for ename in Plan.ENGINES:
            def runner(eng, _ops=env.plan.ops[ename]):
                for f in _ops:
                    f(eng)
            getattr(block, ename)(runner)

    return nc


GROUPS = [list(range(8))]


def kernel(**inputs):
    dev = host_prep(inputs)
    x = np.asarray(inputs["L0_attn_a_x"])
    cur = [np.ascontiguousarray(x[c]) for c in range(8)]
    for g in GROUPS:
        nc = build_program(g, dev)
        in_maps = []
        for c in range(8):
            m = dict(dev)
            m["x"] = cur[c]
            in_maps.append(m)
        res = run_bass_kernel_spmd(nc, in_maps, core_ids=list(range(8)))
        cur = [np.ascontiguousarray(r["y"]) for r in res.results]
    return np.stack(cur).astype(np.float32)
